# version 29
# baseline (speedup 1.0000x reference)
import numpy as np
import ml_dtypes
from contextlib import ExitStack

import concourse.bass as bass
import concourse.mybir as mybir
from concourse.bass_utils import run_bass_kernel_spmd

F32 = mybir.dt.float32
BF16 = mybir.dt.bfloat16
AF = mybir.ActivationFunctionType
ALU = mybir.AluOpType
AX = mybir.AxisListType.X

S = 2048
D = 2048
NM = 16
NK = 16
NH = 16
NT2 = 2
W2 = 1024
NT3 = 4
W3 = 512
ER = 512
NEB = 4
EPS = 1e-5


def ts(i, w):
    return slice(i * w, (i + 1) * w)


_SER_OPS = {
    "activation", "add", "mul", "copy", "square", "sqrt",
    "tensor_copy", "tensor_tensor", "tensor_scalar", "tensor_scalar_mul",
    "tensor_scalar_add", "tensor_add", "tensor_sub", "tensor_mul",
    "tensor_max", "reduce_sum", "reduce_max", "tensor_reduce", "reciprocal",
    "memset", "tensor_tensor_scan", "scalar_tensor_tensor", "select",
    "bn_stats", "bn_aggr",
}


class _SerInstr:
    def __init__(self, instr, ser):
        self._i = instr
        self._s = ser

    def then_inc(self, sem, n=1):
        self._i.then_inc(sem, n)
        self._s._note_inc(self._i, sem, n)
        return self


class Ser:
    def __init__(self, eng, own_sem, st):
        self._e = eng
        self._own = own_sem
        self._st = st

    def _note_inc(self, instr, sem, n):
        st = self._st
        key = id(sem)
        v = st["counts"].get(key, 0) + n
        st["counts"][key] = v
        if instr is st["pending"]:
            st["pending"] = None
        st["last"] = (sem, v)

    def _flush(self):
        st = self._st
        if st["pending"] is not None:
            st["n"] += 1
            st["pending"].then_inc(self._own, 1)
            st["last"] = (self._own, st["n"])
            st["pending"] = None
        if st["last"] is not None:
            sem, v = st["last"]
            self._e.wait_ge(sem, v)
            st["last"] = None

    def wait_ge(self, sem, v):
        self._e.wait_ge(sem, v)

    def __getattr__(self, name):
        fn = getattr(self._e, name)
        if name not in _SER_OPS:
            return fn

        def wrapped(*a, **k):
            self._flush()
            instr = fn(*a, **k)
            self._st["pending"] = instr
            return _SerInstr(instr, self)

        return wrapped


def build_nc(debug=False):
    nc = bass.Bass("TRN2", target_bir_lowering=False)

    xin = nc.dram_tensor("x", [S, D], F32, kind="ExternalInput").ap()
    waT_h = nc.dram_tensor("waT", [D, D], F32, kind="ExternalInput")
    waT = waT_h.ap()
    wxT_h = nc.dram_tensor("wxT", [D, D], F32, kind="ExternalInput")
    wxT = wxT_h.ap()
    apk = nc.dram_tensor("apk", [D, ER], F32, kind="ExternalInput").ap()
    btp = nc.dram_tensor("btp", [ER, D], F32, kind="ExternalInput").ap()
    wrT_h = nc.dram_tensor("wrT", [D, 8], F32, kind="ExternalInput")
    wrT = wrT_h.ap()
    ba_d = nc.dram_tensor("ba_t", [128, 16], F32, kind="ExternalInput").ap()
    bx_d = nc.dram_tensor("bx_t", [128, 16], F32, kind="ExternalInput").ap()
    lam_d = nc.dram_tensor("lam_t", [128, 16], F32, kind="ExternalInput").ap()
    br_d = nc.dram_tensor("br", [8, 1], F32, kind="ExternalInput").ap()
    id_d = nc.dram_tensor("ident", [128, 128], F32, kind="ExternalInput").ap()
    on_d = nc.dram_tensor("ones1", [1, 128], F32, kind="ExternalInput").ap()
    sel_d = nc.dram_tensor("sel2", [2, 128], F32, kind="ExternalInput").ap()

    out = nc.dram_tensor("out", [D, S], F32, kind="ExternalOutput").ap()

    dbg = "ExternalOutput" if debug else "Internal"
    xph_h = nc.dram_tensor("xph", [D, S], BF16, kind=dbg)
    xph = xph_h.ap()
    xpl_h = nc.dram_tensor("xpl", [D, S], BF16, kind=dbg)
    xpl = xpl_h.ap()
    xfm_h = nc.dram_tensor("xfm", [D, S], F32, kind=dbg)
    xfm = xfm_h.ap()
    yat = nc.dram_tensor("yat", [D, S], F32, kind=dbg).ap()

    top = ExitStack()
    sem = lambda name: top.enter_context(nc.semaphore(name))
    sbt = lambda name, shape, dt=F32: top.enter_context(nc.sbuf_tensor(name, shape, dt))

    ssS = sem("ssS")
    ssV = sem("ssV")
    stS = {"n": 0, "counts": {}, "pending": None, "last": None}
    stV = {"n": 0, "counts": {}, "pending": None, "last": None}

    identf = sbt("identf", [128, 128])
    sel2 = sbt("sel2s", [2, 128])
    ba = sbt("ba_s", [128, 16])
    bx = sbt("bx_s", [128, 16])
    lam = sbt("lam_s", [128, 16])
    cng = sbt("cng", [128, 16])
    brs = sbt("brs", [8, 1])

    s3pre = ExitStack()
    sb3p = lambda n, s, dt=F32: s3pre.enter_context(nc.sbuf_tensor(n, s, dt))
    xc3h0 = sb3p("xc3h0", [128, NK * W3], BF16)
    xc3l0 = sb3p("xc3l0", [128, NK * W3], BF16)
    wrf = sb3p("wrf", [128, 128])
    wrh = sb3p("wrh", [128, 128], BF16)
    wrl = sb3p("wrl", [128, 128], BF16)
    s2pre = ExitStack()
    sb2p = lambda n, s, dt=F32: s2pre.enter_context(nc.sbuf_tensor(n, s, dt))
    xc2h0 = sb2p("xc2h0", [128, NK * W2], BF16)
    xc2l = sb2p("xc2l", [128, NK * W2], BF16)
    wfa = sb2p("wfa", [128, NK * 128])
    wfx = sb2p("wfx", [128, NK * 128])
    wra_h0 = sb2p("wrah0", [128, NK * 128], BF16)
    wra_l0 = sb2p("wral0", [128, NK * 128], BF16)
    wrx_h0 = sb2p("wrxh0", [128, NK * 128], BF16)
    wrx_l0 = sb2p("wrxl0", [128, NK * 128], BF16)

    s_wr = sem("s_wr")
    s_wda = sem("s_wda")
    s_wdx = sem("s_wdx")
    s_xc2 = [sem(f"s_xc2{j}") for j in range(2)]
    s_xc2s = [sem(f"s_xc2s{j}") for j in range(2)]
    s_xc2g = sem("s_xc2g")
    s_xc2g2 = sem("s_xc2g2")
    s_xc2gs = sem("s_xc2gs")
    s_xc2t = sem("s_xc2t")
    s_xc2t2 = sem("s_xc2t2")
    s_xc30 = sem("s_xc3p0")
    s_wrc = sem("s_wrc")
    s_wrs = sem("s_wrs")
    s_wca = sem("s_wca")
    s_wcx = sem("s_wcx")
    s_wsa = sem("s_wsa")
    s_wsg = sem("s_wsg")

    with ExitStack() as ph:
        sb1 = lambda n, s, dt=F32: ph.enter_context(nc.sbuf_tensor(n, s, dt))
        xr = [sb1(f"xr{j}", [128, D]) for j in range(3)]
        xnb = sb1("xnb", [128, D])
        XB = [sb1(f"XB{j}", [128, D]) for j in range(2)]
        HB = [sb1(f"HB{j}", [128, D], BF16) for j in range(2)]
        LB = [sb1(f"LB{j}", [128, D], BF16) for j in range(2)]
        st6 = sb1("st6", [128, 24])
        st2 = sb1("st2", [128, 2])
        nmb = sb1("nmb", [128, 1])
        stdt = sb1("stdt", [128, 1])
        rstd = sb1("rstd", [128, 1])
        epsb = sb1("epsb", [128, 1])
        ptA = ph.enter_context(nc.psum_tensor("ptA", [128, 2048], F32))
        ptB = ph.enter_context(nc.psum_tensor("ptB", [128, 2048], F32))

        s_cst = sem("s_cst")
        s_x = [sem(f"s_x{j}") for j in range(3)]
        s_bn = sem("s_bn")
        s_sd = sem("s_sd")
        s_rc = sem("s_rc")
        s_xn = sem("s_xn")
        s_tp = sem("s_tp")
        s_cpX = sem("s_cpX")
        s_cpXV = sem("s_cpXV")
        s_hic = sem("s_hic")
        s_lo = sem("s_lo")
        s_sp = [sem(f"s_sp{j}") for j in range(2)]
        s_spx = [sem(f"s_spx{j}") for j in range(2)]

        def fm_dst(base_h, m):
            return bass.AP(base_h, m * 128,
                           [[S, 128], [128 * S, NK], [1, 128]])

        with nc.Block("ph1") as blk:
            @blk.sync
            def _(sync):
                for dst, srcd in [(identf, id_d), (sel2, sel_d),
                                  (ba, ba_d), (bx, bx_d), (lam, lam_d),
                                  (brs, br_d)]:
                    sync.dma_start(dst[:], srcd).then_inc(s_cst, 16)
                sync.dma_start(
                    wrf[:],
                    bass.AP(wrT_h, 0, [[8, 128], [128 * 8, NK], [1, 8]]),
                ).then_inc(s_wr, 16)
                sync.dma_start(
                    wfa[:],
                    bass.AP(waT_h, 0, [[D, 128], [128 * D, NK], [1, 128]]),
                ).then_inc(s_wda, 16)
                for m in range(NM):
                    if m >= 3:
                        sync.wait_ge(s_xn, m - 2)
                        sync.wait_ge(s_tp, 32 * (m - 2))
                    sync.dma_start(xr[m % 3][:],
                                   xin[ts(m, 128), :]).then_inc(s_x[m % 3], 16)
                for j in range(2):
                    sync.wait_ge(s_sp[j], 32 * 4)
                for k in range(NK):
                    half = s_xc2[0] if k < 8 else s_xc2s[0]
                    sync.dma_start(xc2h0[:, ts(k, W2)],
                                   xph[ts(k, 128), ts(0, W2)]).then_inc(half, 16)
                    if k % 2 == 1:
                        sync.dma_start(xc2l[:, ts(k, W2)],
                                       xpl[ts(k, 128),
                                           ts(0, W2)]).then_inc(half, 16)

            @blk.scalar
            def _(scalar):
                scalar = Ser(scalar, ssS, stS)
                scalar.dma_start(
                    wfx[:],
                    bass.AP(wxT_h, 0, [[D, 128], [128 * D, NK], [1, 128]]),
                ).then_inc(s_wdx, 16)
                scalar.wait_ge(s_cst, 16 * 6)
                scalar.activation(cng[:], lam[:], AF.Exp, scale=-1.0)
                scalar.activation(lam[:], cng[:], AF.Ln, bias=1.0)
                scalar.activation(cng[:], lam[:], AF.Copy, scale=-8.0)

                def hicast(j):
                    scalar.wait_ge(s_tp, 32 * j + 16)
                    if j >= 2:
                        scalar.wait_ge(s_sp[j % 2], 32 * (j // 2))
                    scalar.activation(HB[j % 2][:], ptA[:],
                                      AF.Copy).then_inc(s_hic, 1)

                def xbev(j):
                    scalar.wait_ge(s_tp, 32 * j + 32)
                    if j >= 2:
                        scalar.wait_ge(s_spx[j % 2], 16 * (j // 2))
                    scalar.activation(XB[j % 2][:], ptB[:],
                                      AF.Copy).then_inc(s_cpX, 1)
                    scalar.wait_ge(s_cpX, j + 1)
                    scalar.dma_start(fm_dst(xfm_h, j),
                                     XB[j % 2][:]).then_inc(s_spx[j % 2], 16)

                for m in range(NM):
                    scalar.wait_ge(s_bn, m + 1)
                    if m >= 1:
                        scalar.wait_ge(s_rc, m)
                    scalar.activation(stdt[:], st2[:, 1:2], AF.Sqrt,
                                      bias=epsb[:]).then_inc(s_sd, 1)
                    if m >= 1:
                        hicast(m - 1)
                        xbev(m - 1)
                    if m == 10:
                        scalar.wait_ge(s_wdx, 16)
                        scalar.activation(wrx_h0[:], wfx[:],
                                          AF.Copy).then_inc(s_wcx, 1)
                hicast(NM - 1)
                xbev(NM - 1)

            @blk.vector
            def _(vector):
                vector = Ser(vector, ssV, stV)
                vector.memset(epsb[:], EPS)

                def losub(j):
                    vector.wait_ge(s_hic, j + 1)
                    vector.tensor_tensor(LB[j % 2][:], ptA[:],
                                         HB[j % 2][:],
                                         ALU.subtract).then_inc(s_lo, 1)

                for m in range(NM):
                    vector.wait_ge(s_x[m % 3], 16 * (m // 3 + 1))
                    if m >= 1:
                        vector.wait_ge(s_sd, m)
                    for q in range(4):
                        vector.bn_stats(st6[:, 6 * q:6 * q + 6],
                                        xr[m % 3][:, ts(q, 512)])
                    vector.bn_aggr(st2[:], st6[:]).then_inc(s_bn, 1)
                    vector.wait_ge(s_sd, m + 1)
                    vector.reciprocal(rstd[:], stdt[:]).then_inc(s_rc, 1)
                    vector.tensor_scalar_mul(nmb[:], st2[:, 0:1], -1.0)
                    if m >= 1:
                        vector.wait_ge(s_tp, 32 * (m - 1) + 16)
                    vector.tensor_scalar(xnb[:], xr[m % 3][:], nmb[:],
                                         rstd[:], ALU.add,
                                         ALU.mult).then_inc(s_xn, 1)
                    if m == 10:
                        vector.wait_ge(s_wda, 16)
                        vector.tensor_copy(wra_h0[:], wfa[:]).then_inc(s_wca, 1)
                    if m == 11:
                        vector.tensor_tensor(wra_l0[:], wfa[:], wra_h0[:],
                                             ALU.subtract).then_inc(s_wsa, 1)
                    if m == 13:
                        vector.wait_ge(s_wcx, 1)
                        vector.tensor_tensor(wrx_l0[:], wfx[:], wrx_h0[:],
                                             ALU.subtract).then_inc(s_wsg, 1)
                    if m >= 1:
                        losub(m - 1)
                losub(NM - 1)

            @blk.tensor
            def _(tensor):
                for m in range(NM):
                    tensor.wait_ge(s_xn, m + 1)
                    if m >= 1:
                        tensor.wait_ge(s_lo, m)
                    for k in range(NK):
                        tensor.transpose(ptA[:, k * 128:k * 128 + 128],
                                         xnb[:, ts(k, 128)],
                                         identf[:]).then_inc(s_tp, 1)
                    if m >= 1:
                        tensor.wait_ge(s_cpX, m)
                    for k in range(NK):
                        tensor.transpose(ptB[:, k * 128:k * 128 + 128],
                                         xr[m % 3][:, ts(k, 128)],
                                         identf[:]).then_inc(s_tp, 1)

            @blk.gpsimd
            def _(gpsimd):
                for j in range(NM):
                    gpsimd.wait_ge(s_hic, j + 1)
                    gpsimd.dma_start(fm_dst(xph_h, j),
                                     HB[j % 2][:]).then_inc(s_sp[j % 2], 16)
                    gpsimd.wait_ge(s_lo, j + 1)
                    gpsimd.dma_start(fm_dst(xpl_h, j),
                                     LB[j % 2][:]).then_inc(s_sp[j % 2], 16)
                    if j >= 8:
                        if j == 8:
                            for q in range(2):
                                gpsimd.wait_ge(s_sp[q], 32 * 4)
                        k = 2 * (j - 8)
                        gpsimd.dma_start(xc2l[:, ts(k, W2)],
                                         xpl[ts(k, 128), ts(0, W2)]
                                         ).then_inc(s_xc2g if k < 8
                                                    else s_xc2g2, 16)
                for j in range(2):
                    gpsimd.wait_ge(s_sp[j], 32 * (NM // 2))
                    gpsimd.wait_ge(s_spx[j], 16 * (NM // 2))

    with ExitStack() as ph:
        sb1 = lambda n, s, dt=F32: ph.enter_context(nc.sbuf_tensor(n, s, dt))
        xc2h = [xc2h0, sb1("xc2h1", [128, NK * W2], BF16)]
        wra_h = [wra_h0, sb1("wrah1", [128, NK * 128], BF16)]
        wra_l = [wra_l0, sb1("wral1", [128, NK * 128], BF16)]
        wrx_h = [wrx_h0, sb1("wrxh1", [128, NK * 128], BF16)]
        wrx_l = [wrx_l0, sb1("wrxl1", [128, NK * 128], BF16)]
        isb = [sb1(f"isb{j}", [128, W2]) for j in range(2)]
        aab = [sb1(f"aab{j}", [128, W2]) for j in range(2)]
        sfb = [sb1(f"sfb{j}", [128, W2]) for j in range(2)]
        uub = sb1("uub", [128, W2])
        car = sb1("car", [128, 16])
        pr2 = [ph.enter_context(nc.psum_tensor(f"pr2{j}", [128, W2], F32))
               for j in range(2)]
        pi2 = [ph.enter_context(nc.psum_tensor(f"pi2{j}", [128, W2], F32))
               for j in range(2)]

        s_rec = sem("s_rec")
        s_mmA = sem("s_mmA")
        s_acA = sem("s_acA")
        s_sfc = sem("s_sfc")
        s_vA = sem("s_vA")
        s_ya = sem("s_ya")

        NIDX2 = NT2 * NH

        with nc.Block("ph2") as blk:
            @blk.sync
            def _(sync):
                for idx2 in range(1, NIDX2):
                    h = idx2 % NH
                    if 4 <= idx2 <= 11:
                        for k in (2 * (idx2 - 4), 2 * (idx2 - 4) + 1):
                            sync.dma_start(xc3h0[:, ts(k, W3)],
                                           xph[ts(k, 128),
                                               ts(0, W3)]).then_inc(s_xc30, 16)
                            sync.dma_start(xc3l0[:, ts(k, W3)],
                                           xpl[ts(k, 128),
                                               ts(0, W3)]).then_inc(s_xc30, 16)
                    sync.wait_ge(s_wsa, idx2)
                    sync.dma_start(
                        wfa[:],
                        bass.AP(waT_h, h * 128,
                                [[D, 128], [128 * D, NK], [1, 128]]),
                    ).then_inc(s_wda, 16)
                    if idx2 == NH:
                        sync.wait_ge(s_mmA, NH)
                        for k in (1, 3, 5, 7):
                            sync.dma_start(xc2l[:, ts(k, W2)],
                                           xpl[ts(k, 128),
                                               ts(1, W2)]).then_inc(s_xc2t2, 16)
                    if idx2 == 2:
                        for k in range(0, NK, 2):
                            sync.dma_start(xc2h[1][:, ts(k, W2)],
                                           xph[ts(k, 128),
                                               ts(1, W2)]).then_inc(s_xc2[1], 16)

            @blk.tensor
            def _(tensor):
                for tb in range(NT2):
                    if tb == 0:
                        tensor.wait_ge(s_xc2[0], 192)
                        tensor.wait_ge(s_xc2g, 64)
                    else:
                        tensor.wait_ge(s_xc2[1], 128)
                        tensor.wait_ge(s_xc2s[1], 128)
                        tensor.wait_ge(s_xc2g, 192)
                        tensor.wait_ge(s_xc2g2, 64)
                        tensor.wait_ge(s_xc2gs, 64)
                        tensor.wait_ge(s_xc2t2, 64)
                    for h in range(NH):
                        idx2 = tb * NH + h
                        p2 = idx2 % 2
                        tensor.wait_ge(s_wsa, idx2 + 1)
                        tensor.wait_ge(s_wsg, idx2 + 1)
                        if idx2 >= 2:
                            tensor.wait_ge(s_acA, idx2 - 1)
                        pr, pi = pr2[p2], pi2[p2]
                        mm = None
                        for k in range(NK):
                            if tb == 0 and h == 0 and k == 8:
                                tensor.wait_ge(s_xc2s[0], 192)
                                tensor.wait_ge(s_xc2g2, 64)
                            first = (k == 0)
                            last = (k == NK - 1)
                            for wh, wl, po in ((wra_h, wra_l, pr),
                                               (wrx_h, wrx_l, pi)):
                                whk = wh[p2][:, ts(k, 128)]
                                wlk = wl[p2][:, ts(k, 128)]
                                for half in range(2):
                                    rhs_h = xc2h[tb][:, k * W2 + half * 512:
                                                 k * W2 + half * 512 + 512]
                                    rhs_l = xc2l[:, k * W2 + half * 512:
                                                 k * W2 + half * 512 + 512]
                                    dst = po[:, half * 512:half * 512 + 512]
                                    tensor.matmul(dst, whk, rhs_h,
                                                  start=first, stop=False,
                                                  skip_group_check=True)
                                    tensor.matmul(dst, whk, rhs_l,
                                                  start=False, stop=False,
                                                  skip_group_check=True)
                                    mm = tensor.matmul(dst, wlk, rhs_h,
                                                       start=False, stop=last,
                                                       skip_group_check=True)
                        mm.then_inc(s_mmA, 1)

            @blk.scalar
            def _(scalar):
                scalar = Ser(scalar, ssS, stS)

                def wsplit(j):
                    pj = j % 2
                    hj = j % NH
                    if j >= 1:
                        scalar.wait_ge(s_wsg, j)
                        scalar.dma_start(
                            wfx[:],
                            bass.AP(wxT_h, hj * 128,
                                    [[D, 128], [128 * D, NK], [1, 128]]),
                        ).then_inc(s_wdx, 16)
                    scalar.wait_ge(s_wda, 16 * (j + 1))
                    if j >= 2:
                        scalar.wait_ge(s_mmA, j - 1)
                    scalar.activation(wra_h[pj][:], wfa[:],
                                      AF.Copy).then_inc(s_wca, 1)
                    scalar.wait_ge(s_wdx, 16 * (j + 1))
                    scalar.activation(wrx_h[pj][:], wfx[:],
                                      AF.Copy).then_inc(s_wcx, 1)

                for idx2 in range(NIDX2):
                    h = idx2 % NH
                    p2 = idx2 % 2
                    if idx2 == 2:
                        for k in range(1, NK, 2):
                            scalar.dma_start(xc2h[1][:, ts(k, W2)],
                                             xph[ts(k, 128),
                                                 ts(1, W2)]).then_inc(s_xc2s[1], 16)
                    if idx2 == NH:
                        scalar.wait_ge(s_mmA, NH)
                        for k in (9, 11, 13, 15):
                            scalar.wait_ge(s_rec, k + 1)
                            scalar.dma_start(xc2l[:, ts(k, W2)],
                                             xpl[ts(k, 128),
                                                 ts(1, W2)]).then_inc(s_xc2gs, 16)
                    if idx2 + 1 < NIDX2:
                        wsplit(idx2 + 1)
                    scalar.wait_ge(s_mmA, idx2 + 1)
                    if idx2 >= 2:
                        scalar.wait_ge(s_vA, idx2 - 1)
                        scalar.wait_ge(s_ya, 16 * (idx2 - 1))
                    scalar.activation(aab[p2][:], pr2[p2][:], AF.Sigmoid,
                                      bias=ba[:, h:h + 1])
                    scalar.activation(isb[p2][:], pi2[p2][:], AF.Sigmoid,
                                      bias=bx[:, h:h + 1]).then_inc(s_acA, 1)
                    scalar.activation(aab[p2][:], aab[p2][:], AF.Exp,
                                      scale=cng[:, h:h + 1])
                    scalar.activation(sfb[p2][:], aab[p2][:], AF.Square)
                    scalar.activation(sfb[p2][:], sfb[p2][:], AF.Sqrt,
                                      scale=-1.0, bias=1.0).then_inc(s_sfc, 1)
                    if idx2 == NIDX2 - 2:
                        scalar.wait_ge(s_wr, 16)
                        scalar.activation(wrh[:], wrf[:],
                                          AF.Copy).then_inc(s_wrc, 1)

            @blk.vector
            def _(vector):
                vector = Ser(vector, ssV, stV)

                def wsub(j):
                    pj = j % 2
                    vector.wait_ge(s_wca, j + 1)
                    if j >= 2:
                        vector.wait_ge(s_mmA, j - 1)
                    vector.tensor_tensor(wra_l[pj][:], wfa[:],
                                         wra_h[pj][:],
                                         ALU.subtract).then_inc(s_wsa, 1)

                for idx2 in range(NIDX2):
                    tb, h = idx2 // NH, idx2 % NH
                    p2 = idx2 % 2
                    if idx2 + 1 < NIDX2:
                        wsub(idx2 + 1)
                    vector.wait_ge(s_sfc, idx2 + 1)
                    vector.tensor_tensor(uub[:], sfb[p2][:], isb[p2][:], ALU.mult)
                    vector.tensor_tensor(sfb[p2][:], xc2h[tb][:, ts(h, W2)],
                                         xc2l[:, ts(h, W2)],
                                         ALU.add).then_inc(s_rec, 1)
                    vector.tensor_tensor(uub[:], uub[:], sfb[p2][:], ALU.mult)
                    if idx2 >= 1:
                        vector.wait_ge(s_ya, 16 * idx2)
                    init = 0.0 if tb == 0 else car[:, h:h + 1]
                    last = vector.tensor_tensor_scan(isb[p2][:], aab[p2][:],
                                                     uub[:], init,
                                                     ALU.mult, ALU.add)
                    if tb + 1 < NT2:
                        last = vector.tensor_copy(car[:, h:h + 1],
                                                  isb[p2][:, W2 - 1:W2])
                    last.then_inc(s_vA, 1)
                    if idx2 == NIDX2 - 2:
                        vector.wait_ge(s_wrc, 1)
                        vector.tensor_tensor(wrl[:], wrf[:], wrh[:],
                                             ALU.subtract).then_inc(s_wrs, 1)

            @blk.gpsimd
            def _(gpsimd):
                def wsubg(j):
                    pj = j % 2
                    gpsimd.wait_ge(s_wcx, j + 1)
                    if j >= 2:
                        gpsimd.wait_ge(s_mmA, j - 1)
                    gpsimd.tensor_tensor(wrx_l[pj][:], wfx[:],
                                         wrx_h[pj][:],
                                         ALU.subtract).then_inc(s_wsg, 1)

                for idx2 in range(NIDX2):
                    tb, h = idx2 // NH, idx2 % NH
                    if idx2 + 1 < NIDX2:
                        wsubg(idx2 + 1)
                    if idx2 == NH - 1:
                        gpsimd.wait_ge(s_mmA, NH)
                        for k in range(0, NK, 2):
                            gpsimd.wait_ge(s_rec, k + 1)
                            gpsimd.dma_start(xc2l[:, ts(k, W2)],
                                             xpl[ts(k, 128),
                                                 ts(1, W2)]).then_inc(s_xc2g, 16)
                    gpsimd.wait_ge(s_vA, idx2 + 1)
                    gpsimd.dma_start(yat[ts(h, 128), ts(tb, W2)],
                                     isb[idx2 % 2][:]).then_inc(s_ya, 16)
                gpsimd.wait_ge(s_ya, 16 * NIDX2)

    s2pre.close()

    with ExitStack() as ph:
        sb1 = lambda n, s, dt=F32: ph.enter_context(nc.sbuf_tensor(n, s, dt))
        LG = sb1("LG", [8, S])
        GR = LG
        LTt = sb1("LTt", [128, 32])
        LSt = sb1("LSt", [128, 32])
        EXt = sb1("EXt", [128, 32])
        MX4 = sb1("MX4", [128, 4])
        SM4 = sb1("SM4", [128, 4])
        RC4 = sb1("RC4", [128, 4])
        TM4 = sb1("TM4", [128, 4])
        PRt = sb1("PRt", [128, 32])
        PMt = sb1("PMt", [128, 32])
        C1t = sb1("C1t", [128, 32])
        C2t = sb1("C2t", [128, 32])
        MKt = sb1("MKt", [128, 32])
        Gt = sb1("Gt", [128, 32])
        aph = sb1("aph", [128, NK * ER], BF16)
        apl = sb1("apl", [128, NK * ER], BF16)
        bth = sb1("bth", [128, NEB * D], BF16)
        btl = sb1("btl", [128, NEB * D], BF16)
        apf = [sb1(f"apf{j}", [128, ER]) for j in range(2)]
        apfS = [sb1(f"apfS{j}", [128, ER]) for j in range(2)]
        btf = [sb1(f"btf{j}", [128, W3]) for j in range(2)]
        btq = [apfS[0], apfS[1], apf[0], apf[1]]
        apq = [apf[0], apf[1], btf[0], btf[1]]
        apqS = None
        xc3h = [xc3h0, sb1("xc3h1", [128, NK * W3], BF16)]
        xc3l = [xc3l0, sb1("xc3l1", [128, NK * W3], BF16)]
        GBC = sb1("GBC", [128, W3])
        GR2 = [sb1(f"GR2{j}", [2, W3]) for j in range(2)]
        HGF = [sb1(f"HGF{j}", [128, W3]) for j in range(2)]
        apqS = [apfS[0], apfS[1], HGF[0], HGF[1]]
        hgh = [sb1(f"hgh{j}", [128, NEB * W3], BF16) for j in range(2)]
        hgl = [sb1(f"hgl{j}", [128, NEB * W3], BF16) for j in range(2)]
        YA = [sb1(f"YA{j}", [128, W3]) for j in range(4)]
        XF3 = [sb1(f"XF3{j}", [128, W3]) for j in range(4)]
        T1 = sb1("T1", [128, W3])
        OT = [sb1(f"OT{j}", [128, W3]) for j in range(4)]
        pH = [ph.enter_context(nc.psum_tensor(f"pH{j}", [128, 512], F32))
              for j in range(4)]
        pY = [ph.enter_context(nc.psum_tensor(f"pY{j}", [128, 512], F32))
              for j in range(4)]

        s_mR = sem("s_mR")
        s_lg = sem("s_lg")
        s_lgT = sem("s_lgT")
        s_ltv = sem("s_ltv")
        s_ls = sem("s_ls")
        s_sex = sem("s_sex")
        s_g3 = sem("s_g3")
        s_gT = sem("s_gT")
        s_gr = sem("s_gr")
        s_apd = [sem(f"s_apd{j}") for j in range(4)]
        s_apc = sem("s_apc")
        s_aps = sem("s_aps")
        s_apdS = [sem(f"s_apdS{j}") for j in range(4)]
        s_apcS = sem("s_apcS")
        s_apsS = sem("s_apsS")
        s_btd = [sem(f"s_btd{j}") for j in range(2)]
        s_btc = sem("s_btc")
        s_bts = sem("s_bts")
        s_apcE = sem("s_apcE")
        s_btcE = sem("s_btcE")
        s_btcS = sem("s_btcS")
        s_btsO = sem("s_btsO")
        s_btdS = [sem(f"s_btdS{j}") for j in range(4)]
        s_xc3 = [sem(f"s_xc3{j}") for j in range(2)]
        s_xc3s = [sem(f"s_xc3s{j}") for j in range(2)]
        s_gr2 = [sem(f"s_gr2{j}") for j in range(2)]
        s_gbm = sem("s_gbm")
        s_gbv = sem("s_gbv")
        s_mH = sem("s_mH")
        s_hgf = sem("s_hgf")
        s_hgc = sem("s_hgc")
        s_hgs = sem("s_hgs")
        s_mY = sem("s_mY")
        s_cio = [sem(f"s_cio{j}") for j in range(4)]
        s_cmp = sem("s_cmp")
        s_cmb = sem("s_cmb")
        s_out = [sem(f"s_out{j}") for j in range(4)]

        def g4(t, off, d3=8):
            return bass.AP(t, off, [[32, 128], [8, 4], [1, d3]])

        def g4b(t, off):
            return bass.AP(t, off, [[4, 128], [1, 4], [0, 8]])

        def g4s(t, off):
            return bass.AP(t, off, [[32, 128], [8, 4]])

        def hmm_group(tensor, tb, eb):
            idx = tb * NEB + eb
            if idx >= 4:
                tensor.wait_ge(s_hgf, idx - 3)
            mm = None
            for k in range(NK):
                first, last = (k == 0), (k == NK - 1)
                ah = aph[:, k * ER + eb * 128: k * ER + eb * 128 + 128]
                al = apl[:, k * ER + eb * 128: k * ER + eb * 128 + 128]
                rh = xc3h[tb % 2][:, ts(k, W3)]
                rl = xc3l[tb % 2][:, ts(k, W3)]
                tensor.matmul(pH[eb][:], ah, rh, start=first,
                              stop=False, skip_group_check=True)
                tensor.matmul(pH[eb][:], ah, rl, start=False,
                              stop=False, skip_group_check=True)
                mm = tensor.matmul(pH[eb][:], al, rh, start=False,
                                   stop=last, skip_group_check=True)
            mm.then_inc(s_mH, 1)

        def gbc_mm(tensor, tb, eb):
            idx = tb * NEB + eb
            tensor.wait_ge(s_gr2[idx % 2], 16 * (idx // 2 + 1))
            if eb == 0:
                tensor.wait_ge(s_gr, tb + 1)
            tensor.matmul(pY[tb % 4][:], sel2[:], GR2[idx % 2][:],
                          start=True, stop=True,
                          skip_group_check=True).then_inc(s_gbm, 1)
            tensor.wait_ge(s_gbv, idx + 1)

        with nc.Block("ph3") as blk:
            @blk.sync
            def _(sync):
                def xc3_loads(tb):
                    if tb >= 2:
                        sync.wait_ge(s_mH, NEB * (tb - 1))
                    for k in range(NK):
                        sync.dma_start(xc3h[tb % 2][:, ts(k, W3)],
                                       xph[ts(k, 128),
                                           ts(tb, W3)]).then_inc(s_xc3[tb % 2], 16)
                        if k % 2 == 0:
                            sync.dma_start(xc3l[tb % 2][:, ts(k, W3)],
                                           xpl[ts(k, 128),
                                               ts(tb, W3)]).then_inc(s_xc3[tb % 2], 16)

                for tb in range(NT3):
                    if tb + 1 < NT3:
                        xc3_loads(tb + 1)
                    sync.wait_ge(s_gr, tb + 1)
                    for eb in range(NEB):
                        idx = tb * NEB + eb
                        if idx >= 2:
                            sync.wait_ge(s_gbm, idx - 1)
                        sync.dma_start(GR2[idx % 2][:],
                                       GR[2 * eb:2 * eb + 2, ts(tb, W3)]
                                       ).then_inc(s_gr2[idx % 2], 16)
                    for o in range(NH):
                        idx4 = tb * NH + o
                        if idx4 >= 4:
                            sync.wait_ge(s_cmb, idx4 - 3)
                        sync.dma_start(YA[idx4 % 4][:],
                                       yat[ts(o, 128),
                                           ts(tb, W3)]).then_inc(s_cio[idx4 % 4], 16)
                        sync.dma_start(XF3[idx4 % 4][:],
                                       xfm[ts(o, 128),
                                           ts(tb, W3)]).then_inc(s_cio[idx4 % 4], 16)

            @blk.tensor
            def _(tensor):
                tensor.wait_ge(s_wrs, 1)
                for tb in range(NT3):
                    if tb == 0:
                        tensor.wait_ge(s_xc30, 512)
                    else:
                        tensor.wait_ge(s_xc3[tb % 2], 16 * 24 * ((tb + 1) // 2))
                        tensor.wait_ge(s_xc3s[tb % 2], 16 * 8 * ((tb + 1) // 2))
                    BR = pY[tb % 4]
                    if tb >= 1:
                        tensor.wait_ge(s_cmp, 16 * tb)
                    for k in range(NK):
                        first, last = (k == 0), (k == NK - 1)
                        wh = wrh[:, ts(k, 8)]
                        wl = wrl[:, ts(k, 8)]
                        tensor.matmul(BR[:8, :], wh, xc3h[tb % 2][:, ts(k, W3)],
                                      start=first, stop=False, skip_group_check=True)
                        tensor.matmul(BR[:8, :], wh, xc3l[tb % 2][:, ts(k, W3)],
                                      start=False, stop=False, skip_group_check=True)
                        tensor.matmul(BR[:8, :], wl, xc3h[tb % 2][:, ts(k, W3)],
                                      start=False, stop=last,
                                      skip_group_check=True).then_inc(s_mR, 1)
                    tensor.wait_ge(s_lg, tb + 1)
                    tr = None
                    for j in range(4):
                        tr = tensor.transpose(BR[:, j * 8:j * 8 + 8],
                                              LG[:8, ts(4 * tb + j, 128)],
                                              identf[:8, :8])
                    tr.then_inc(s_lgT, 1)
                    if tb == 0:
                        tensor.wait_ge(s_aps, NK // 2)
                        tensor.wait_ge(s_apsS, NK // 2)
                    hmm_group(tensor, tb, 0)
                    tensor.wait_ge(s_g3, tb + 1)
                    tensor.wait_ge(s_ltv, tb + 1)
                    tr = None
                    for j in range(4):
                        tr = tensor.transpose(BR[:8, ts(j, 128)],
                                              Gt[:, ts(j, 8)], identf[:])
                    tr.then_inc(s_gT, 1)
                    hmm_group(tensor, tb, 1)
                    gbc_mm(tensor, tb, 0)
                    gbc_mm(tensor, tb, 1)
                    hmm_group(tensor, tb, 2)
                    gbc_mm(tensor, tb, 2)
                    gbc_mm(tensor, tb, 3)
                    hmm_group(tensor, tb, 3)
                    tensor.wait_ge(s_hgs, NEB * (tb + 1))
                    tensor.wait_ge(s_hgc, NEB * (tb + 1))
                    if tb == 0:
                        tensor.wait_ge(s_bts, 8)
                        tensor.wait_ge(s_btsO, 8)

                    for o in range(NH):
                        idx4 = tb * NH + o
                        if idx4 >= 4:
                            tensor.wait_ge(s_cmp, idx4 - 3)
                        po = pY[idx4 % 4]
                        mm = None
                        for kb in range(NEB):
                            first, last = (kb == 0), (kb == NEB - 1)
                            bh = bth[:, kb * D + o * 128: kb * D + o * 128 + 128]
                            bl = btl[:, kb * D + o * 128: kb * D + o * 128 + 128]
                            rh = hgh[tb % 2][:, ts(kb, W3)]
                            rl = hgl[tb % 2][:, ts(kb, W3)]
                            tensor.matmul(po[:], bh, rh, start=first,
                                          stop=False, skip_group_check=True)
                            tensor.matmul(po[:], bh, rl, start=False,
                                          stop=False, skip_group_check=True)
                            mm = tensor.matmul(po[:], bl, rh, start=False,
                                               stop=last, skip_group_check=True)
                        mm.then_inc(s_mY, 1)

            @blk.scalar
            def _(scalar):
                scalar = Ser(scalar, ssS, stS)

                def xc3s_loads(tb):
                    if tb >= 2:
                        scalar.wait_ge(s_mH, NEB * (tb - 1))
                    for k in range(1, NK, 2):
                        scalar.dma_start(xc3l[tb % 2][:, ts(k, W3)],
                                         xpl[ts(k, 128),
                                             ts(tb, W3)]).then_inc(s_xc3s[tb % 2], 16)

                for i0 in range(4):
                    scalar.dma_start(apqS[i0][:],
                                     apk[ts(2 * i0 + 1, 128), :]
                                     ).then_inc(s_apdS[i0], 16)
                for i, k in enumerate(range(1, NK, 2)):
                    scalar.wait_ge(s_apdS[i % 4], 16 * (i // 4 + 1))
                    scalar.activation(aph[:, ts(k, ER)], apqS[i % 4][:],
                                      AF.Copy).then_inc(s_apcS, 1)
                    if i + 4 < NK // 2:
                        scalar.wait_ge(s_apsS, i + 1)
                        scalar.dma_start(apqS[i % 4][:],
                                         apk[ts(2 * i + 9, 128), :]
                                         ).then_inc(s_apdS[i % 4], 16)
                    scalar.wait_ge(s_apd[i % 4], 16 * (i // 4 + 1))
                    scalar.activation(aph[:, ts(2 * i, ER)], apq[i % 4][:],
                                      AF.Copy).then_inc(s_apcE, 1)

                def bt_odd_dma(i):
                    jo = 2 * i + 1
                    if i >= 4:
                        scalar.wait_ge(s_btsO, i - 3)
                    scalar.dma_start(btq[i % 4][:],
                                     btp[ts(jo // 4, 128),
                                         ts(jo % 4, W3)]).then_inc(s_btdS[i % 4], 16)

                def bt_casts(i):
                    je, jo = 2 * i, 2 * i + 1
                    sle = slice((je // 4) * D + (je % 4) * W3,
                                (je // 4) * D + (je % 4) * W3 + W3)
                    slo = slice((jo // 4) * D + (jo % 4) * W3,
                                (jo // 4) * D + (jo % 4) * W3 + W3)
                    scalar.wait_ge(s_btd[i % 2], 16 * (i // 2 + 1))
                    scalar.activation(bth[:, sle], btf[i % 2][:],
                                      AF.Copy).then_inc(s_btcE, 1)
                    scalar.wait_ge(s_btdS[i % 4], 16 * (i // 4 + 1))
                    scalar.activation(bth[:, slo], btq[i % 4][:],
                                      AF.Copy).then_inc(s_btcS, 1)

                scalar.wait_ge(s_apsS, NK // 2)
                scalar.wait_ge(s_aps, NK // 2)
                for i in range(4):
                    bt_odd_dma(i)
                for tb in range(NT3):
                    if tb + 1 < NT3:
                        xc3s_loads(tb + 1)
                    scalar.wait_ge(s_mR, 16 * (tb + 1))
                    scalar.add(LG[:8, ts(tb, W3)], pY[tb % 4][:8, :],
                               brs[:]).then_inc(s_lg, 1)
                    if tb == 0:
                        for i in range(2):
                            bt_casts(i)
                    scalar.wait_ge(s_ls, tb + 1)
                    scalar.activation(EXt[:], LSt[:], AF.Exp).then_inc(s_sex, 1)
                    if tb == 0:
                        for i in range(2, 8):
                            if i <= 5:
                                bt_odd_dma(i + 2)
                            bt_casts(i)
                    for eb in range(NEB):
                        idx = tb * NEB + eb
                        scalar.wait_ge(s_hgf, idx + 1)
                        if tb >= 2 and eb == 0:
                            scalar.wait_ge(s_mY, NH * (tb - 1))
                        scalar.activation(hgh[tb % 2][:, ts(eb, W3)],
                                          HGF[idx % 2][:],
                                          AF.Copy).then_inc(s_hgc, 1)

            @blk.vector
            def _(vector):
                vector = Ser(vector, ssV, stV)
                for i, k in enumerate(range(1, NK, 2)):
                    vector.wait_ge(s_apcS, i + 1)
                    vector.tensor_tensor(apl[:, ts(k, ER)], apqS[i % 4][:],
                                         aph[:, ts(k, ER)],
                                         ALU.subtract).then_inc(s_apsS, 1)
                for tb in range(NT3):
                    vector.wait_ge(s_lgT, tb + 1)
                    if tb >= 1:
                        vector.wait_ge(s_sex, tb)
                    vector.tensor_copy(LTt[:], pY[tb % 4][:, :32]).then_inc(s_ltv, 1)
                    vector.reduce_max(MX4[:], g4(LTt, 0), axis=AX)
                    vector.tensor_tensor(g4(LSt, 0), g4(LTt, 0), g4b(MX4, 0),
                                         ALU.subtract).then_inc(s_ls, 1)
                    vector.wait_ge(s_sex, tb + 1)
                    vector.reduce_sum(SM4[:], g4(EXt, 0), axis=AX)
                    vector.reciprocal(RC4[:], SM4[:])
                    vector.tensor_tensor(g4(PRt, 0), g4(EXt, 0), g4b(RC4, 0),
                                         ALU.mult)
                    vector.memset(PMt[:], -1.0)
                    for e in range(7):
                        vector.tensor_max(g4s(PMt, e + 1), g4s(PMt, e),
                                          g4s(PRt, e))
                    vector.tensor_max(TM4[:], g4s(PMt, 7), g4s(PRt, 7))
                    vector.tensor_tensor(C1t[:], PRt[:], PMt[:], ALU.is_gt)
                    vector.tensor_tensor(g4(C2t, 0), g4(PRt, 0), g4b(TM4, 0),
                                         ALU.is_ge)
                    vector.tensor_tensor(MKt[:], C1t[:], C2t[:], ALU.mult)
                    vector.scalar_tensor_tensor(Gt[:], PRt[:], 1.0 / 64.0,
                                                MKt[:], ALU.mult,
                                                ALU.mult).then_inc(s_g3, 1)
                    vector.wait_ge(s_gT, tb + 1)
                    vector.tensor_copy(GR[:8, ts(tb, W3)],
                                       pY[tb % 4][:8, :512]).then_inc(s_gr, 1)
                    for eb in range(NEB):
                        idx = tb * NEB + eb
                        vector.wait_ge(s_gbm, idx + 1)
                        vector.tensor_copy(GBC[:], pY[tb % 4][:]).then_inc(s_gbv, 1)
                        vector.wait_ge(s_mH, idx + 1)
                        if idx == 0:
                            vector.wait_ge(s_apcS, NK // 2)
                        if idx >= 2:
                            vector.wait_ge(s_hgc, idx - 1)
                        if tb >= 2 and eb == 0:
                            vector.wait_ge(s_mY, NH * (tb - 1))
                        vector.tensor_tensor(HGF[idx % 2][:], pH[eb][:], GBC[:],
                                             ALU.mult).then_inc(s_hgf, 1)
                        vector.wait_ge(s_hgc, idx + 1)
                        vector.tensor_tensor(hgl[tb % 2][:, ts(eb, W3)],
                                             HGF[idx % 2][:],
                                             hgh[tb % 2][:, ts(eb, W3)],
                                             ALU.subtract).then_inc(s_hgs, 1)
                    for o in range(NH):
                        idx4 = tb * NH + o
                        vector.wait_ge(s_mY, idx4 + 1)
                        vector.wait_ge(s_cio[idx4 % 4], 32 * (idx4 // 4 + 1))
                        vector.tensor_tensor(T1[:], pY[idx4 % 4][:],
                                             YA[idx4 % 4][:],
                                             ALU.add).then_inc(s_cmp, 1)
                        if idx4 >= 4:
                            vector.wait_ge(s_out[idx4 % 4], 16 * (idx4 // 4))
                        vector.tensor_tensor(OT[idx4 % 4][:], T1[:],
                                             XF3[idx4 % 4][:],
                                             ALU.add).then_inc(s_cmb, 1)

            @blk.gpsimd
            def _(gpsimd):
                for i0 in range(4):
                    gpsimd.dma_start(apq[i0][:],
                                     apk[ts(2 * i0, 128), :]
                                     ).then_inc(s_apd[i0], 16)
                for i, k in enumerate(range(0, NK, 2)):
                    gpsimd.wait_ge(s_apcE, i + 1)
                    gpsimd.tensor_tensor(apl[:, ts(k, ER)], apq[i % 4][:],
                                         aph[:, ts(k, ER)],
                                         ALU.subtract).then_inc(s_aps, 1)
                    if i + 4 < NK // 2:
                        gpsimd.wait_ge(s_aps, i + 1)
                        gpsimd.dma_start(apq[i % 4][:],
                                         apk[ts(2 * i + 8, 128), :]
                                         ).then_inc(s_apd[i % 4], 16)
                gpsimd.wait_ge(s_aps, NK // 2)
                gpsimd.dma_start(btf[0][:],
                                 btp[ts(0, 128), ts(0, W3)]).then_inc(s_btd[0], 16)
                for i in range(8):
                    je, jo = 2 * i, 2 * i + 1
                    sle = slice((je // 4) * D + (je % 4) * W3,
                                (je // 4) * D + (je % 4) * W3 + W3)
                    slo = slice((jo // 4) * D + (jo % 4) * W3,
                                (jo // 4) * D + (jo % 4) * W3 + W3)
                    if i + 1 < 8:
                        if i + 1 >= 2:
                            gpsimd.wait_ge(s_bts, i)
                        je2 = 2 * (i + 1)
                        gpsimd.dma_start(btf[(i + 1) % 2][:],
                                         btp[ts(je2 // 4, 128),
                                             ts(je2 % 4, W3)]
                                         ).then_inc(s_btd[(i + 1) % 2], 16)
                    gpsimd.wait_ge(s_btcE, i + 1)
                    gpsimd.tensor_tensor(btl[:, sle], btf[i % 2][:],
                                         bth[:, sle],
                                         ALU.subtract).then_inc(s_bts, 1)
                    gpsimd.wait_ge(s_btcS, i + 1)
                    gpsimd.tensor_tensor(btl[:, slo], btq[i % 4][:],
                                         bth[:, slo],
                                         ALU.subtract).then_inc(s_btsO, 1)
                for tb in range(NT3):
                    for o in range(NH):
                        idx4 = tb * NH + o
                        gpsimd.wait_ge(s_cmb, idx4 + 1)
                        gpsimd.dma_start(out[ts(o, 128), ts(tb, W3)],
                                         OT[idx4 % 4][:]).then_inc(s_out[idx4 % 4], 16)
                for j in range(4):
                    gpsimd.wait_ge(s_out[j], 16 * (NT3 * NH // 4))

    s3pre.close()
    top.close()
    return nc


def _prep_shared(inputs):
    f32 = np.float32
    Wa = np.asarray(inputs["L0_attn_Wa"], f32)
    Wx = np.asarray(inputs["L0_attn_Wx"], f32)
    A = np.asarray(inputs["L0_ffn_A"], f32)
    B = np.asarray(inputs["L0_ffn_B"], f32)
    Wr = np.asarray(inputs["L0_ffn_Wr"], f32)
    ba = np.asarray(inputs["L0_attn_ba"], f32)
    bx = np.asarray(inputs["L0_attn_bx"], f32)
    lam = np.asarray(inputs["L0_attn_Lam"], f32)
    br = np.asarray(inputs["L0_ffn_br"], f32)
    shared = {
        "waT": np.ascontiguousarray(Wa.T),
        "wxT": np.ascontiguousarray(Wx.T),
        "apk": np.ascontiguousarray(A.transpose(2, 0, 1).reshape(D, ER)),
        "btp": np.ascontiguousarray(B.transpose(0, 2, 1).reshape(ER, D)),
        "wrT": np.ascontiguousarray(Wr.T),
        "ba_t": np.ascontiguousarray(ba.reshape(16, 128).T),
        "bx_t": np.ascontiguousarray(bx.reshape(16, 128).T),
        "lam_t": np.ascontiguousarray(lam.reshape(16, 128).T),
        "br": np.ascontiguousarray(br.reshape(8, 1)),
        "ident": np.eye(128, dtype=f32),
        "ones1": np.ones((1, 128), dtype=f32),
        "sel2": np.repeat(np.eye(2, dtype=f32), 64, axis=1),
    }
    return shared


def kernel(**inputs):
    x = np.asarray(inputs["L0_attn_x"], np.float32)
    shared = _prep_shared(inputs)
    nc = build_nc(debug=False)
    in_maps = [dict(shared, x=np.ascontiguousarray(x[c])) for c in range(8)]
    res = run_bass_kernel_spmd(nc, in_maps, core_ids=list(range(8)))
    outp = np.empty((8, S, D), np.float32)
    for c, r in enumerate(res.results):
        outp[c] = np.asarray(r["out"]).T
    return outp


# version 30
# speedup vs baseline: 1.0225x; 1.0225x over previous
import numpy as np
import ml_dtypes
from contextlib import ExitStack

import concourse.bass as bass
import concourse.mybir as mybir
from concourse.bass_utils import run_bass_kernel_spmd

F32 = mybir.dt.float32
BF16 = mybir.dt.bfloat16
AF = mybir.ActivationFunctionType
ALU = mybir.AluOpType
AX = mybir.AxisListType.X

S = 2048
D = 2048
NM = 16
NK = 16
NH = 16
NT2 = 2
W2 = 1024
NT3 = 4
W3 = 512
ER = 512
NEB = 4
EPS = 1e-5


def ts(i, w):
    return slice(i * w, (i + 1) * w)


_SER_OPS = {
    "activation", "add", "mul", "copy", "square", "sqrt",
    "tensor_copy", "tensor_tensor", "tensor_scalar", "tensor_scalar_mul",
    "tensor_scalar_add", "tensor_add", "tensor_sub", "tensor_mul",
    "tensor_max", "reduce_sum", "reduce_max", "tensor_reduce", "reciprocal",
    "memset", "tensor_tensor_scan", "scalar_tensor_tensor", "select",
    "bn_stats", "bn_aggr",
}


class _SerInstr:
    def __init__(self, instr, ser):
        self._i = instr
        self._s = ser

    def then_inc(self, sem, n=1):
        self._i.then_inc(sem, n)
        self._s._note_inc(self._i, sem, n)
        return self


class Ser:
    def __init__(self, eng, own_sem, st):
        self._e = eng
        self._own = own_sem
        self._st = st

    def _note_inc(self, instr, sem, n):
        st = self._st
        key = id(sem)
        v = st["counts"].get(key, 0) + n
        st["counts"][key] = v
        if instr is st["pending"]:
            st["pending"] = None
        st["last"] = (sem, v)

    def _flush(self):
        st = self._st
        if st["pending"] is not None:
            st["n"] += 1
            st["pending"].then_inc(self._own, 1)
            st["last"] = (self._own, st["n"])
            st["pending"] = None
        if st["last"] is not None:
            sem, v = st["last"]
            self._e.wait_ge(sem, v)
            st["last"] = None

    def wait_ge(self, sem, v):
        self._e.wait_ge(sem, v)

    def __getattr__(self, name):
        fn = getattr(self._e, name)
        if name not in _SER_OPS:
            return fn

        def wrapped(*a, **k):
            self._flush()
            instr = fn(*a, **k)
            self._st["pending"] = instr
            return _SerInstr(instr, self)

        return wrapped


def build_nc(debug=False):
    nc = bass.Bass("TRN2", target_bir_lowering=False)

    xin = nc.dram_tensor("x", [S, D], F32, kind="ExternalInput").ap()
    waT_h = nc.dram_tensor("waT", [D, D], F32, kind="ExternalInput")
    waT = waT_h.ap()
    wxT_h = nc.dram_tensor("wxT", [D, D], F32, kind="ExternalInput")
    wxT = wxT_h.ap()
    apk = nc.dram_tensor("apk", [D, ER], F32, kind="ExternalInput").ap()
    btp = nc.dram_tensor("btp", [ER, D], F32, kind="ExternalInput").ap()
    wrT_h = nc.dram_tensor("wrT", [D, 8], F32, kind="ExternalInput")
    wrT = wrT_h.ap()
    ba_d = nc.dram_tensor("ba_t", [128, 16], F32, kind="ExternalInput").ap()
    bx_d = nc.dram_tensor("bx_t", [128, 16], F32, kind="ExternalInput").ap()
    lam_d = nc.dram_tensor("lam_t", [128, 16], F32, kind="ExternalInput").ap()
    br_d = nc.dram_tensor("br", [8, 1], F32, kind="ExternalInput").ap()
    id_d = nc.dram_tensor("ident", [128, 128], F32, kind="ExternalInput").ap()
    on_d = nc.dram_tensor("ones1", [1, 128], F32, kind="ExternalInput").ap()
    sel_d = nc.dram_tensor("sel2", [2, 128], F32, kind="ExternalInput").ap()

    out = nc.dram_tensor("out", [D, S], F32, kind="ExternalOutput").ap()

    dbg = "ExternalOutput" if debug else "Internal"
    xph_h = nc.dram_tensor("xph", [D, S], BF16, kind=dbg)
    xph = xph_h.ap()
    xpl_h = nc.dram_tensor("xpl", [D, S], BF16, kind=dbg)
    xpl = xpl_h.ap()
    xfm_h = nc.dram_tensor("xfm", [D, S], F32, kind=dbg)
    xfm = xfm_h.ap()
    yat = nc.dram_tensor("yat", [D, S], F32, kind=dbg).ap()

    top = ExitStack()
    sem = lambda name: top.enter_context(nc.semaphore(name))
    sbt = lambda name, shape, dt=F32: top.enter_context(nc.sbuf_tensor(name, shape, dt))

    ssS = sem("ssS")
    ssV = sem("ssV")
    stS = {"n": 0, "counts": {}, "pending": None, "last": None}
    stV = {"n": 0, "counts": {}, "pending": None, "last": None}

    identf = sbt("identf", [128, 128])
    sel2 = sbt("sel2s", [2, 128])
    ba = sbt("ba_s", [128, 16])
    bx = sbt("bx_s", [128, 16])
    lam = sbt("lam_s", [128, 16])
    cng = sbt("cng", [128, 16])
    brs = sbt("brs", [8, 1])

    s3pre = ExitStack()
    sb3p = lambda n, s, dt=F32: s3pre.enter_context(nc.sbuf_tensor(n, s, dt))
    xc3h0 = sb3p("xc3h0", [128, NK * W3], BF16)
    xc3l0 = sb3p("xc3l0", [128, NK * W3], BF16)
    wrf = sb3p("wrf", [128, 128])
    wrh = sb3p("wrh", [128, 128], BF16)
    wrl = sb3p("wrl", [128, 128], BF16)
    s2pre = ExitStack()
    sb2p = lambda n, s, dt=F32: s2pre.enter_context(nc.sbuf_tensor(n, s, dt))
    xc2h0 = sb2p("xc2h0", [128, NK * W2], BF16)
    xc2l = sb2p("xc2l", [128, NK * W2], BF16)
    wfa = sb2p("wfa", [128, NK * 128])
    wfx = sb2p("wfx", [128, NK * 128])
    wra_h0 = sb2p("wrah0", [128, NK * 128], BF16)
    wra_l0 = sb2p("wral0", [128, NK * 128], BF16)
    wrx_h0 = sb2p("wrxh0", [128, NK * 128], BF16)
    wrx_l0 = sb2p("wrxl0", [128, NK * 128], BF16)

    s_wr = sem("s_wr")
    s_wda = sem("s_wda")
    s_wdx = sem("s_wdx")
    s_xc2 = [sem(f"s_xc2{j}") for j in range(2)]
    s_xc2s = [sem(f"s_xc2s{j}") for j in range(2)]
    s_xc2g = sem("s_xc2g")
    s_xc2g2 = sem("s_xc2g2")
    s_xc2gs = sem("s_xc2gs")
    s_xc2t = sem("s_xc2t")
    s_xc2t2 = sem("s_xc2t2")
    s_xc30 = sem("s_xc3p0")
    s_wrc = sem("s_wrc")
    s_wrs = sem("s_wrs")
    s_wca = sem("s_wca")
    s_wcx = sem("s_wcx")
    s_wsa = sem("s_wsa")
    s_wsg = sem("s_wsg")

    with ExitStack() as ph:
        sb1 = lambda n, s, dt=F32: ph.enter_context(nc.sbuf_tensor(n, s, dt))
        xr = [sb1(f"xr{j}", [128, D]) for j in range(3)]
        xnb = sb1("xnb", [128, D])
        XB = [sb1(f"XB{j}", [128, D]) for j in range(2)]
        HB = [sb1(f"HB{j}", [128, D], BF16) for j in range(2)]
        LB = [sb1(f"LB{j}", [128, D], BF16) for j in range(2)]
        st6 = sb1("st6", [128, 24])
        st2 = sb1("st2", [128, 2])
        nmb = sb1("nmb", [128, 1])
        stdt = sb1("stdt", [128, 1])
        rstd = sb1("rstd", [128, 1])
        epsb = sb1("epsb", [128, 1])
        ptA = ph.enter_context(nc.psum_tensor("ptA", [128, 2048], F32))
        ptB = ph.enter_context(nc.psum_tensor("ptB", [128, 2048], F32))

        s_cst = sem("s_cst")
        s_x = [sem(f"s_x{j}") for j in range(3)]
        s_bn = sem("s_bn")
        s_sd = sem("s_sd")
        s_rc = sem("s_rc")
        s_xn = sem("s_xn")
        s_tp = sem("s_tp")
        s_cpX = sem("s_cpX")
        s_cpXV = sem("s_cpXV")
        s_hic = sem("s_hic")
        s_lo = sem("s_lo")
        s_sp = [sem(f"s_sp{j}") for j in range(2)]
        s_spx = [sem(f"s_spx{j}") for j in range(2)]

        def fm_dst(base_h, m):
            return bass.AP(base_h, m * 128,
                           [[S, 128], [128 * S, NK], [1, 128]])

        with nc.Block("ph1") as blk:
            @blk.sync
            def _(sync):
                for dst, srcd in [(identf, id_d), (sel2, sel_d),
                                  (ba, ba_d), (bx, bx_d), (lam, lam_d),
                                  (brs, br_d)]:
                    sync.dma_start(dst[:], srcd).then_inc(s_cst, 16)
                sync.dma_start(
                    wrf[:],
                    bass.AP(wrT_h, 0, [[8, 128], [128 * 8, NK], [1, 8]]),
                ).then_inc(s_wr, 16)
                sync.dma_start(
                    wfa[:],
                    bass.AP(waT_h, 0, [[D, 128], [128 * D, NK], [1, 128]]),
                ).then_inc(s_wda, 16)
                for m in range(NM):
                    if m >= 3:
                        sync.wait_ge(s_xn, m - 2)
                        sync.wait_ge(s_tp, 32 * (m - 2))
                    sync.dma_start(xr[m % 3][:],
                                   xin[ts(m, 128), :]).then_inc(s_x[m % 3], 16)
                for j in range(2):
                    sync.wait_ge(s_sp[j], 32 * 4)
                for k in range(NK):
                    half = s_xc2[0] if k < 8 else s_xc2s[0]
                    sync.dma_start(xc2h0[:, ts(k, W2)],
                                   xph[ts(k, 128), ts(0, W2)]).then_inc(half, 16)
                    if k % 2 == 1:
                        sync.dma_start(xc2l[:, ts(k, W2)],
                                       xpl[ts(k, 128),
                                           ts(0, W2)]).then_inc(half, 16)

            @blk.scalar
            def _(scalar):
                scalar = Ser(scalar, ssS, stS)
                scalar.dma_start(
                    wfx[:],
                    bass.AP(wxT_h, 0, [[D, 128], [128 * D, NK], [1, 128]]),
                ).then_inc(s_wdx, 16)
                scalar.wait_ge(s_cst, 16 * 6)
                scalar.activation(cng[:], lam[:], AF.Exp, scale=-1.0)
                scalar.activation(lam[:], cng[:], AF.Ln, bias=1.0)
                scalar.activation(cng[:], lam[:], AF.Copy, scale=-8.0)

                def hicast(j):
                    scalar.wait_ge(s_tp, 32 * j + 16)
                    if j >= 2:
                        scalar.wait_ge(s_sp[j % 2], 32 * (j // 2))
                    scalar.activation(HB[j % 2][:], ptA[:],
                                      AF.Copy).then_inc(s_hic, 1)

                def xbev(j):
                    scalar.wait_ge(s_tp, 32 * j + 32)
                    if j >= 2:
                        scalar.wait_ge(s_spx[j % 2], 16 * (j // 2))
                    scalar.activation(XB[j % 2][:], ptB[:],
                                      AF.Copy).then_inc(s_cpX, 1)
                    scalar.wait_ge(s_cpX, j + 1)
                    scalar.dma_start(fm_dst(xfm_h, j),
                                     XB[j % 2][:]).then_inc(s_spx[j % 2], 16)

                for m in range(NM):
                    scalar.wait_ge(s_bn, m + 1)
                    if m >= 1:
                        scalar.wait_ge(s_rc, m)
                    scalar.activation(stdt[:], st2[:, 1:2], AF.Sqrt,
                                      bias=epsb[:]).then_inc(s_sd, 1)
                    if m >= 1:
                        hicast(m - 1)
                        xbev(m - 1)
                    if m == 10:
                        scalar.wait_ge(s_wdx, 16)
                        scalar.activation(wrx_h0[:], wfx[:],
                                          AF.Copy).then_inc(s_wcx, 1)
                hicast(NM - 1)
                xbev(NM - 1)

            @blk.vector
            def _(vector):
                vector = Ser(vector, ssV, stV)
                vector.memset(epsb[:], EPS)

                def losub(j):
                    vector.wait_ge(s_hic, j + 1)
                    vector.tensor_tensor(LB[j % 2][:], ptA[:],
                                         HB[j % 2][:],
                                         ALU.subtract).then_inc(s_lo, 1)

                for m in range(NM):
                    vector.wait_ge(s_x[m % 3], 16 * (m // 3 + 1))
                    if m >= 1:
                        vector.wait_ge(s_sd, m)
                    for q in range(4):
                        vector.bn_stats(st6[:, 6 * q:6 * q + 6],
                                        xr[m % 3][:, ts(q, 512)])
                    vector.bn_aggr(st2[:], st6[:]).then_inc(s_bn, 1)
                    vector.wait_ge(s_sd, m + 1)
                    vector.reciprocal(rstd[:], stdt[:]).then_inc(s_rc, 1)
                    vector.tensor_scalar_mul(nmb[:], st2[:, 0:1], -1.0)
                    if m >= 1:
                        vector.wait_ge(s_tp, 32 * (m - 1) + 16)
                    vector.tensor_scalar(xnb[:], xr[m % 3][:], nmb[:],
                                         rstd[:], ALU.add,
                                         ALU.mult).then_inc(s_xn, 1)
                    if m == 10:
                        vector.wait_ge(s_wda, 16)
                        vector.tensor_copy(wra_h0[:], wfa[:]).then_inc(s_wca, 1)
                    if m == 11:
                        vector.tensor_tensor(wra_l0[:], wfa[:], wra_h0[:],
                                             ALU.subtract).then_inc(s_wsa, 1)
                    if m == 13:
                        vector.wait_ge(s_wcx, 1)
                        vector.tensor_tensor(wrx_l0[:], wfx[:], wrx_h0[:],
                                             ALU.subtract).then_inc(s_wsg, 1)
                    if m >= 1:
                        losub(m - 1)
                losub(NM - 1)

            @blk.tensor
            def _(tensor):
                for m in range(NM):
                    tensor.wait_ge(s_xn, m + 1)
                    if m >= 1:
                        tensor.wait_ge(s_lo, m)
                    for k in range(NK):
                        tensor.transpose(ptA[:, k * 128:k * 128 + 128],
                                         xnb[:, ts(k, 128)],
                                         identf[:]).then_inc(s_tp, 1)
                    if m >= 1:
                        tensor.wait_ge(s_cpX, m)
                    for k in range(NK):
                        tensor.transpose(ptB[:, k * 128:k * 128 + 128],
                                         xr[m % 3][:, ts(k, 128)],
                                         identf[:]).then_inc(s_tp, 1)

            @blk.gpsimd
            def _(gpsimd):
                for j in range(NM):
                    gpsimd.wait_ge(s_hic, j + 1)
                    gpsimd.dma_start(fm_dst(xph_h, j),
                                     HB[j % 2][:]).then_inc(s_sp[j % 2], 16)
                    gpsimd.wait_ge(s_lo, j + 1)
                    gpsimd.dma_start(fm_dst(xpl_h, j),
                                     LB[j % 2][:]).then_inc(s_sp[j % 2], 16)
                    if j >= 8:
                        if j == 8:
                            for q in range(2):
                                gpsimd.wait_ge(s_sp[q], 32 * 4)
                        k = 2 * (j - 8)
                        gpsimd.dma_start(xc2l[:, ts(k, W2)],
                                         xpl[ts(k, 128), ts(0, W2)]
                                         ).then_inc(s_xc2g if k < 8
                                                    else s_xc2g2, 16)
                for j in range(2):
                    gpsimd.wait_ge(s_sp[j], 32 * (NM // 2))
                    gpsimd.wait_ge(s_spx[j], 16 * (NM // 2))

    with ExitStack() as ph:
        sb1 = lambda n, s, dt=F32: ph.enter_context(nc.sbuf_tensor(n, s, dt))
        xc2h = [xc2h0, sb1("xc2h1", [128, NK * W2], BF16)]
        wra_h = [wra_h0, sb1("wrah1", [128, NK * 128], BF16)]
        wra_l = [wra_l0, sb1("wral1", [128, NK * 128], BF16)]
        wrx_h = [wrx_h0, sb1("wrxh1", [128, NK * 128], BF16)]
        wrx_l = [wrx_l0, sb1("wrxl1", [128, NK * 128], BF16)]
        isb = [sb1(f"isb{j}", [128, W2]) for j in range(2)]
        aab = [sb1(f"aab{j}", [128, W2]) for j in range(2)]
        sfb = [sb1(f"sfb{j}", [128, W2]) for j in range(2)]
        uub = sb1("uub", [128, W2])
        car = sb1("car", [128, 16])
        pr2 = [ph.enter_context(nc.psum_tensor(f"pr2{j}", [128, W2], F32))
               for j in range(2)]
        pi2 = [ph.enter_context(nc.psum_tensor(f"pi2{j}", [128, W2], F32))
               for j in range(2)]

        s_rec = sem("s_rec")
        s_mmA = sem("s_mmA")
        s_acA = sem("s_acA")
        s_sfc = sem("s_sfc")
        s_vA = sem("s_vA")
        s_ya = sem("s_ya")

        NIDX2 = NT2 * NH

        with nc.Block("ph2") as blk:
            @blk.sync
            def _(sync):
                for idx2 in range(1, NIDX2):
                    h = idx2 % NH
                    if 4 <= idx2 <= 11:
                        for k in (2 * (idx2 - 4), 2 * (idx2 - 4) + 1):
                            sync.dma_start(xc3h0[:, ts(k, W3)],
                                           xph[ts(k, 128),
                                               ts(0, W3)]).then_inc(s_xc30, 16)
                            sync.dma_start(xc3l0[:, ts(k, W3)],
                                           xpl[ts(k, 128),
                                               ts(0, W3)]).then_inc(s_xc30, 16)
                    sync.wait_ge(s_wsa, idx2)
                    sync.dma_start(
                        wfa[:],
                        bass.AP(waT_h, h * 128,
                                [[D, 128], [128 * D, NK], [1, 128]]),
                    ).then_inc(s_wda, 16)
                    if idx2 == NH:
                        sync.wait_ge(s_mmA, NH)
                        for k in (1, 3, 5, 7):
                            sync.dma_start(xc2l[:, ts(k, W2)],
                                           xpl[ts(k, 128),
                                               ts(1, W2)]).then_inc(s_xc2t2, 16)
                    if idx2 == 2:
                        for k in range(0, NK, 2):
                            sync.dma_start(xc2h[1][:, ts(k, W2)],
                                           xph[ts(k, 128),
                                               ts(1, W2)]).then_inc(s_xc2[1], 16)

            @blk.tensor
            def _(tensor):
                for tb in range(NT2):
                    if tb == 0:
                        tensor.wait_ge(s_xc2[0], 192)
                        tensor.wait_ge(s_xc2g, 64)
                    else:
                        tensor.wait_ge(s_xc2[1], 128)
                        tensor.wait_ge(s_xc2s[1], 128)
                        tensor.wait_ge(s_xc2g, 192)
                        tensor.wait_ge(s_xc2g2, 64)
                        tensor.wait_ge(s_xc2gs, 64)
                        tensor.wait_ge(s_xc2t2, 64)
                    for h in range(NH):
                        idx2 = tb * NH + h
                        p2 = idx2 % 2
                        tensor.wait_ge(s_wsa, idx2 + 1)
                        tensor.wait_ge(s_wsg, idx2 + 1)
                        if idx2 >= 2:
                            tensor.wait_ge(s_acA, idx2 - 1)
                        pr, pi = pr2[p2], pi2[p2]
                        mm = None
                        for k in range(NK):
                            if tb == 0 and h == 0 and k == 8:
                                tensor.wait_ge(s_xc2s[0], 192)
                                tensor.wait_ge(s_xc2g2, 64)
                            first = (k == 0)
                            last = (k == NK - 1)
                            for wh, wl, po in ((wra_h, wra_l, pr),
                                               (wrx_h, wrx_l, pi)):
                                whk = wh[p2][:, ts(k, 128)]
                                wlk = wl[p2][:, ts(k, 128)]
                                for half in range(2):
                                    rhs_h = xc2h[tb][:, k * W2 + half * 512:
                                                 k * W2 + half * 512 + 512]
                                    rhs_l = xc2l[:, k * W2 + half * 512:
                                                 k * W2 + half * 512 + 512]
                                    dst = po[:, half * 512:half * 512 + 512]
                                    tensor.matmul(dst, whk, rhs_h,
                                                  start=first, stop=False,
                                                  skip_group_check=True)
                                    tensor.matmul(dst, whk, rhs_l,
                                                  start=False, stop=False,
                                                  skip_group_check=True)
                                    mm = tensor.matmul(dst, wlk, rhs_h,
                                                       start=False, stop=last,
                                                       skip_group_check=True)
                        mm.then_inc(s_mmA, 1)

            @blk.scalar
            def _(scalar):
                scalar = Ser(scalar, ssS, stS)

                def wsplit(j):
                    pj = j % 2
                    hj = j % NH
                    if j >= 1:
                        scalar.wait_ge(s_wsg, j)
                        scalar.dma_start(
                            wfx[:],
                            bass.AP(wxT_h, hj * 128,
                                    [[D, 128], [128 * D, NK], [1, 128]]),
                        ).then_inc(s_wdx, 16)
                    scalar.wait_ge(s_wda, 16 * (j + 1))
                    if j >= 2:
                        scalar.wait_ge(s_mmA, j - 1)
                    scalar.activation(wra_h[pj][:], wfa[:],
                                      AF.Copy).then_inc(s_wca, 1)
                    scalar.wait_ge(s_wdx, 16 * (j + 1))
                    scalar.activation(wrx_h[pj][:], wfx[:],
                                      AF.Copy).then_inc(s_wcx, 1)

                for idx2 in range(NIDX2):
                    h = idx2 % NH
                    p2 = idx2 % 2
                    if idx2 == 2:
                        for k in range(1, NK, 2):
                            scalar.dma_start(xc2h[1][:, ts(k, W2)],
                                             xph[ts(k, 128),
                                                 ts(1, W2)]).then_inc(s_xc2s[1], 16)
                    if idx2 == NH:
                        scalar.wait_ge(s_mmA, NH)
                        for k in (9, 11, 13, 15):
                            scalar.wait_ge(s_rec, k + 1)
                            scalar.dma_start(xc2l[:, ts(k, W2)],
                                             xpl[ts(k, 128),
                                                 ts(1, W2)]).then_inc(s_xc2gs, 16)
                    if idx2 + 1 < NIDX2:
                        wsplit(idx2 + 1)
                    scalar.wait_ge(s_mmA, idx2 + 1)
                    if idx2 >= 2:
                        scalar.wait_ge(s_vA, idx2 - 1)
                        scalar.wait_ge(s_ya, 16 * (idx2 - 1))
                    scalar.activation(aab[p2][:], pr2[p2][:], AF.Sigmoid,
                                      bias=ba[:, h:h + 1])
                    scalar.activation(isb[p2][:], pi2[p2][:], AF.Sigmoid,
                                      bias=bx[:, h:h + 1]).then_inc(s_acA, 1)
                    scalar.activation(aab[p2][:], aab[p2][:], AF.Exp,
                                      scale=cng[:, h:h + 1])
                    scalar.activation(sfb[p2][:], aab[p2][:], AF.Square)
                    scalar.activation(sfb[p2][:], sfb[p2][:], AF.Sqrt,
                                      scale=-1.0, bias=1.0).then_inc(s_sfc, 1)
                    if idx2 == NIDX2 - 2:
                        scalar.wait_ge(s_wr, 16)
                        scalar.activation(wrh[:], wrf[:],
                                          AF.Copy).then_inc(s_wrc, 1)

            @blk.vector
            def _(vector):
                vector = Ser(vector, ssV, stV)

                def wsub(j):
                    pj = j % 2
                    vector.wait_ge(s_wca, j + 1)
                    if j >= 2:
                        vector.wait_ge(s_mmA, j - 1)
                    vector.tensor_tensor(wra_l[pj][:], wfa[:],
                                         wra_h[pj][:],
                                         ALU.subtract).then_inc(s_wsa, 1)

                for idx2 in range(NIDX2):
                    tb, h = idx2 // NH, idx2 % NH
                    p2 = idx2 % 2
                    if idx2 + 1 < NIDX2:
                        wsub(idx2 + 1)
                    vector.wait_ge(s_sfc, idx2 + 1)
                    vector.tensor_tensor(uub[:], sfb[p2][:], isb[p2][:], ALU.mult)
                    vector.tensor_tensor(sfb[p2][:], xc2h[tb][:, ts(h, W2)],
                                         xc2l[:, ts(h, W2)],
                                         ALU.add).then_inc(s_rec, 1)
                    vector.tensor_tensor(uub[:], uub[:], sfb[p2][:], ALU.mult)
                    if idx2 >= 1:
                        vector.wait_ge(s_ya, 16 * idx2)
                    init = 0.0 if tb == 0 else car[:, h:h + 1]
                    last = vector.tensor_tensor_scan(isb[p2][:], aab[p2][:],
                                                     uub[:], init,
                                                     ALU.mult, ALU.add)
                    if tb + 1 < NT2:
                        last = vector.tensor_copy(car[:, h:h + 1],
                                                  isb[p2][:, W2 - 1:W2])
                    last.then_inc(s_vA, 1)
                    if idx2 == NIDX2 - 2:
                        vector.wait_ge(s_wrc, 1)
                        vector.tensor_tensor(wrl[:], wrf[:], wrh[:],
                                             ALU.subtract).then_inc(s_wrs, 1)

            @blk.gpsimd
            def _(gpsimd):
                def wsubg(j):
                    pj = j % 2
                    gpsimd.wait_ge(s_wcx, j + 1)
                    if j >= 2:
                        gpsimd.wait_ge(s_mmA, j - 1)
                    gpsimd.tensor_tensor(wrx_l[pj][:], wfx[:],
                                         wrx_h[pj][:],
                                         ALU.subtract).then_inc(s_wsg, 1)

                for idx2 in range(NIDX2):
                    tb, h = idx2 // NH, idx2 % NH
                    if idx2 + 1 < NIDX2:
                        wsubg(idx2 + 1)
                    if idx2 == NH - 1:
                        gpsimd.wait_ge(s_mmA, NH)
                        for k in range(0, NK, 2):
                            gpsimd.wait_ge(s_rec, k + 1)
                            gpsimd.dma_start(xc2l[:, ts(k, W2)],
                                             xpl[ts(k, 128),
                                                 ts(1, W2)]).then_inc(s_xc2g, 16)
                    gpsimd.wait_ge(s_vA, idx2 + 1)
                    gpsimd.dma_start(yat[ts(h, 128), ts(tb, W2)],
                                     isb[idx2 % 2][:]).then_inc(s_ya, 16)
                gpsimd.wait_ge(s_ya, 16 * NIDX2)

    s2pre.close()

    with ExitStack() as ph:
        sb1 = lambda n, s, dt=F32: ph.enter_context(nc.sbuf_tensor(n, s, dt))
        LG = sb1("LG", [8, S])
        GR = LG
        LTt = sb1("LTt", [128, 32])
        LSt = sb1("LSt", [128, 32])
        EXt = sb1("EXt", [128, 32])
        MX4 = sb1("MX4", [128, 4])
        SM4 = sb1("SM4", [128, 4])
        RC4 = sb1("RC4", [128, 4])
        TM4 = sb1("TM4", [128, 4])
        PRt = sb1("PRt", [128, 32])
        PMt = sb1("PMt", [128, 32])
        C1t = sb1("C1t", [128, 32])
        C2t = sb1("C2t", [128, 32])
        MKt = sb1("MKt", [128, 32])
        Gt = sb1("Gt", [128, 32])
        aph = sb1("aph", [128, NK * ER], BF16)
        apl = sb1("apl", [128, NK * ER], BF16)
        bth = sb1("bth", [128, NEB * D], BF16)
        btl = sb1("btl", [128, NEB * D], BF16)
        apf = [sb1(f"apf{j}", [128, ER]) for j in range(2)]
        apfS = [sb1(f"apfS{j}", [128, ER]) for j in range(2)]
        btf = [sb1(f"btf{j}", [128, W3]) for j in range(2)]
        btq = [apfS[0], apfS[1], apf[0], apf[1]]
        apq = [apf[0], apf[1], btf[0], btf[1]]
        apqS = None
        xc3h = [xc3h0, sb1("xc3h1", [128, NK * W3], BF16)]
        xc3l = [xc3l0, sb1("xc3l1", [128, NK * W3], BF16)]
        GBC = sb1("GBC", [128, W3])
        GR2 = [sb1(f"GR2{j}", [2, W3]) for j in range(2)]
        HGF = [sb1(f"HGF{j}", [128, W3]) for j in range(2)]
        apqS = [apfS[0], apfS[1], HGF[0], HGF[1]]
        hgh = [sb1(f"hgh{j}", [128, NEB * W3], BF16) for j in range(2)]
        hgl = [sb1(f"hgl{j}", [128, NEB * W3], BF16) for j in range(2)]
        YA = [sb1(f"YA{j}", [128, W3]) for j in range(4)]
        XF3 = [sb1(f"XF3{j}", [128, W3]) for j in range(4)]
        T1 = sb1("T1", [128, W3])
        OT = [sb1(f"OT{j}", [128, W3]) for j in range(4)]
        pH = [ph.enter_context(nc.psum_tensor(f"pH{j}", [128, 512], F32))
              for j in range(4)]
        pY = [ph.enter_context(nc.psum_tensor(f"pY{j}", [128, 512], F32))
              for j in range(4)]

        s_mR = sem("s_mR")
        s_lg = sem("s_lg")
        s_lgT = sem("s_lgT")
        s_ltv = sem("s_ltv")
        s_ls = sem("s_ls")
        s_sex = sem("s_sex")
        s_g3 = sem("s_g3")
        s_gT = sem("s_gT")
        s_gr = sem("s_gr")
        s_apd = [sem(f"s_apd{j}") for j in range(4)]
        s_apc = sem("s_apc")
        s_aps = sem("s_aps")
        s_apdS = [sem(f"s_apdS{j}") for j in range(4)]
        s_apcS = sem("s_apcS")
        s_apsS = sem("s_apsS")
        s_btd = [sem(f"s_btd{j}") for j in range(2)]
        s_btc = sem("s_btc")
        s_bts = sem("s_bts")
        s_apcE = sem("s_apcE")
        s_btcE = sem("s_btcE")
        s_btcS = sem("s_btcS")
        s_btsO = sem("s_btsO")
        s_btdS = [sem(f"s_btdS{j}") for j in range(4)]
        s_xc3 = [sem(f"s_xc3{j}") for j in range(2)]
        s_xc3s = [sem(f"s_xc3s{j}") for j in range(2)]
        s_gr2 = [sem(f"s_gr2{j}") for j in range(2)]
        s_gbm = sem("s_gbm")
        s_gbv = sem("s_gbv")
        s_mH = sem("s_mH")
        s_hgf = sem("s_hgf")
        s_hgc = sem("s_hgc")
        s_hgs = sem("s_hgs")
        s_mY = sem("s_mY")
        s_cio = [sem(f"s_cio{j}") for j in range(4)]
        s_cmp = sem("s_cmp")
        s_cmb = sem("s_cmb")
        s_out = [sem(f"s_out{j}") for j in range(4)]

        def g4(t, off, d3=8):
            return bass.AP(t, off, [[32, 128], [8, 4], [1, d3]])

        def g4b(t, off):
            return bass.AP(t, off, [[4, 128], [1, 4], [0, 8]])

        def g4s(t, off):
            return bass.AP(t, off, [[32, 128], [8, 4]])

        def hmm_group(tensor, tb, eb, apwait=False):
            idx = tb * NEB + eb
            if idx >= 4:
                tensor.wait_ge(s_hgf, idx - 3)
            mm = None
            for k in range(NK):
                if apwait:
                    if k % 2 == 0:
                        tensor.wait_ge(s_aps, k // 2 + 1)
                    else:
                        tensor.wait_ge(s_apsS, k // 2 + 1)
                first, last = (k == 0), (k == NK - 1)
                ah = aph[:, k * ER + eb * 128: k * ER + eb * 128 + 128]
                al = apl[:, k * ER + eb * 128: k * ER + eb * 128 + 128]
                rh = xc3h[tb % 2][:, ts(k, W3)]
                rl = xc3l[tb % 2][:, ts(k, W3)]
                tensor.matmul(pH[eb][:], ah, rh, start=first,
                              stop=False, skip_group_check=True)
                tensor.matmul(pH[eb][:], ah, rl, start=False,
                              stop=False, skip_group_check=True)
                mm = tensor.matmul(pH[eb][:], al, rh, start=False,
                                   stop=last, skip_group_check=True)
            mm.then_inc(s_mH, 1)

        def gbc_mm(tensor, tb, eb):
            idx = tb * NEB + eb
            tensor.wait_ge(s_gr2[idx % 2], 16 * (idx // 2 + 1))
            if eb == 0:
                tensor.wait_ge(s_gr, tb + 1)
            tensor.matmul(pY[tb % 4][:], sel2[:], GR2[idx % 2][:],
                          start=True, stop=True,
                          skip_group_check=True).then_inc(s_gbm, 1)
            tensor.wait_ge(s_gbv, idx + 1)

        with nc.Block("ph3") as blk:
            @blk.sync
            def _(sync):
                def xc3_loads(tb):
                    if tb >= 2:
                        sync.wait_ge(s_mH, NEB * (tb - 1))
                    for k in range(NK):
                        sync.dma_start(xc3h[tb % 2][:, ts(k, W3)],
                                       xph[ts(k, 128),
                                           ts(tb, W3)]).then_inc(s_xc3[tb % 2], 16)
                        if k % 2 == 0:
                            sync.dma_start(xc3l[tb % 2][:, ts(k, W3)],
                                           xpl[ts(k, 128),
                                               ts(tb, W3)]).then_inc(s_xc3[tb % 2], 16)

                for tb in range(NT3):
                    if tb + 1 < NT3:
                        xc3_loads(tb + 1)
                    sync.wait_ge(s_gr, tb + 1)
                    for eb in range(NEB):
                        idx = tb * NEB + eb
                        if idx >= 2:
                            sync.wait_ge(s_gbm, idx - 1)
                        sync.dma_start(GR2[idx % 2][:],
                                       GR[2 * eb:2 * eb + 2, ts(tb, W3)]
                                       ).then_inc(s_gr2[idx % 2], 16)
                    for o in range(NH):
                        idx4 = tb * NH + o
                        if idx4 >= 4:
                            sync.wait_ge(s_cmb, idx4 - 3)
                        sync.dma_start(YA[idx4 % 4][:],
                                       yat[ts(o, 128),
                                           ts(tb, W3)]).then_inc(s_cio[idx4 % 4], 16)
                        sync.dma_start(XF3[idx4 % 4][:],
                                       xfm[ts(o, 128),
                                           ts(tb, W3)]).then_inc(s_cio[idx4 % 4], 16)

            @blk.tensor
            def _(tensor):
                tensor.wait_ge(s_wrs, 1)
                for tb in range(NT3):
                    if tb == 0:
                        tensor.wait_ge(s_xc30, 512)
                    else:
                        tensor.wait_ge(s_xc3[tb % 2], 16 * 24 * ((tb + 1) // 2))
                        tensor.wait_ge(s_xc3s[tb % 2], 16 * 8 * ((tb + 1) // 2))
                    BR = pY[tb % 4]
                    if tb >= 1:
                        tensor.wait_ge(s_cmp, 16 * tb)
                    for k in range(NK):
                        first, last = (k == 0), (k == NK - 1)
                        wh = wrh[:, ts(k, 8)]
                        wl = wrl[:, ts(k, 8)]
                        tensor.matmul(BR[:8, :], wh, xc3h[tb % 2][:, ts(k, W3)],
                                      start=first, stop=False, skip_group_check=True)
                        tensor.matmul(BR[:8, :], wh, xc3l[tb % 2][:, ts(k, W3)],
                                      start=False, stop=False, skip_group_check=True)
                        tensor.matmul(BR[:8, :], wl, xc3h[tb % 2][:, ts(k, W3)],
                                      start=False, stop=last,
                                      skip_group_check=True).then_inc(s_mR, 1)
                    if tb == 0:
                        hmm_group(tensor, 0, 0, apwait=True)
                    tensor.wait_ge(s_lg, tb + 1)
                    tr = None
                    for j in range(4):
                        tr = tensor.transpose(BR[:, j * 8:j * 8 + 8],
                                              LG[:8, ts(4 * tb + j, 128)],
                                              identf[:8, :8])
                    tr.then_inc(s_lgT, 1)
                    if tb != 0:
                        hmm_group(tensor, tb, 0)
                    tensor.wait_ge(s_g3, tb + 1)
                    tensor.wait_ge(s_ltv, tb + 1)
                    tr = None
                    for j in range(4):
                        tr = tensor.transpose(BR[:8, ts(j, 128)],
                                              Gt[:, ts(j, 8)], identf[:])
                    tr.then_inc(s_gT, 1)
                    hmm_group(tensor, tb, 1)
                    gbc_mm(tensor, tb, 0)
                    gbc_mm(tensor, tb, 1)
                    hmm_group(tensor, tb, 2)
                    gbc_mm(tensor, tb, 2)
                    gbc_mm(tensor, tb, 3)
                    hmm_group(tensor, tb, 3)
                    tensor.wait_ge(s_hgs, NEB * (tb + 1))
                    tensor.wait_ge(s_hgc, NEB * (tb + 1))
                    if tb == 0:
                        tensor.wait_ge(s_bts, 8)
                        tensor.wait_ge(s_btsO, 8)

                    for o in range(NH):
                        idx4 = tb * NH + o
                        if idx4 >= 4:
                            tensor.wait_ge(s_cmp, idx4 - 3)
                        po = pY[idx4 % 4]
                        mm = None
                        for kb in range(NEB):
                            first, last = (kb == 0), (kb == NEB - 1)
                            bh = bth[:, kb * D + o * 128: kb * D + o * 128 + 128]
                            bl = btl[:, kb * D + o * 128: kb * D + o * 128 + 128]
                            rh = hgh[tb % 2][:, ts(kb, W3)]
                            rl = hgl[tb % 2][:, ts(kb, W3)]
                            tensor.matmul(po[:], bh, rh, start=first,
                                          stop=False, skip_group_check=True)
                            tensor.matmul(po[:], bh, rl, start=False,
                                          stop=False, skip_group_check=True)
                            mm = tensor.matmul(po[:], bl, rh, start=False,
                                               stop=last, skip_group_check=True)
                        mm.then_inc(s_mY, 1)

            @blk.scalar
            def _(scalar):
                scalar = Ser(scalar, ssS, stS)

                def xc3s_loads(tb):
                    if tb >= 2:
                        scalar.wait_ge(s_mH, NEB * (tb - 1))
                    for k in range(1, NK, 2):
                        scalar.dma_start(xc3l[tb % 2][:, ts(k, W3)],
                                         xpl[ts(k, 128),
                                             ts(tb, W3)]).then_inc(s_xc3s[tb % 2], 16)

                for i0 in range(4):
                    scalar.dma_start(apqS[i0][:],
                                     apk[ts(2 * i0 + 1, 128), :]
                                     ).then_inc(s_apdS[i0], 16)
                for i, k in enumerate(range(1, NK, 2)):
                    scalar.wait_ge(s_apdS[i % 4], 16 * (i // 4 + 1))
                    scalar.activation(aph[:, ts(k, ER)], apqS[i % 4][:],
                                      AF.Copy).then_inc(s_apcS, 1)
                    if i + 4 < NK // 2:
                        scalar.wait_ge(s_apsS, i + 1)
                        scalar.dma_start(apqS[i % 4][:],
                                         apk[ts(2 * i + 9, 128), :]
                                         ).then_inc(s_apdS[i % 4], 16)
                    scalar.wait_ge(s_apd[i % 4], 16 * (i // 4 + 1))
                    scalar.activation(aph[:, ts(2 * i, ER)], apq[i % 4][:],
                                      AF.Copy).then_inc(s_apcE, 1)

                def bt_odd_dma(i):
                    jo = 2 * i + 1
                    if i >= 4:
                        scalar.wait_ge(s_btsO, i - 3)
                    scalar.dma_start(btq[i % 4][:],
                                     btp[ts(jo // 4, 128),
                                         ts(jo % 4, W3)]).then_inc(s_btdS[i % 4], 16)

                def bt_casts(i):
                    je, jo = 2 * i, 2 * i + 1
                    sle = slice((je // 4) * D + (je % 4) * W3,
                                (je // 4) * D + (je % 4) * W3 + W3)
                    slo = slice((jo // 4) * D + (jo % 4) * W3,
                                (jo // 4) * D + (jo % 4) * W3 + W3)
                    scalar.wait_ge(s_btd[i % 2], 16 * (i // 2 + 1))
                    scalar.activation(bth[:, sle], btf[i % 2][:],
                                      AF.Copy).then_inc(s_btcE, 1)
                    scalar.wait_ge(s_btdS[i % 4], 16 * (i // 4 + 1))
                    scalar.activation(bth[:, slo], btq[i % 4][:],
                                      AF.Copy).then_inc(s_btcS, 1)

                scalar.wait_ge(s_apsS, NK // 2)
                scalar.wait_ge(s_aps, NK // 2)
                for i in range(4):
                    bt_odd_dma(i)
                for tb in range(NT3):
                    scalar.wait_ge(s_mR, 16 * (tb + 1))
                    scalar.add(LG[:8, ts(tb, W3)], pY[tb % 4][:8, :],
                               brs[:]).then_inc(s_lg, 1)
                    if tb + 1 < NT3:
                        xc3s_loads(tb + 1)
                    if tb == 0:
                        for i in range(2):
                            bt_casts(i)
                    scalar.wait_ge(s_ls, tb + 1)
                    scalar.activation(EXt[:], LSt[:], AF.Exp).then_inc(s_sex, 1)
                    if tb == 0:
                        for i in range(2, 8):
                            if i <= 5:
                                bt_odd_dma(i + 2)
                            bt_casts(i)
                    for eb in range(NEB):
                        idx = tb * NEB + eb
                        scalar.wait_ge(s_hgf, idx + 1)
                        if tb >= 2 and eb == 0:
                            scalar.wait_ge(s_mY, NH * (tb - 1))
                        scalar.activation(hgh[tb % 2][:, ts(eb, W3)],
                                          HGF[idx % 2][:],
                                          AF.Copy).then_inc(s_hgc, 1)

            @blk.vector
            def _(vector):
                vector = Ser(vector, ssV, stV)
                for i, k in enumerate(range(1, NK, 2)):
                    vector.wait_ge(s_apcS, i + 1)
                    vector.tensor_tensor(apl[:, ts(k, ER)], apqS[i % 4][:],
                                         aph[:, ts(k, ER)],
                                         ALU.subtract).then_inc(s_apsS, 1)
                for tb in range(NT3):
                    vector.wait_ge(s_lgT, tb + 1)
                    if tb >= 1:
                        vector.wait_ge(s_sex, tb)
                    vector.tensor_copy(LTt[:], pY[tb % 4][:, :32]).then_inc(s_ltv, 1)
                    vector.reduce_max(MX4[:], g4(LTt, 0), axis=AX)
                    vector.tensor_tensor(g4(LSt, 0), g4(LTt, 0), g4b(MX4, 0),
                                         ALU.subtract).then_inc(s_ls, 1)
                    vector.wait_ge(s_sex, tb + 1)
                    vector.reduce_sum(SM4[:], g4(EXt, 0), axis=AX)
                    vector.reciprocal(RC4[:], SM4[:])
                    vector.tensor_tensor(g4(PRt, 0), g4(EXt, 0), g4b(RC4, 0),
                                         ALU.mult)
                    vector.memset(PMt[:], -1.0)
                    for e in range(7):
                        vector.tensor_max(g4s(PMt, e + 1), g4s(PMt, e),
                                          g4s(PRt, e))
                    vector.tensor_max(TM4[:], g4s(PMt, 7), g4s(PRt, 7))
                    vector.tensor_tensor(C1t[:], PRt[:], PMt[:], ALU.is_gt)
                    vector.tensor_tensor(g4(C2t, 0), g4(PRt, 0), g4b(TM4, 0),
                                         ALU.is_ge)
                    vector.tensor_tensor(MKt[:], C1t[:], C2t[:], ALU.mult)
                    vector.scalar_tensor_tensor(Gt[:], PRt[:], 1.0 / 64.0,
                                                MKt[:], ALU.mult,
                                                ALU.mult).then_inc(s_g3, 1)
                    vector.wait_ge(s_gT, tb + 1)
                    vector.tensor_copy(GR[:8, ts(tb, W3)],
                                       pY[tb % 4][:8, :512]).then_inc(s_gr, 1)
                    for eb in range(NEB):
                        idx = tb * NEB + eb
                        vector.wait_ge(s_gbm, idx + 1)
                        vector.tensor_copy(GBC[:], pY[tb % 4][:]).then_inc(s_gbv, 1)
                        vector.wait_ge(s_mH, idx + 1)
                        if idx == 0:
                            vector.wait_ge(s_apcS, NK // 2)
                        if idx >= 2:
                            vector.wait_ge(s_hgc, idx - 1)
                        if tb >= 2 and eb == 0:
                            vector.wait_ge(s_mY, NH * (tb - 1))
                        vector.tensor_tensor(HGF[idx % 2][:], pH[eb][:], GBC[:],
                                             ALU.mult).then_inc(s_hgf, 1)
                        vector.wait_ge(s_hgc, idx + 1)
                        vector.tensor_tensor(hgl[tb % 2][:, ts(eb, W3)],
                                             HGF[idx % 2][:],
                                             hgh[tb % 2][:, ts(eb, W3)],
                                             ALU.subtract).then_inc(s_hgs, 1)
                    for o in range(NH):
                        idx4 = tb * NH + o
                        vector.wait_ge(s_mY, idx4 + 1)
                        vector.wait_ge(s_cio[idx4 % 4], 32 * (idx4 // 4 + 1))
                        vector.tensor_tensor(T1[:], pY[idx4 % 4][:],
                                             YA[idx4 % 4][:],
                                             ALU.add).then_inc(s_cmp, 1)
                        if idx4 >= 4:
                            vector.wait_ge(s_out[idx4 % 4], 16 * (idx4 // 4))
                        vector.tensor_tensor(OT[idx4 % 4][:], T1[:],
                                             XF3[idx4 % 4][:],
                                             ALU.add).then_inc(s_cmb, 1)

            @blk.gpsimd
            def _(gpsimd):
                for i0 in range(4):
                    gpsimd.dma_start(apq[i0][:],
                                     apk[ts(2 * i0, 128), :]
                                     ).then_inc(s_apd[i0], 16)
                for i, k in enumerate(range(0, NK, 2)):
                    gpsimd.wait_ge(s_apcE, i + 1)
                    gpsimd.tensor_tensor(apl[:, ts(k, ER)], apq[i % 4][:],
                                         aph[:, ts(k, ER)],
                                         ALU.subtract).then_inc(s_aps, 1)
                    if i + 4 < NK // 2:
                        gpsimd.wait_ge(s_aps, i + 1)
                        gpsimd.dma_start(apq[i % 4][:],
                                         apk[ts(2 * i + 8, 128), :]
                                         ).then_inc(s_apd[i % 4], 16)
                gpsimd.wait_ge(s_aps, NK // 2)
                gpsimd.dma_start(btf[0][:],
                                 btp[ts(0, 128), ts(0, W3)]).then_inc(s_btd[0], 16)
                for i in range(8):
                    je, jo = 2 * i, 2 * i + 1
                    sle = slice((je // 4) * D + (je % 4) * W3,
                                (je // 4) * D + (je % 4) * W3 + W3)
                    slo = slice((jo // 4) * D + (jo % 4) * W3,
                                (jo // 4) * D + (jo % 4) * W3 + W3)
                    if i + 1 < 8:
                        if i + 1 >= 2:
                            gpsimd.wait_ge(s_bts, i)
                        je2 = 2 * (i + 1)
                        gpsimd.dma_start(btf[(i + 1) % 2][:],
                                         btp[ts(je2 // 4, 128),
                                             ts(je2 % 4, W3)]
                                         ).then_inc(s_btd[(i + 1) % 2], 16)
                    gpsimd.wait_ge(s_btcE, i + 1)
                    gpsimd.tensor_tensor(btl[:, sle], btf[i % 2][:],
                                         bth[:, sle],
                                         ALU.subtract).then_inc(s_bts, 1)
                    gpsimd.wait_ge(s_btcS, i + 1)
                    gpsimd.tensor_tensor(btl[:, slo], btq[i % 4][:],
                                         bth[:, slo],
                                         ALU.subtract).then_inc(s_btsO, 1)
                for tb in range(NT3):
                    for o in range(NH):
                        idx4 = tb * NH + o
                        gpsimd.wait_ge(s_cmb, idx4 + 1)
                        gpsimd.dma_start(out[ts(o, 128), ts(tb, W3)],
                                         OT[idx4 % 4][:]).then_inc(s_out[idx4 % 4], 16)
                for j in range(4):
                    gpsimd.wait_ge(s_out[j], 16 * (NT3 * NH // 4))

    s3pre.close()
    top.close()
    return nc


def _prep_shared(inputs):
    f32 = np.float32
    Wa = np.asarray(inputs["L0_attn_Wa"], f32)
    Wx = np.asarray(inputs["L0_attn_Wx"], f32)
    A = np.asarray(inputs["L0_ffn_A"], f32)
    B = np.asarray(inputs["L0_ffn_B"], f32)
    Wr = np.asarray(inputs["L0_ffn_Wr"], f32)
    ba = np.asarray(inputs["L0_attn_ba"], f32)
    bx = np.asarray(inputs["L0_attn_bx"], f32)
    lam = np.asarray(inputs["L0_attn_Lam"], f32)
    br = np.asarray(inputs["L0_ffn_br"], f32)
    shared = {
        "waT": np.ascontiguousarray(Wa.T),
        "wxT": np.ascontiguousarray(Wx.T),
        "apk": np.ascontiguousarray(A.transpose(2, 0, 1).reshape(D, ER)),
        "btp": np.ascontiguousarray(B.transpose(0, 2, 1).reshape(ER, D)),
        "wrT": np.ascontiguousarray(Wr.T),
        "ba_t": np.ascontiguousarray(ba.reshape(16, 128).T),
        "bx_t": np.ascontiguousarray(bx.reshape(16, 128).T),
        "lam_t": np.ascontiguousarray(lam.reshape(16, 128).T),
        "br": np.ascontiguousarray(br.reshape(8, 1)),
        "ident": np.eye(128, dtype=f32),
        "ones1": np.ones((1, 128), dtype=f32),
        "sel2": np.repeat(np.eye(2, dtype=f32), 64, axis=1),
    }
    return shared


def kernel(**inputs):
    x = np.asarray(inputs["L0_attn_x"], np.float32)
    shared = _prep_shared(inputs)
    nc = build_nc(debug=False)
    in_maps = [dict(shared, x=np.ascontiguousarray(x[c])) for c in range(8)]
    res = run_bass_kernel_spmd(nc, in_maps, core_ids=list(range(8)))
    outp = np.empty((8, S, D), np.float32)
    for c, r in enumerate(res.results):
        outp[c] = np.asarray(r["out"]).T
    return outp


# version 31
# speedup vs baseline: 1.0371x; 1.0143x over previous
import numpy as np
import ml_dtypes
from contextlib import ExitStack

import concourse.bass as bass
import concourse.mybir as mybir
from concourse.bass_utils import run_bass_kernel_spmd

F32 = mybir.dt.float32
BF16 = mybir.dt.bfloat16
AF = mybir.ActivationFunctionType
ALU = mybir.AluOpType
AX = mybir.AxisListType.X

S = 2048
D = 2048
NM = 16
NK = 16
NH = 16
NT2 = 2
W2 = 1024
NT3 = 4
W3 = 512
ER = 512
NEB = 4
EPS = 1e-5


def ts(i, w):
    return slice(i * w, (i + 1) * w)


_SER_OPS = {
    "activation", "add", "mul", "copy", "square", "sqrt",
    "tensor_copy", "tensor_tensor", "tensor_scalar", "tensor_scalar_mul",
    "tensor_scalar_add", "tensor_add", "tensor_sub", "tensor_mul",
    "tensor_max", "reduce_sum", "reduce_max", "tensor_reduce", "reciprocal",
    "memset", "tensor_tensor_scan", "scalar_tensor_tensor", "select",
    "bn_stats", "bn_aggr",
}


class _SerInstr:
    def __init__(self, instr, ser):
        self._i = instr
        self._s = ser

    def then_inc(self, sem, n=1):
        self._i.then_inc(sem, n)
        self._s._note_inc(self._i, sem, n)
        return self


class Ser:
    def __init__(self, eng, own_sem, st):
        self._e = eng
        self._own = own_sem
        self._st = st

    def _note_inc(self, instr, sem, n):
        st = self._st
        key = id(sem)
        v = st["counts"].get(key, 0) + n
        st["counts"][key] = v
        if instr is st["pending"]:
            st["pending"] = None
        st["last"] = (sem, v)

    def _flush(self):
        st = self._st
        if st["pending"] is not None:
            st["n"] += 1
            st["pending"].then_inc(self._own, 1)
            st["last"] = (self._own, st["n"])
            st["pending"] = None
        if st["last"] is not None:
            sem, v = st["last"]
            self._e.wait_ge(sem, v)
            st["last"] = None

    def wait_ge(self, sem, v):
        self._e.wait_ge(sem, v)

    def __getattr__(self, name):
        fn = getattr(self._e, name)
        if name not in _SER_OPS:
            return fn

        def wrapped(*a, **k):
            self._flush()
            instr = fn(*a, **k)
            self._st["pending"] = instr
            return _SerInstr(instr, self)

        return wrapped


def build_nc(debug=False):
    nc = bass.Bass("TRN2", target_bir_lowering=False)

    xin = nc.dram_tensor("x", [S, D], F32, kind="ExternalInput").ap()
    waT_h = nc.dram_tensor("waT", [D, D], F32, kind="ExternalInput")
    waT = waT_h.ap()
    wxT_h = nc.dram_tensor("wxT", [D, D], F32, kind="ExternalInput")
    wxT = wxT_h.ap()
    apk = nc.dram_tensor("apk", [D, ER], F32, kind="ExternalInput").ap()
    btp = nc.dram_tensor("btp", [ER, D], F32, kind="ExternalInput").ap()
    wrT_h = nc.dram_tensor("wrT", [D, 8], F32, kind="ExternalInput")
    wrT = wrT_h.ap()
    ba_d = nc.dram_tensor("ba_t", [128, 16], F32, kind="ExternalInput").ap()
    bx_d = nc.dram_tensor("bx_t", [128, 16], F32, kind="ExternalInput").ap()
    lam_d = nc.dram_tensor("lam_t", [128, 16], F32, kind="ExternalInput").ap()
    br_d = nc.dram_tensor("br", [8, 1], F32, kind="ExternalInput").ap()
    id_d = nc.dram_tensor("ident", [128, 128], F32, kind="ExternalInput").ap()
    on_d = nc.dram_tensor("ones1", [1, 128], F32, kind="ExternalInput").ap()
    sel_d = nc.dram_tensor("sel2", [2, 128], F32, kind="ExternalInput").ap()

    out = nc.dram_tensor("out", [D, S], F32, kind="ExternalOutput").ap()

    dbg = "ExternalOutput" if debug else "Internal"
    xph_h = nc.dram_tensor("xph", [D, S], BF16, kind=dbg)
    xph = xph_h.ap()
    xpl_h = nc.dram_tensor("xpl", [D, S], BF16, kind=dbg)
    xpl = xpl_h.ap()
    xfm_h = nc.dram_tensor("xfm", [D, S], F32, kind=dbg)
    xfm = xfm_h.ap()
    yat = nc.dram_tensor("yat", [D, S], F32, kind=dbg).ap()

    top = ExitStack()
    sem = lambda name: top.enter_context(nc.semaphore(name))
    sbt = lambda name, shape, dt=F32: top.enter_context(nc.sbuf_tensor(name, shape, dt))

    ssS = sem("ssS")
    ssV = sem("ssV")
    stS = {"n": 0, "counts": {}, "pending": None, "last": None}
    stV = {"n": 0, "counts": {}, "pending": None, "last": None}

    identf = sbt("identf", [128, 128])
    sel2 = sbt("sel2s", [2, 128])
    ba = sbt("ba_s", [128, 16])
    bx = sbt("bx_s", [128, 16])
    lam = sbt("lam_s", [128, 16])
    cng = sbt("cng", [128, 16])
    brs = sbt("brs", [8, 1])

    s3pre = ExitStack()
    sb3p = lambda n, s, dt=F32: s3pre.enter_context(nc.sbuf_tensor(n, s, dt))
    xc3h0 = sb3p("xc3h0", [128, NK * W3], BF16)
    xc3l0 = sb3p("xc3l0", [128, NK * W3], BF16)
    wrf = sb3p("wrf", [128, 128])
    wrh = sb3p("wrh", [128, 128], BF16)
    wrl = sb3p("wrl", [128, 128], BF16)
    s2pre = ExitStack()
    sb2p = lambda n, s, dt=F32: s2pre.enter_context(nc.sbuf_tensor(n, s, dt))
    xc2h0 = sb2p("xc2h0", [128, NK * W2], BF16)
    xc2l = sb2p("xc2l", [128, NK * W2], BF16)
    wfa = sb2p("wfa", [128, NK * 128])
    wfx = sb2p("wfx", [128, NK * 128])
    wra_h0 = sb2p("wrah0", [128, NK * 128], BF16)
    wra_l0 = sb2p("wral0", [128, NK * 128], BF16)
    wrx_h0 = sb2p("wrxh0", [128, NK * 128], BF16)
    wrx_l0 = sb2p("wrxl0", [128, NK * 128], BF16)

    s_wr = sem("s_wr")
    s_wda = sem("s_wda")
    s_wdx = sem("s_wdx")
    s_xc2 = [sem(f"s_xc2{j}") for j in range(2)]
    s_xc2s = [sem(f"s_xc2s{j}") for j in range(2)]
    s_xc2g = sem("s_xc2g")
    s_xc2g2 = sem("s_xc2g2")
    s_xc2gs = sem("s_xc2gs")
    s_xc2t2 = sem("s_xc2t2")
    s_xc2u = sem("s_xc2u")
    s_xc30 = sem("s_xc3p0")
    s_wrc = sem("s_wrc")
    s_wrs = sem("s_wrs")
    s_wca = sem("s_wca")
    s_wcx = sem("s_wcx")
    s_wsa = sem("s_wsa")
    s_wsg = sem("s_wsg")

    with ExitStack() as ph:
        sb1 = lambda n, s, dt=F32: ph.enter_context(nc.sbuf_tensor(n, s, dt))
        xr = [sb1(f"xr{j}", [128, D]) for j in range(3)]
        xnb = sb1("xnb", [128, D])
        XB = [sb1(f"XB{j}", [128, D]) for j in range(3)]
        HB = [sb1(f"HB{j}", [128, D], BF16) for j in range(2)]
        LB = [sb1(f"LB{j}", [128, D], BF16) for j in range(2)]
        st6 = sb1("st6", [128, 24])
        st2 = sb1("st2", [128, 2])
        nmb = sb1("nmb", [128, 1])
        stdt = sb1("stdt", [128, 1])
        rstd = sb1("rstd", [128, 1])
        epsb = sb1("epsb", [128, 1])
        ptA = ph.enter_context(nc.psum_tensor("ptA", [128, 2048], F32))
        ptB = ph.enter_context(nc.psum_tensor("ptB", [128, 2048], F32))

        s_cst = sem("s_cst")
        s_x = [sem(f"s_x{j}") for j in range(3)]
        s_bn = sem("s_bn")
        s_sd = sem("s_sd")
        s_rc = sem("s_rc")
        s_xn = sem("s_xn")
        s_tp = sem("s_tp")
        s_cpX = sem("s_cpX")
        s_hic = sem("s_hic")
        s_lo = sem("s_lo")
        s_sp = [sem(f"s_sp{j}") for j in range(2)]
        s_spx = [sem(f"s_spx{j}") for j in range(3)]

        def fm_dst(base_h, m):
            return bass.AP(base_h, m * 128,
                           [[S, 128], [128 * S, NK], [1, 128]])

        with nc.Block("ph1") as blk:
            @blk.sync
            def _(sync):
                for dst, srcd in [(identf, id_d), (sel2, sel_d),
                                  (ba, ba_d), (bx, bx_d), (lam, lam_d),
                                  (brs, br_d)]:
                    sync.dma_start(dst[:], srcd).then_inc(s_cst, 16)
                sync.dma_start(
                    wrf[:],
                    bass.AP(wrT_h, 0, [[8, 128], [128 * 8, NK], [1, 8]]),
                ).then_inc(s_wr, 16)
                sync.dma_start(
                    wfa[:],
                    bass.AP(waT_h, 0, [[D, 128], [128 * D, NK], [1, 128]]),
                ).then_inc(s_wda, 16)
                for m in range(NM):
                    if m >= 3:
                        sync.wait_ge(s_xn, m - 2)
                        sync.wait_ge(s_tp, 32 * (m - 2))
                    sync.dma_start(xr[m % 3][:],
                                   xin[ts(m, 128), :]).then_inc(s_x[m % 3], 16)
                for j in range(2):
                    sync.wait_ge(s_sp[j], 32 * 4)
                for k in range(8):
                    sync.dma_start(xc2h0[:, ts(k, W2)],
                                   xph[ts(k, 128), ts(0, W2)]).then_inc(s_xc2[0], 16)
                    if k % 2 == 1:
                        sync.dma_start(xc2l[:, ts(k, W2)],
                                       xpl[ts(k, 128),
                                           ts(0, W2)]).then_inc(s_xc2[0], 16)

            @blk.scalar
            def _(scalar):
                scalar = Ser(scalar, ssS, stS)
                scalar.dma_start(
                    wfx[:],
                    bass.AP(wxT_h, 0, [[D, 128], [128 * D, NK], [1, 128]]),
                ).then_inc(s_wdx, 16)
                scalar.wait_ge(s_cst, 16 * 6)
                scalar.activation(cng[:], lam[:], AF.Exp, scale=-1.0)
                scalar.activation(lam[:], cng[:], AF.Ln, bias=1.0)
                scalar.activation(cng[:], lam[:], AF.Copy, scale=-8.0)

                def hicast(j):
                    scalar.wait_ge(s_tp, 32 * j + 16)
                    if j >= 2:
                        scalar.wait_ge(s_sp[j % 2], 32 * (j // 2))
                    scalar.activation(HB[j % 2][:], ptA[:],
                                      AF.Copy).then_inc(s_hic, 1)

                def xbev(j):
                    scalar.wait_ge(s_tp, 32 * j + 32)
                    if j >= 3:
                        scalar.wait_ge(s_spx[j % 3], 16 * (j // 3))
                    scalar.activation(XB[j % 3][:], ptB[:],
                                      AF.Copy).then_inc(s_cpX, 1)
                    scalar.wait_ge(s_cpX, j + 1)
                    scalar.dma_start(fm_dst(xfm_h, j),
                                     XB[j % 3][:]).then_inc(s_spx[j % 3], 16)

                for m in range(NM):
                    scalar.wait_ge(s_bn, m + 1)
                    if m >= 1:
                        scalar.wait_ge(s_rc, m)
                    scalar.activation(stdt[:], st2[:, 1:2], AF.Sqrt,
                                      bias=epsb[:]).then_inc(s_sd, 1)
                    if m >= 1:
                        hicast(m - 1)
                        xbev(m - 1)
                    if m == 10:
                        scalar.wait_ge(s_wdx, 16)
                        scalar.activation(wrx_h0[:], wfx[:],
                                          AF.Copy).then_inc(s_wcx, 1)
                hicast(NM - 1)
                xbev(NM - 1)

            @blk.vector
            def _(vector):
                vector = Ser(vector, ssV, stV)
                vector.memset(epsb[:], EPS)

                def losub(j):
                    vector.wait_ge(s_hic, j + 1)
                    vector.tensor_tensor(LB[j % 2][:], ptA[:],
                                         HB[j % 2][:],
                                         ALU.subtract).then_inc(s_lo, 1)

                for m in range(NM):
                    vector.wait_ge(s_x[m % 3], 16 * (m // 3 + 1))
                    if m >= 1:
                        vector.wait_ge(s_sd, m)
                    for q in range(4):
                        vector.bn_stats(st6[:, 6 * q:6 * q + 6],
                                        xr[m % 3][:, ts(q, 512)])
                    vector.bn_aggr(st2[:], st6[:]).then_inc(s_bn, 1)
                    vector.wait_ge(s_sd, m + 1)
                    vector.reciprocal(rstd[:], stdt[:]).then_inc(s_rc, 1)
                    vector.tensor_scalar_mul(nmb[:], st2[:, 0:1], -1.0)
                    if m >= 1:
                        vector.wait_ge(s_tp, 32 * (m - 1) + 16)
                    vector.tensor_scalar(xnb[:], xr[m % 3][:], nmb[:],
                                         rstd[:], ALU.add,
                                         ALU.mult).then_inc(s_xn, 1)
                    if m == 10:
                        vector.wait_ge(s_wda, 16)
                        vector.tensor_copy(wra_h0[:], wfa[:]).then_inc(s_wca, 1)
                    if m == 11:
                        vector.tensor_tensor(wra_l0[:], wfa[:], wra_h0[:],
                                             ALU.subtract).then_inc(s_wsa, 1)
                    if m == 13:
                        vector.wait_ge(s_wcx, 1)
                        vector.tensor_tensor(wrx_l0[:], wfx[:], wrx_h0[:],
                                             ALU.subtract).then_inc(s_wsg, 1)
                    if m >= 1:
                        losub(m - 1)
                losub(NM - 1)

            @blk.tensor
            def _(tensor):
                for m in range(NM):
                    tensor.wait_ge(s_xn, m + 1)
                    if m >= 1:
                        tensor.wait_ge(s_lo, m)
                    for k in range(NK):
                        tensor.transpose(ptA[:, k * 128:k * 128 + 128],
                                         xnb[:, ts(k, 128)],
                                         identf[:]).then_inc(s_tp, 1)
                    if m >= 1:
                        tensor.wait_ge(s_cpX, m)
                    for k in range(NK):
                        tensor.transpose(ptB[:, k * 128:k * 128 + 128],
                                         xr[m % 3][:, ts(k, 128)],
                                         identf[:]).then_inc(s_tp, 1)

            @blk.gpsimd
            def _(gpsimd):
                for j in range(NM):
                    gpsimd.wait_ge(s_hic, j + 1)
                    gpsimd.dma_start(fm_dst(xph_h, j),
                                     HB[j % 2][:]).then_inc(s_sp[j % 2], 16)
                    gpsimd.wait_ge(s_lo, j + 1)
                    gpsimd.dma_start(fm_dst(xpl_h, j),
                                     LB[j % 2][:]).then_inc(s_sp[j % 2], 16)
                    if j >= 8:
                        if j == 8:
                            for q in range(2):
                                gpsimd.wait_ge(s_sp[q], 32 * 4)
                        k = 2 * (j - 8)
                        gpsimd.dma_start(xc2l[:, ts(k, W2)],
                                         xpl[ts(k, 128), ts(0, W2)]
                                         ).then_inc(s_xc2g if k < 8
                                                    else s_xc2g2, 16)
                for j in range(2):
                    gpsimd.wait_ge(s_sp[j], 32 * (NM // 2))
                for j in range(3):
                    gpsimd.wait_ge(s_spx[j], 16 * len(range(j, NM, 3)))

    with ExitStack() as ph:
        sb1 = lambda n, s, dt=F32: ph.enter_context(nc.sbuf_tensor(n, s, dt))
        xc2h = [xc2h0, sb1("xc2h1", [128, NK * W2], BF16)]
        wra_h = [wra_h0, sb1("wrah1", [128, NK * 128], BF16)]
        wra_l = [wra_l0, sb1("wral1", [128, NK * 128], BF16)]
        wrx_h = [wrx_h0, sb1("wrxh1", [128, NK * 128], BF16)]
        wrx_l = [wrx_l0, sb1("wrxl1", [128, NK * 128], BF16)]
        isb = [sb1(f"isb{j}", [128, W2]) for j in range(2)]
        aab = [sb1(f"aab{j}", [128, W2]) for j in range(2)]
        sfb = [sb1(f"sfb{j}", [128, W2]) for j in range(2)]
        uub = sb1("uub", [128, W2])
        car = sb1("car", [128, 16])
        pr2 = [ph.enter_context(nc.psum_tensor(f"pr2{j}", [128, W2], F32))
               for j in range(2)]
        pi2 = [ph.enter_context(nc.psum_tensor(f"pi2{j}", [128, W2], F32))
               for j in range(2)]

        s_rec = sem("s_rec")
        s_mmA = sem("s_mmA")
        s_acA = sem("s_acA")
        s_sfc = sem("s_sfc")
        s_vA = sem("s_vA")
        s_ya = sem("s_ya")

        NIDX2 = NT2 * NH

        with nc.Block("ph2") as blk:
            @blk.sync
            def _(sync):
                for k in range(8, NK, 2):
                    sync.dma_start(xc2h0[:, ts(k, W2)],
                                   xph[ts(k, 128),
                                       ts(0, W2)]).then_inc(s_xc2s[0], 16)
                for idx2 in range(1, NIDX2):
                    h = idx2 % NH
                    if 4 <= idx2 <= 11:
                        for k in (2 * (idx2 - 4), 2 * (idx2 - 4) + 1):
                            sync.dma_start(xc3h0[:, ts(k, W3)],
                                           xph[ts(k, 128),
                                               ts(0, W3)]).then_inc(s_xc30, 16)
                            sync.dma_start(xc3l0[:, ts(k, W3)],
                                           xpl[ts(k, 128),
                                               ts(0, W3)]).then_inc(s_xc30, 16)
                    sync.wait_ge(s_wsa, idx2)
                    sync.dma_start(
                        wfa[:],
                        bass.AP(waT_h, h * 128,
                                [[D, 128], [128 * D, NK], [1, 128]]),
                    ).then_inc(s_wda, 16)
                    if idx2 == NH:
                        sync.wait_ge(s_mmA, NH)
                        for k in (1, 3, 5, 7):
                            sync.dma_start(xc2l[:, ts(k, W2)],
                                           xpl[ts(k, 128),
                                               ts(1, W2)]).then_inc(s_xc2t2, 16)
                    if idx2 == 2:
                        for k in range(0, NK, 2):
                            sync.dma_start(xc2h[1][:, ts(k, W2)],
                                           xph[ts(k, 128),
                                               ts(1, W2)]).then_inc(s_xc2[1], 16)

            @blk.tensor
            def _(tensor):
                for tb in range(NT2):
                    if tb == 0:
                        tensor.wait_ge(s_xc2[0], 192)
                        tensor.wait_ge(s_xc2g, 64)
                    else:
                        tensor.wait_ge(s_xc2[1], 128)
                        tensor.wait_ge(s_xc2s[1], 128)
                        tensor.wait_ge(s_xc2g, 192)
                        tensor.wait_ge(s_xc2g2, 64)
                        tensor.wait_ge(s_xc2gs, 64)
                        tensor.wait_ge(s_xc2t2, 64)
                    for h in range(NH):
                        idx2 = tb * NH + h
                        p2 = idx2 % 2
                        tensor.wait_ge(s_wsa, idx2 + 1)
                        tensor.wait_ge(s_wsg, idx2 + 1)
                        if idx2 >= 2:
                            tensor.wait_ge(s_acA, idx2 - 1)
                        pr, pi = pr2[p2], pi2[p2]
                        mm = None
                        for k in range(NK):
                            if tb == 0 and h == 0 and k == 8:
                                tensor.wait_ge(s_xc2s[0], 64)
                                tensor.wait_ge(s_xc2u, 128)
                                tensor.wait_ge(s_xc2g2, 64)
                            first = (k == 0)
                            last = (k == NK - 1)
                            for wh, wl, po in ((wra_h, wra_l, pr),
                                               (wrx_h, wrx_l, pi)):
                                whk = wh[p2][:, ts(k, 128)]
                                wlk = wl[p2][:, ts(k, 128)]
                                for half in range(2):
                                    rhs_h = xc2h[tb][:, k * W2 + half * 512:
                                                 k * W2 + half * 512 + 512]
                                    rhs_l = xc2l[:, k * W2 + half * 512:
                                                 k * W2 + half * 512 + 512]
                                    dst = po[:, half * 512:half * 512 + 512]
                                    tensor.matmul(dst, whk, rhs_h,
                                                  start=first, stop=False,
                                                  skip_group_check=True)
                                    tensor.matmul(dst, whk, rhs_l,
                                                  start=False, stop=False,
                                                  skip_group_check=True)
                                    mm = tensor.matmul(dst, wlk, rhs_h,
                                                       start=False, stop=last,
                                                       skip_group_check=True)
                        mm.then_inc(s_mmA, 1)

            @blk.scalar
            def _(scalar):
                scalar = Ser(scalar, ssS, stS)
                for k in range(9, NK, 2):
                    scalar.dma_start(xc2h0[:, ts(k, W2)],
                                     xph[ts(k, 128),
                                         ts(0, W2)]).then_inc(s_xc2u, 16)
                    scalar.dma_start(xc2l[:, ts(k, W2)],
                                     xpl[ts(k, 128),
                                         ts(0, W2)]).then_inc(s_xc2u, 16)

                def wsplit(j):
                    pj = j % 2
                    hj = j % NH
                    if j >= 1:
                        scalar.wait_ge(s_wsg, j)
                        scalar.dma_start(
                            wfx[:],
                            bass.AP(wxT_h, hj * 128,
                                    [[D, 128], [128 * D, NK], [1, 128]]),
                        ).then_inc(s_wdx, 16)
                    scalar.wait_ge(s_wda, 16 * (j + 1))
                    if j >= 2:
                        scalar.wait_ge(s_mmA, j - 1)
                    scalar.activation(wra_h[pj][:], wfa[:],
                                      AF.Copy).then_inc(s_wca, 1)
                    scalar.wait_ge(s_wdx, 16 * (j + 1))
                    scalar.activation(wrx_h[pj][:], wfx[:],
                                      AF.Copy).then_inc(s_wcx, 1)

                for idx2 in range(NIDX2):
                    h = idx2 % NH
                    p2 = idx2 % 2
                    if idx2 == 2:
                        for k in range(1, NK, 2):
                            scalar.dma_start(xc2h[1][:, ts(k, W2)],
                                             xph[ts(k, 128),
                                                 ts(1, W2)]).then_inc(s_xc2s[1], 16)
                    if idx2 == NH:
                        scalar.wait_ge(s_mmA, NH)
                        for k in (9, 11, 13, 15):
                            scalar.wait_ge(s_rec, k + 1)
                            scalar.dma_start(xc2l[:, ts(k, W2)],
                                             xpl[ts(k, 128),
                                                 ts(1, W2)]).then_inc(s_xc2gs, 16)
                    if idx2 + 1 < NIDX2:
                        wsplit(idx2 + 1)
                    scalar.wait_ge(s_mmA, idx2 + 1)
                    if idx2 >= 2:
                        scalar.wait_ge(s_vA, idx2 - 1)
                        scalar.wait_ge(s_ya, 16 * (idx2 - 1))
                    scalar.activation(aab[p2][:], pr2[p2][:], AF.Sigmoid,
                                      bias=ba[:, h:h + 1])
                    scalar.activation(isb[p2][:], pi2[p2][:], AF.Sigmoid,
                                      bias=bx[:, h:h + 1]).then_inc(s_acA, 1)
                    scalar.activation(aab[p2][:], aab[p2][:], AF.Exp,
                                      scale=cng[:, h:h + 1])
                    scalar.activation(sfb[p2][:], aab[p2][:], AF.Square)
                    scalar.activation(sfb[p2][:], sfb[p2][:], AF.Sqrt,
                                      scale=-1.0, bias=1.0).then_inc(s_sfc, 1)
                    if idx2 == NIDX2 - 2:
                        scalar.wait_ge(s_wr, 16)
                        scalar.activation(wrh[:], wrf[:],
                                          AF.Copy).then_inc(s_wrc, 1)

            @blk.vector
            def _(vector):
                vector = Ser(vector, ssV, stV)

                def wsub(j):
                    pj = j % 2
                    vector.wait_ge(s_wca, j + 1)
                    if j >= 2:
                        vector.wait_ge(s_mmA, j - 1)
                    vector.tensor_tensor(wra_l[pj][:], wfa[:],
                                         wra_h[pj][:],
                                         ALU.subtract).then_inc(s_wsa, 1)

                for idx2 in range(NIDX2):
                    tb, h = idx2 // NH, idx2 % NH
                    p2 = idx2 % 2
                    if idx2 + 1 < NIDX2:
                        wsub(idx2 + 1)
                    vector.wait_ge(s_sfc, idx2 + 1)
                    vector.tensor_tensor(uub[:], sfb[p2][:], isb[p2][:], ALU.mult)
                    vector.tensor_tensor(sfb[p2][:], xc2h[tb][:, ts(h, W2)],
                                         xc2l[:, ts(h, W2)],
                                         ALU.add).then_inc(s_rec, 1)
                    vector.tensor_tensor(uub[:], uub[:], sfb[p2][:], ALU.mult)
                    if idx2 >= 1:
                        vector.wait_ge(s_ya, 16 * idx2)
                    init = 0.0 if tb == 0 else car[:, h:h + 1]
                    last = vector.tensor_tensor_scan(isb[p2][:], aab[p2][:],
                                                     uub[:], init,
                                                     ALU.mult, ALU.add)
                    if tb + 1 < NT2:
                        last = vector.tensor_copy(car[:, h:h + 1],
                                                  isb[p2][:, W2 - 1:W2])
                    last.then_inc(s_vA, 1)
                    if idx2 == NIDX2 - 2:
                        vector.wait_ge(s_wrc, 1)
                        vector.tensor_tensor(wrl[:], wrf[:], wrh[:],
                                             ALU.subtract).then_inc(s_wrs, 1)

            @blk.gpsimd
            def _(gpsimd):
                def wsubg(j):
                    pj = j % 2
                    gpsimd.wait_ge(s_wcx, j + 1)
                    if j >= 2:
                        gpsimd.wait_ge(s_mmA, j - 1)
                    gpsimd.tensor_tensor(wrx_l[pj][:], wfx[:],
                                         wrx_h[pj][:],
                                         ALU.subtract).then_inc(s_wsg, 1)

                for idx2 in range(NIDX2):
                    tb, h = idx2 // NH, idx2 % NH
                    if idx2 + 1 < NIDX2:
                        wsubg(idx2 + 1)
                    if idx2 == NH - 1:
                        gpsimd.wait_ge(s_mmA, NH)
                        for k in range(0, NK, 2):
                            gpsimd.wait_ge(s_rec, k + 1)
                            gpsimd.dma_start(xc2l[:, ts(k, W2)],
                                             xpl[ts(k, 128),
                                                 ts(1, W2)]).then_inc(s_xc2g, 16)
                    gpsimd.wait_ge(s_vA, idx2 + 1)
                    gpsimd.dma_start(yat[ts(h, 128), ts(tb, W2)],
                                     isb[idx2 % 2][:]).then_inc(s_ya, 16)
                gpsimd.wait_ge(s_ya, 16 * NIDX2)

    s2pre.close()

    with ExitStack() as ph:
        sb1 = lambda n, s, dt=F32: ph.enter_context(nc.sbuf_tensor(n, s, dt))
        LG = sb1("LG", [8, S])
        GR = LG
        LTt = sb1("LTt", [128, 32])
        LSt = sb1("LSt", [128, 32])
        EXt = sb1("EXt", [128, 32])
        MX4 = sb1("MX4", [128, 4])
        SM4 = sb1("SM4", [128, 4])
        RC4 = sb1("RC4", [128, 4])
        TM4 = sb1("TM4", [128, 4])
        PRt = sb1("PRt", [128, 32])
        PMt = sb1("PMt", [128, 32])
        C1t = sb1("C1t", [128, 32])
        C2t = sb1("C2t", [128, 32])
        MKt = sb1("MKt", [128, 32])
        Gt = sb1("Gt", [128, 32])
        aph = sb1("aph", [128, NK * ER], BF16)
        apl = sb1("apl", [128, NK * ER], BF16)
        bth = sb1("bth", [128, NEB * D], BF16)
        btl = sb1("btl", [128, NEB * D], BF16)
        apf = [sb1(f"apf{j}", [128, ER]) for j in range(2)]
        apfS = [sb1(f"apfS{j}", [128, ER]) for j in range(2)]
        btf = [sb1(f"btf{j}", [128, W3]) for j in range(2)]
        btq = [apfS[0], apfS[1], apf[0], apf[1]]
        apq = [apf[0], apf[1], btf[0], btf[1]]
        apqS = None
        xc3h = [xc3h0, sb1("xc3h1", [128, NK * W3], BF16)]
        xc3l = [xc3l0, sb1("xc3l1", [128, NK * W3], BF16)]
        GBC = sb1("GBC", [128, W3])
        GR2 = [sb1(f"GR2{j}", [2, W3]) for j in range(2)]
        HGF = [sb1(f"HGF{j}", [128, W3]) for j in range(2)]
        apqS = [apfS[0], apfS[1], HGF[0], HGF[1]]
        hgh = [sb1(f"hgh{j}", [128, NEB * W3], BF16) for j in range(2)]
        hgl = [sb1(f"hgl{j}", [128, NEB * W3], BF16) for j in range(2)]
        YA = [sb1(f"YA{j}", [128, W3]) for j in range(4)]
        XF3 = [sb1(f"XF3{j}", [128, W3]) for j in range(4)]
        T1 = sb1("T1", [128, W3])
        OT = [sb1(f"OT{j}", [128, W3]) for j in range(4)]
        pH = [ph.enter_context(nc.psum_tensor(f"pH{j}", [128, 512], F32))
              for j in range(4)]
        pY = [ph.enter_context(nc.psum_tensor(f"pY{j}", [128, 512], F32))
              for j in range(4)]

        s_mR = sem("s_mR")
        s_lg = sem("s_lg")
        s_lgT = sem("s_lgT")
        s_ltv = sem("s_ltv")
        s_ls = sem("s_ls")
        s_sex = sem("s_sex")
        s_g3 = sem("s_g3")
        s_gT = sem("s_gT")
        s_gr = sem("s_gr")
        s_apd = [sem(f"s_apd{j}") for j in range(4)]
        s_aps = sem("s_aps")
        s_apdS = [sem(f"s_apdS{j}") for j in range(4)]
        s_apcS = sem("s_apcS")
        s_apsS = sem("s_apsS")
        s_btd = [sem(f"s_btd{j}") for j in range(2)]
        s_bts = sem("s_bts")
        s_apcE = sem("s_apcE")
        s_btcE = sem("s_btcE")
        s_btcS = sem("s_btcS")
        s_btsO = sem("s_btsO")
        s_btdS = [sem(f"s_btdS{j}") for j in range(4)]
        s_xc3 = [sem(f"s_xc3{j}") for j in range(2)]
        s_xc3s = [sem(f"s_xc3s{j}") for j in range(2)]
        s_gr2 = [sem(f"s_gr2{j}") for j in range(2)]
        s_gbm = sem("s_gbm")
        s_gbv = sem("s_gbv")
        s_mH = sem("s_mH")
        s_hgf = sem("s_hgf")
        s_hgc = sem("s_hgc")
        s_hgs = sem("s_hgs")
        s_mY = sem("s_mY")
        s_cio = [sem(f"s_cio{j}") for j in range(4)]
        s_cmp = sem("s_cmp")
        s_cmb = sem("s_cmb")
        s_out = [sem(f"s_out{j}") for j in range(4)]

        def g4(t, off, d3=8):
            return bass.AP(t, off, [[32, 128], [8, 4], [1, d3]])

        def g4b(t, off):
            return bass.AP(t, off, [[4, 128], [1, 4], [0, 8]])

        def g4s(t, off):
            return bass.AP(t, off, [[32, 128], [8, 4]])

        def hmm_group(tensor, tb, eb, apwait=False):
            idx = tb * NEB + eb
            if idx >= 4:
                tensor.wait_ge(s_hgf, idx - 3)
            mm = None
            for k in range(NK):
                if apwait:
                    if k % 2 == 0:
                        tensor.wait_ge(s_aps, k // 2 + 1)
                    else:
                        tensor.wait_ge(s_apsS, k // 2 + 1)
                first, last = (k == 0), (k == NK - 1)
                ah = aph[:, k * ER + eb * 128: k * ER + eb * 128 + 128]
                al = apl[:, k * ER + eb * 128: k * ER + eb * 128 + 128]
                rh = xc3h[tb % 2][:, ts(k, W3)]
                rl = xc3l[tb % 2][:, ts(k, W3)]
                tensor.matmul(pH[eb][:], ah, rh, start=first,
                              stop=False, skip_group_check=True)
                tensor.matmul(pH[eb][:], ah, rl, start=False,
                              stop=False, skip_group_check=True)
                mm = tensor.matmul(pH[eb][:], al, rh, start=False,
                                   stop=last, skip_group_check=True)
            mm.then_inc(s_mH, 1)

        def gbc_mm(tensor, tb, eb):
            idx = tb * NEB + eb
            tensor.wait_ge(s_gr2[idx % 2], 16 * (idx // 2 + 1))
            if eb == 0:
                tensor.wait_ge(s_gr, tb + 1)
            tensor.matmul(pY[tb % 4][:], sel2[:], GR2[idx % 2][:],
                          start=True, stop=True,
                          skip_group_check=True).then_inc(s_gbm, 1)
            tensor.wait_ge(s_gbv, idx + 1)

        with nc.Block("ph3") as blk:
            @blk.sync
            def _(sync):
                def xc3_loads(tb):
                    if tb >= 2:
                        sync.wait_ge(s_mH, NEB * (tb - 1))
                    for k in range(NK):
                        sync.dma_start(xc3h[tb % 2][:, ts(k, W3)],
                                       xph[ts(k, 128),
                                           ts(tb, W3)]).then_inc(s_xc3[tb % 2], 16)
                        if k % 2 == 0:
                            sync.dma_start(xc3l[tb % 2][:, ts(k, W3)],
                                           xpl[ts(k, 128),
                                               ts(tb, W3)]).then_inc(s_xc3[tb % 2], 16)

                for tb in range(NT3):
                    if tb + 1 < NT3:
                        xc3_loads(tb + 1)
                    sync.wait_ge(s_gr, tb + 1)
                    for eb in range(NEB):
                        idx = tb * NEB + eb
                        if idx >= 2:
                            sync.wait_ge(s_gbm, idx - 1)
                        sync.dma_start(GR2[idx % 2][:],
                                       GR[2 * eb:2 * eb + 2, ts(tb, W3)]
                                       ).then_inc(s_gr2[idx % 2], 16)
                    for o in range(NH):
                        idx4 = tb * NH + o
                        if idx4 >= 4:
                            sync.wait_ge(s_cmb, idx4 - 3)
                        sync.dma_start(YA[idx4 % 4][:],
                                       yat[ts(o, 128),
                                           ts(tb, W3)]).then_inc(s_cio[idx4 % 4], 16)
                        sync.dma_start(XF3[idx4 % 4][:],
                                       xfm[ts(o, 128),
                                           ts(tb, W3)]).then_inc(s_cio[idx4 % 4], 16)

            @blk.tensor
            def _(tensor):
                tensor.wait_ge(s_wrs, 1)
                for tb in range(NT3):
                    if tb == 0:
                        tensor.wait_ge(s_xc30, 512)
                    else:
                        tensor.wait_ge(s_xc3[tb % 2], 16 * 24 * ((tb + 1) // 2))
                        tensor.wait_ge(s_xc3s[tb % 2], 16 * 8 * ((tb + 1) // 2))
                    BR = pY[tb % 4]
                    if tb >= 1:
                        tensor.wait_ge(s_cmp, 16 * tb)
                    for k in range(NK):
                        first, last = (k == 0), (k == NK - 1)
                        wh = wrh[:, ts(k, 8)]
                        wl = wrl[:, ts(k, 8)]
                        tensor.matmul(BR[:8, :], wh, xc3h[tb % 2][:, ts(k, W3)],
                                      start=first, stop=False, skip_group_check=True)
                        tensor.matmul(BR[:8, :], wh, xc3l[tb % 2][:, ts(k, W3)],
                                      start=False, stop=False, skip_group_check=True)
                        tensor.matmul(BR[:8, :], wl, xc3h[tb % 2][:, ts(k, W3)],
                                      start=False, stop=last,
                                      skip_group_check=True).then_inc(s_mR, 1)
                    if tb == 0:
                        hmm_group(tensor, 0, 0, apwait=True)
                    tensor.wait_ge(s_lg, tb + 1)
                    tr = None
                    for j in range(4):
                        tr = tensor.transpose(BR[:, j * 8:j * 8 + 8],
                                              LG[:8, ts(4 * tb + j, 128)],
                                              identf[:8, :8])
                    tr.then_inc(s_lgT, 1)
                    if tb != 0:
                        hmm_group(tensor, tb, 0)
                    else:
                        hmm_group(tensor, 0, 1)
                    tensor.wait_ge(s_g3, tb + 1)
                    tensor.wait_ge(s_ltv, tb + 1)
                    tr = None
                    for j in range(4):
                        tr = tensor.transpose(BR[:8, ts(j, 128)],
                                              Gt[:, ts(j, 8)], identf[:])
                    tr.then_inc(s_gT, 1)
                    hmm_group(tensor, tb, 1 if tb != 0 else 2)
                    gbc_mm(tensor, tb, 0)
                    gbc_mm(tensor, tb, 1)
                    if tb != 0:
                        hmm_group(tensor, tb, 2)
                    gbc_mm(tensor, tb, 2)
                    gbc_mm(tensor, tb, 3)
                    hmm_group(tensor, tb, 3)
                    tensor.wait_ge(s_hgs, NEB * (tb + 1))
                    tensor.wait_ge(s_hgc, NEB * (tb + 1))
                    if tb == 0:
                        tensor.wait_ge(s_bts, 8)
                        tensor.wait_ge(s_btsO, 8)

                    for o in range(NH):
                        idx4 = tb * NH + o
                        if idx4 >= 4:
                            tensor.wait_ge(s_cmp, idx4 - 3)
                        po = pY[idx4 % 4]
                        mm = None
                        for kb in range(NEB):
                            first, last = (kb == 0), (kb == NEB - 1)
                            bh = bth[:, kb * D + o * 128: kb * D + o * 128 + 128]
                            bl = btl[:, kb * D + o * 128: kb * D + o * 128 + 128]
                            rh = hgh[tb % 2][:, ts(kb, W3)]
                            rl = hgl[tb % 2][:, ts(kb, W3)]
                            tensor.matmul(po[:], bh, rh, start=first,
                                          stop=False, skip_group_check=True)
                            tensor.matmul(po[:], bh, rl, start=False,
                                          stop=False, skip_group_check=True)
                            mm = tensor.matmul(po[:], bl, rh, start=False,
                                               stop=last, skip_group_check=True)
                        mm.then_inc(s_mY, 1)

            @blk.scalar
            def _(scalar):
                scalar = Ser(scalar, ssS, stS)

                def xc3s_loads(tb):
                    if tb >= 2:
                        scalar.wait_ge(s_mH, NEB * (tb - 1))
                    for k in range(1, NK, 2):
                        scalar.dma_start(xc3l[tb % 2][:, ts(k, W3)],
                                         xpl[ts(k, 128),
                                             ts(tb, W3)]).then_inc(s_xc3s[tb % 2], 16)

                for i0 in range(4):
                    scalar.dma_start(apqS[i0][:],
                                     apk[ts(2 * i0 + 1, 128), :]
                                     ).then_inc(s_apdS[i0], 16)
                for i, k in enumerate(range(1, NK, 2)):
                    scalar.wait_ge(s_apdS[i % 4], 16 * (i // 4 + 1))
                    scalar.activation(aph[:, ts(k, ER)], apqS[i % 4][:],
                                      AF.Copy).then_inc(s_apcS, 1)
                    if i + 4 < NK // 2:
                        scalar.wait_ge(s_apsS, i + 1)
                        scalar.dma_start(apqS[i % 4][:],
                                         apk[ts(2 * i + 9, 128), :]
                                         ).then_inc(s_apdS[i % 4], 16)
                    scalar.wait_ge(s_apd[i % 4], 16 * (i // 4 + 1))
                    scalar.activation(aph[:, ts(2 * i, ER)], apq[i % 4][:],
                                      AF.Copy).then_inc(s_apcE, 1)

                def bt_odd_dma(i):
                    jo = 2 * i + 1
                    if i >= 4:
                        scalar.wait_ge(s_btsO, i - 3)
                    scalar.dma_start(btq[i % 4][:],
                                     btp[ts(jo // 4, 128),
                                         ts(jo % 4, W3)]).then_inc(s_btdS[i % 4], 16)

                def bt_casts(i):
                    je, jo = 2 * i, 2 * i + 1
                    sle = slice((je // 4) * D + (je % 4) * W3,
                                (je // 4) * D + (je % 4) * W3 + W3)
                    slo = slice((jo // 4) * D + (jo % 4) * W3,
                                (jo // 4) * D + (jo % 4) * W3 + W3)
                    scalar.wait_ge(s_btd[i % 2], 16 * (i // 2 + 1))
                    scalar.activation(bth[:, sle], btf[i % 2][:],
                                      AF.Copy).then_inc(s_btcE, 1)
                    scalar.wait_ge(s_btdS[i % 4], 16 * (i // 4 + 1))
                    scalar.activation(bth[:, slo], btq[i % 4][:],
                                      AF.Copy).then_inc(s_btcS, 1)

                scalar.wait_ge(s_apsS, NK // 2)
                scalar.wait_ge(s_aps, NK // 2)
                for i in range(4):
                    bt_odd_dma(i)
                for tb in range(NT3):
                    scalar.wait_ge(s_mR, 16 * (tb + 1))
                    scalar.add(LG[:8, ts(tb, W3)], pY[tb % 4][:8, :],
                               brs[:]).then_inc(s_lg, 1)
                    if tb + 1 < NT3:
                        xc3s_loads(tb + 1)
                    scalar.wait_ge(s_ls, tb + 1)
                    scalar.activation(EXt[:], LSt[:], AF.Exp).then_inc(s_sex, 1)
                    if tb == 0:
                        for i in range(8):
                            if 2 <= i <= 5:
                                bt_odd_dma(i + 2)
                            bt_casts(i)
                    for eb in range(NEB):
                        idx = tb * NEB + eb
                        scalar.wait_ge(s_hgf, idx + 1)
                        if tb >= 2 and eb == 0:
                            scalar.wait_ge(s_mY, NH * (tb - 1))
                        scalar.activation(hgh[tb % 2][:, ts(eb, W3)],
                                          HGF[idx % 2][:],
                                          AF.Copy).then_inc(s_hgc, 1)

            @blk.vector
            def _(vector):
                vector = Ser(vector, ssV, stV)
                for i, k in enumerate(range(1, NK, 2)):
                    vector.wait_ge(s_apcS, i + 1)
                    vector.tensor_tensor(apl[:, ts(k, ER)], apqS[i % 4][:],
                                         aph[:, ts(k, ER)],
                                         ALU.subtract).then_inc(s_apsS, 1)
                for tb in range(NT3):
                    vector.wait_ge(s_lgT, tb + 1)
                    if tb >= 1:
                        vector.wait_ge(s_sex, tb)
                    vector.tensor_copy(LTt[:], pY[tb % 4][:, :32]).then_inc(s_ltv, 1)
                    vector.reduce_max(MX4[:], g4(LTt, 0), axis=AX)
                    vector.tensor_tensor(g4(LSt, 0), g4(LTt, 0), g4b(MX4, 0),
                                         ALU.subtract).then_inc(s_ls, 1)
                    vector.wait_ge(s_sex, tb + 1)
                    vector.reduce_sum(SM4[:], g4(EXt, 0), axis=AX)
                    vector.reciprocal(RC4[:], SM4[:])
                    vector.tensor_tensor(g4(PRt, 0), g4(EXt, 0), g4b(RC4, 0),
                                         ALU.mult)
                    vector.memset(PMt[:], -1.0)
                    for e in range(7):
                        vector.tensor_max(g4s(PMt, e + 1), g4s(PMt, e),
                                          g4s(PRt, e))
                    vector.tensor_max(TM4[:], g4s(PMt, 7), g4s(PRt, 7))
                    vector.tensor_tensor(C1t[:], PRt[:], PMt[:], ALU.is_gt)
                    vector.tensor_tensor(g4(C2t, 0), g4(PRt, 0), g4b(TM4, 0),
                                         ALU.is_ge)
                    vector.tensor_tensor(MKt[:], C1t[:], C2t[:], ALU.mult)
                    vector.scalar_tensor_tensor(Gt[:], PRt[:], 1.0 / 64.0,
                                                MKt[:], ALU.mult,
                                                ALU.mult).then_inc(s_g3, 1)
                    vector.wait_ge(s_gT, tb + 1)
                    vector.tensor_copy(GR[:8, ts(tb, W3)],
                                       pY[tb % 4][:8, :512]).then_inc(s_gr, 1)
                    for eb in range(NEB):
                        idx = tb * NEB + eb
                        vector.wait_ge(s_gbm, idx + 1)
                        vector.tensor_copy(GBC[:], pY[tb % 4][:]).then_inc(s_gbv, 1)
                        vector.wait_ge(s_mH, idx + 1)
                        if idx == 0:
                            vector.wait_ge(s_apcS, NK // 2)
                        if idx >= 2:
                            vector.wait_ge(s_hgc, idx - 1)
                        if tb >= 2 and eb == 0:
                            vector.wait_ge(s_mY, NH * (tb - 1))
                        vector.tensor_tensor(HGF[idx % 2][:], pH[eb][:], GBC[:],
                                             ALU.mult).then_inc(s_hgf, 1)
                        vector.wait_ge(s_hgc, idx + 1)
                        vector.tensor_tensor(hgl[tb % 2][:, ts(eb, W3)],
                                             HGF[idx % 2][:],
                                             hgh[tb % 2][:, ts(eb, W3)],
                                             ALU.subtract).then_inc(s_hgs, 1)
                    for o in range(NH):
                        idx4 = tb * NH + o
                        vector.wait_ge(s_mY, idx4 + 1)
                        vector.wait_ge(s_cio[idx4 % 4], 32 * (idx4 // 4 + 1))
                        vector.tensor_tensor(T1[:], pY[idx4 % 4][:],
                                             YA[idx4 % 4][:],
                                             ALU.add).then_inc(s_cmp, 1)
                        if idx4 >= 4:
                            vector.wait_ge(s_out[idx4 % 4], 16 * (idx4 // 4))
                        vector.tensor_tensor(OT[idx4 % 4][:], T1[:],
                                             XF3[idx4 % 4][:],
                                             ALU.add).then_inc(s_cmb, 1)

            @blk.gpsimd
            def _(gpsimd):
                for i0 in range(4):
                    gpsimd.dma_start(apq[i0][:],
                                     apk[ts(2 * i0, 128), :]
                                     ).then_inc(s_apd[i0], 16)
                for i, k in enumerate(range(0, NK, 2)):
                    gpsimd.wait_ge(s_apcE, i + 1)
                    gpsimd.tensor_tensor(apl[:, ts(k, ER)], apq[i % 4][:],
                                         aph[:, ts(k, ER)],
                                         ALU.subtract).then_inc(s_aps, 1)
                    if i + 4 < NK // 2:
                        gpsimd.wait_ge(s_aps, i + 1)
                        gpsimd.dma_start(apq[i % 4][:],
                                         apk[ts(2 * i + 8, 128), :]
                                         ).then_inc(s_apd[i % 4], 16)
                gpsimd.wait_ge(s_aps, NK // 2)
                gpsimd.dma_start(btf[0][:],
                                 btp[ts(0, 128), ts(0, W3)]).then_inc(s_btd[0], 16)
                for i in range(8):
                    je, jo = 2 * i, 2 * i + 1
                    sle = slice((je // 4) * D + (je % 4) * W3,
                                (je // 4) * D + (je % 4) * W3 + W3)
                    slo = slice((jo // 4) * D + (jo % 4) * W3,
                                (jo // 4) * D + (jo % 4) * W3 + W3)
                    if i + 1 < 8:
                        if i + 1 >= 2:
                            gpsimd.wait_ge(s_bts, i)
                        je2 = 2 * (i + 1)
                        gpsimd.dma_start(btf[(i + 1) % 2][:],
                                         btp[ts(je2 // 4, 128),
                                             ts(je2 % 4, W3)]
                                         ).then_inc(s_btd[(i + 1) % 2], 16)
                    gpsimd.wait_ge(s_btcE, i + 1)
                    gpsimd.tensor_tensor(btl[:, sle], btf[i % 2][:],
                                         bth[:, sle],
                                         ALU.subtract).then_inc(s_bts, 1)
                    gpsimd.wait_ge(s_btcS, i + 1)
                    gpsimd.tensor_tensor(btl[:, slo], btq[i % 4][:],
                                         bth[:, slo],
                                         ALU.subtract).then_inc(s_btsO, 1)
                for tb in range(NT3):
                    for o in range(NH):
                        idx4 = tb * NH + o
                        gpsimd.wait_ge(s_cmb, idx4 + 1)
                        gpsimd.dma_start(out[ts(o, 128), ts(tb, W3)],
                                         OT[idx4 % 4][:]).then_inc(s_out[idx4 % 4], 16)
                for j in range(4):
                    gpsimd.wait_ge(s_out[j], 16 * (NT3 * NH // 4))

    s3pre.close()
    top.close()
    return nc


def _prep_shared(inputs):
    f32 = np.float32
    Wa = np.asarray(inputs["L0_attn_Wa"], f32)
    Wx = np.asarray(inputs["L0_attn_Wx"], f32)
    A = np.asarray(inputs["L0_ffn_A"], f32)
    B = np.asarray(inputs["L0_ffn_B"], f32)
    Wr = np.asarray(inputs["L0_ffn_Wr"], f32)
    ba = np.asarray(inputs["L0_attn_ba"], f32)
    bx = np.asarray(inputs["L0_attn_bx"], f32)
    lam = np.asarray(inputs["L0_attn_Lam"], f32)
    br = np.asarray(inputs["L0_ffn_br"], f32)
    shared = {
        "waT": np.ascontiguousarray(Wa.T),
        "wxT": np.ascontiguousarray(Wx.T),
        "apk": np.ascontiguousarray(A.transpose(2, 0, 1).reshape(D, ER)),
        "btp": np.ascontiguousarray(B.transpose(0, 2, 1).reshape(ER, D)),
        "wrT": np.ascontiguousarray(Wr.T),
        "ba_t": np.ascontiguousarray(ba.reshape(16, 128).T),
        "bx_t": np.ascontiguousarray(bx.reshape(16, 128).T),
        "lam_t": np.ascontiguousarray(lam.reshape(16, 128).T),
        "br": np.ascontiguousarray(br.reshape(8, 1)),
        "ident": np.eye(128, dtype=f32),
        "ones1": np.ones((1, 128), dtype=f32),
        "sel2": np.repeat(np.eye(2, dtype=f32), 64, axis=1),
    }
    return shared


def kernel(**inputs):
    x = np.asarray(inputs["L0_attn_x"], np.float32)
    shared = _prep_shared(inputs)
    nc = build_nc(debug=False)
    in_maps = [dict(shared, x=np.ascontiguousarray(x[c])) for c in range(8)]
    res = run_bass_kernel_spmd(nc, in_maps, core_ids=list(range(8)))
    outp = np.empty((8, S, D), np.float32)
    for c, r in enumerate(res.results):
        outp[c] = np.asarray(r["out"]).T
    return outp


# version 32
# speedup vs baseline: 1.0398x; 1.0026x over previous
import numpy as np
import ml_dtypes
from contextlib import ExitStack

import concourse.bass as bass
import concourse.mybir as mybir
from concourse.bass_utils import run_bass_kernel_spmd

F32 = mybir.dt.float32
BF16 = mybir.dt.bfloat16
AF = mybir.ActivationFunctionType
ALU = mybir.AluOpType
AX = mybir.AxisListType.X

S = 2048
D = 2048
NM = 16
NK = 16
NH = 16
NT2 = 2
W2 = 1024
NT3 = 4
W3 = 512
ER = 512
NEB = 4
EPS = 1e-5


def ts(i, w):
    return slice(i * w, (i + 1) * w)


_SER_OPS = {
    "activation", "add", "mul", "copy", "square", "sqrt",
    "tensor_copy", "tensor_tensor", "tensor_scalar", "tensor_scalar_mul",
    "tensor_scalar_add", "tensor_add", "tensor_sub", "tensor_mul",
    "tensor_max", "reduce_sum", "reduce_max", "tensor_reduce", "reciprocal",
    "memset", "tensor_tensor_scan", "scalar_tensor_tensor", "select",
    "bn_stats", "bn_aggr",
}


class _SerInstr:
    def __init__(self, instr, ser):
        self._i = instr
        self._s = ser

    def then_inc(self, sem, n=1):
        self._i.then_inc(sem, n)
        self._s._note_inc(self._i, sem, n)
        return self


class Ser:
    def __init__(self, eng, own_sem, st):
        self._e = eng
        self._own = own_sem
        self._st = st

    def _note_inc(self, instr, sem, n):
        st = self._st
        key = id(sem)
        v = st["counts"].get(key, 0) + n
        st["counts"][key] = v
        if instr is st["pending"]:
            st["pending"] = None
        st["last"] = (sem, v)

    def _flush(self):
        st = self._st
        if st["pending"] is not None:
            st["n"] += 1
            st["pending"].then_inc(self._own, 1)
            st["last"] = (self._own, st["n"])
            st["pending"] = None
        if st["last"] is not None:
            sem, v = st["last"]
            self._e.wait_ge(sem, v)
            st["last"] = None

    def wait_ge(self, sem, v):
        self._e.wait_ge(sem, v)

    def __getattr__(self, name):
        fn = getattr(self._e, name)
        if name not in _SER_OPS:
            return fn

        def wrapped(*a, **k):
            self._flush()
            instr = fn(*a, **k)
            self._st["pending"] = instr
            return _SerInstr(instr, self)

        return wrapped


def build_nc(debug=False):
    nc = bass.Bass("TRN2", target_bir_lowering=False)

    xin = nc.dram_tensor("x", [S, D], F32, kind="ExternalInput").ap()
    waT_h = nc.dram_tensor("waT", [D, D], F32, kind="ExternalInput")
    waT = waT_h.ap()
    wxT_h = nc.dram_tensor("wxT", [D, D], F32, kind="ExternalInput")
    wxT = wxT_h.ap()
    apk = nc.dram_tensor("apk", [D, ER], F32, kind="ExternalInput").ap()
    btp = nc.dram_tensor("btp", [ER, D], F32, kind="ExternalInput").ap()
    wrT_h = nc.dram_tensor("wrT", [D, 8], F32, kind="ExternalInput")
    wrT = wrT_h.ap()
    ba_d = nc.dram_tensor("ba_t", [128, 16], F32, kind="ExternalInput").ap()
    bx_d = nc.dram_tensor("bx_t", [128, 16], F32, kind="ExternalInput").ap()
    lam_d = nc.dram_tensor("lam_t", [128, 16], F32, kind="ExternalInput").ap()
    br_d = nc.dram_tensor("br", [8, 1], F32, kind="ExternalInput").ap()
    id_d = nc.dram_tensor("ident", [128, 128], F32, kind="ExternalInput").ap()
    on_d = nc.dram_tensor("ones1", [1, 128], F32, kind="ExternalInput").ap()
    sel_d = nc.dram_tensor("sel2", [2, 128], F32, kind="ExternalInput").ap()

    out = nc.dram_tensor("out", [D, S], F32, kind="ExternalOutput").ap()

    dbg = "ExternalOutput" if debug else "Internal"
    xph_h = nc.dram_tensor("xph", [D, S], BF16, kind=dbg)
    xph = xph_h.ap()
    xpl_h = nc.dram_tensor("xpl", [D, S], BF16, kind=dbg)
    xpl = xpl_h.ap()
    xfm_h = nc.dram_tensor("xfm", [D, S], F32, kind=dbg)
    xfm = xfm_h.ap()
    yat = nc.dram_tensor("yat", [D, S], F32, kind=dbg).ap()

    top = ExitStack()
    sem = lambda name: top.enter_context(nc.semaphore(name))
    sbt = lambda name, shape, dt=F32: top.enter_context(nc.sbuf_tensor(name, shape, dt))

    ssS = sem("ssS")
    ssV = sem("ssV")
    stS = {"n": 0, "counts": {}, "pending": None, "last": None}
    stV = {"n": 0, "counts": {}, "pending": None, "last": None}

    identf = sbt("identf", [128, 128])
    sel2 = sbt("sel2s", [2, 128])
    ba = sbt("ba_s", [128, 16])
    bx = sbt("bx_s", [128, 16])
    lam = sbt("lam_s", [128, 16])
    cng = sbt("cng", [128, 16])
    brs = sbt("brs", [8, 1])

    s3pre = ExitStack()
    sb3p = lambda n, s, dt=F32: s3pre.enter_context(nc.sbuf_tensor(n, s, dt))
    xc3h0 = sb3p("xc3h0", [128, NK * W3], BF16)
    xc3l0 = sb3p("xc3l0", [128, NK * W3], BF16)
    wrf = sb3p("wrf", [128, 128])
    wrh = sb3p("wrh", [128, 128], BF16)
    wrl = sb3p("wrl", [128, 128], BF16)
    s2pre = ExitStack()
    sb2p = lambda n, s, dt=F32: s2pre.enter_context(nc.sbuf_tensor(n, s, dt))
    xc2h0 = sb2p("xc2h0", [128, NK * W2], BF16)
    xc2l = sb2p("xc2l", [128, NK * W2], BF16)
    wfa = sb2p("wfa", [128, NK * 128])
    wfx = sb2p("wfx", [128, NK * 128])
    wra_h0 = sb2p("wrah0", [128, NK * 128], BF16)
    wra_l0 = sb2p("wral0", [128, NK * 128], BF16)
    wrx_h0 = sb2p("wrxh0", [128, NK * 128], BF16)
    wrx_l0 = sb2p("wrxl0", [128, NK * 128], BF16)

    s_wr = sem("s_wr")
    s_wda = sem("s_wda")
    s_wdx = sem("s_wdx")
    s_xc2 = [sem(f"s_xc2{j}") for j in range(2)]
    s_xc2s = [sem(f"s_xc2s{j}") for j in range(2)]
    s_xc2g = sem("s_xc2g")
    s_xc2g2 = sem("s_xc2g2")
    s_xc2gs = sem("s_xc2gs")
    s_xc2t2 = sem("s_xc2t2")
    s_xc2u = sem("s_xc2u")
    s_xc30 = sem("s_xc3p0")
    s_wrc = sem("s_wrc")
    s_wrs = sem("s_wrs")
    s_wca = sem("s_wca")
    s_wcx = sem("s_wcx")
    s_wsa = sem("s_wsa")
    s_wsg = sem("s_wsg")

    with ExitStack() as ph:
        sb1 = lambda n, s, dt=F32: ph.enter_context(nc.sbuf_tensor(n, s, dt))
        xr = [sb1(f"xr{j}", [128, D]) for j in range(3)]
        xnb = sb1("xnb", [128, D])
        XB = [sb1(f"XB{j}", [128, D]) for j in range(3)]
        HB = [sb1(f"HB{j}", [128, D], BF16) for j in range(2)]
        LB = [sb1(f"LB{j}", [128, D], BF16) for j in range(2)]
        st6 = sb1("st6", [128, 24])
        st2 = sb1("st2", [128, 2])
        nmb = sb1("nmb", [128, 1])
        stdt = sb1("stdt", [128, 1])
        rstd = sb1("rstd", [128, 1])
        epsb = sb1("epsb", [128, 1])
        ptA = ph.enter_context(nc.psum_tensor("ptA", [128, 2048], F32))
        ptB = ph.enter_context(nc.psum_tensor("ptB", [128, 2048], F32))

        s_cst = sem("s_cst")
        s_x = [sem(f"s_x{j}") for j in range(3)]
        s_bn = sem("s_bn")
        s_sd = sem("s_sd")
        s_rc = sem("s_rc")
        s_xn = sem("s_xn")
        s_tp = sem("s_tp")
        s_cpX = sem("s_cpX")
        s_hic = sem("s_hic")
        s_lo = sem("s_lo")
        s_sp = [sem(f"s_sp{j}") for j in range(2)]
        s_spx = [sem(f"s_spx{j}") for j in range(3)]

        def fm_dst(base_h, m):
            return bass.AP(base_h, m * 128,
                           [[S, 128], [128 * S, NK], [1, 128]])

        with nc.Block("ph1") as blk:
            @blk.sync
            def _(sync):
                for dst, srcd in [(identf, id_d), (sel2, sel_d),
                                  (ba, ba_d), (bx, bx_d), (lam, lam_d),
                                  (brs, br_d)]:
                    sync.dma_start(dst[:], srcd).then_inc(s_cst, 16)
                for m in range(NM):
                    if m >= 3:
                        sync.wait_ge(s_xn, m - 2)
                        sync.wait_ge(s_tp, 32 * (m - 2))
                    sync.dma_start(xr[m % 3][:],
                                   xin[ts(m, 128), :]).then_inc(s_x[m % 3], 16)
                    if m == 1:
                        sync.dma_start(
                            wrf[:],
                            bass.AP(wrT_h, 0, [[8, 128], [128 * 8, NK], [1, 8]]),
                        ).then_inc(s_wr, 16)
                        sync.dma_start(
                            wfa[:],
                            bass.AP(waT_h, 0,
                                    [[D, 128], [128 * D, NK], [1, 128]]),
                        ).then_inc(s_wda, 16)
                for j in range(2):
                    sync.wait_ge(s_sp[j], 32 * 4)
                for k in range(8):
                    sync.dma_start(xc2h0[:, ts(k, W2)],
                                   xph[ts(k, 128), ts(0, W2)]).then_inc(s_xc2[0], 16)
                    if k % 2 == 1:
                        sync.dma_start(xc2l[:, ts(k, W2)],
                                       xpl[ts(k, 128),
                                           ts(0, W2)]).then_inc(s_xc2[0], 16)

            @blk.scalar
            def _(scalar):
                scalar = Ser(scalar, ssS, stS)
                scalar.dma_start(
                    wfx[:],
                    bass.AP(wxT_h, 0, [[D, 128], [128 * D, NK], [1, 128]]),
                ).then_inc(s_wdx, 16)
                scalar.wait_ge(s_cst, 16 * 6)
                scalar.activation(cng[:], lam[:], AF.Exp, scale=-1.0)
                scalar.activation(lam[:], cng[:], AF.Ln, bias=1.0)
                scalar.activation(cng[:], lam[:], AF.Copy, scale=-8.0)

                def hicast(j):
                    scalar.wait_ge(s_tp, 32 * j + 16)
                    if j >= 2:
                        scalar.wait_ge(s_sp[j % 2], 32 * (j // 2))
                    scalar.activation(HB[j % 2][:], ptA[:],
                                      AF.Copy).then_inc(s_hic, 1)

                def xbev(j):
                    scalar.wait_ge(s_tp, 32 * j + 32)
                    if j >= 3:
                        scalar.wait_ge(s_spx[j % 3], 16 * (j // 3))
                    scalar.activation(XB[j % 3][:], ptB[:],
                                      AF.Copy).then_inc(s_cpX, 1)
                    scalar.wait_ge(s_cpX, j + 1)
                    scalar.dma_start(fm_dst(xfm_h, j),
                                     XB[j % 3][:]).then_inc(s_spx[j % 3], 16)

                for m in range(NM):
                    scalar.wait_ge(s_bn, m + 1)
                    if m >= 1:
                        scalar.wait_ge(s_rc, m)
                    scalar.activation(stdt[:], st2[:, 1:2], AF.Sqrt,
                                      bias=epsb[:]).then_inc(s_sd, 1)
                    if m >= 1:
                        hicast(m - 1)
                        xbev(m - 1)
                    if m == 10:
                        scalar.wait_ge(s_wdx, 16)
                        scalar.activation(wrx_h0[:], wfx[:],
                                          AF.Copy).then_inc(s_wcx, 1)
                hicast(NM - 1)
                xbev(NM - 1)

            @blk.vector
            def _(vector):
                vector = Ser(vector, ssV, stV)
                vector.memset(epsb[:], EPS)

                def losub(j):
                    vector.wait_ge(s_hic, j + 1)
                    vector.tensor_tensor(LB[j % 2][:], ptA[:],
                                         HB[j % 2][:],
                                         ALU.subtract).then_inc(s_lo, 1)

                for m in range(NM):
                    vector.wait_ge(s_x[m % 3], 16 * (m // 3 + 1))
                    if m >= 1:
                        vector.wait_ge(s_sd, m)
                    for q in range(4):
                        vector.bn_stats(st6[:, 6 * q:6 * q + 6],
                                        xr[m % 3][:, ts(q, 512)])
                    vector.bn_aggr(st2[:], st6[:]).then_inc(s_bn, 1)
                    vector.wait_ge(s_sd, m + 1)
                    vector.reciprocal(rstd[:], stdt[:]).then_inc(s_rc, 1)
                    vector.tensor_scalar_mul(nmb[:], st2[:, 0:1], -1.0)
                    if m >= 1:
                        vector.wait_ge(s_tp, 32 * (m - 1) + 16)
                    vector.tensor_scalar(xnb[:], xr[m % 3][:], nmb[:],
                                         rstd[:], ALU.add,
                                         ALU.mult).then_inc(s_xn, 1)
                    if m == 10:
                        vector.wait_ge(s_wda, 16)
                        vector.tensor_copy(wra_h0[:], wfa[:]).then_inc(s_wca, 1)
                    if m == 11:
                        vector.tensor_tensor(wra_l0[:], wfa[:], wra_h0[:],
                                             ALU.subtract).then_inc(s_wsa, 1)
                    if m == 13:
                        vector.wait_ge(s_wcx, 1)
                        vector.tensor_tensor(wrx_l0[:], wfx[:], wrx_h0[:],
                                             ALU.subtract).then_inc(s_wsg, 1)
                    if m >= 1:
                        losub(m - 1)
                losub(NM - 1)

            @blk.tensor
            def _(tensor):
                for m in range(NM):
                    tensor.wait_ge(s_xn, m + 1)
                    if m >= 1:
                        tensor.wait_ge(s_lo, m)
                    for k in range(NK):
                        tensor.transpose(ptA[:, k * 128:k * 128 + 128],
                                         xnb[:, ts(k, 128)],
                                         identf[:]).then_inc(s_tp, 1)
                    if m >= 1:
                        tensor.wait_ge(s_cpX, m)
                    for k in range(NK):
                        tensor.transpose(ptB[:, k * 128:k * 128 + 128],
                                         xr[m % 3][:, ts(k, 128)],
                                         identf[:]).then_inc(s_tp, 1)

            @blk.gpsimd
            def _(gpsimd):
                for j in range(NM):
                    gpsimd.wait_ge(s_hic, j + 1)
                    gpsimd.dma_start(fm_dst(xph_h, j),
                                     HB[j % 2][:]).then_inc(s_sp[j % 2], 16)
                    gpsimd.wait_ge(s_lo, j + 1)
                    gpsimd.dma_start(fm_dst(xpl_h, j),
                                     LB[j % 2][:]).then_inc(s_sp[j % 2], 16)
                    if j >= 8:
                        if j == 8:
                            for q in range(2):
                                gpsimd.wait_ge(s_sp[q], 32 * 4)
                        k = 2 * (j - 8)
                        gpsimd.dma_start(xc2l[:, ts(k, W2)],
                                         xpl[ts(k, 128), ts(0, W2)]
                                         ).then_inc(s_xc2g if k < 8
                                                    else s_xc2g2, 16)
                for j in range(2):
                    gpsimd.wait_ge(s_sp[j], 32 * (NM // 2))
                for j in range(3):
                    gpsimd.wait_ge(s_spx[j], 16 * len(range(j, NM, 3)))

    with ExitStack() as ph:
        sb1 = lambda n, s, dt=F32: ph.enter_context(nc.sbuf_tensor(n, s, dt))
        xc2h = [xc2h0, sb1("xc2h1", [128, NK * W2], BF16)]
        wra_h = [wra_h0, sb1("wrah1", [128, NK * 128], BF16)]
        wra_l = [wra_l0, sb1("wral1", [128, NK * 128], BF16)]
        wrx_h = [wrx_h0, sb1("wrxh1", [128, NK * 128], BF16)]
        wrx_l = [wrx_l0, sb1("wrxl1", [128, NK * 128], BF16)]
        isb = [sb1(f"isb{j}", [128, W2]) for j in range(2)]
        aab = [sb1(f"aab{j}", [128, W2]) for j in range(2)]
        sfb = [sb1(f"sfb{j}", [128, W2]) for j in range(2)]
        uub = sb1("uub", [128, W2])
        car = sb1("car", [128, 16])
        pr2 = [ph.enter_context(nc.psum_tensor(f"pr2{j}", [128, W2], F32))
               for j in range(2)]
        pi2 = [ph.enter_context(nc.psum_tensor(f"pi2{j}", [128, W2], F32))
               for j in range(2)]

        s_rec = sem("s_rec")
        s_mmA = sem("s_mmA")
        s_acA = sem("s_acA")
        s_sfc = sem("s_sfc")
        s_vA = sem("s_vA")
        s_ya = sem("s_ya")

        NIDX2 = NT2 * NH

        with nc.Block("ph2") as blk:
            @blk.sync
            def _(sync):
                for k in range(8, NK, 2):
                    sync.dma_start(xc2h0[:, ts(k, W2)],
                                   xph[ts(k, 128),
                                       ts(0, W2)]).then_inc(s_xc2s[0], 16)
                for idx2 in range(1, NIDX2):
                    h = idx2 % NH
                    if 4 <= idx2 <= 11:
                        for k in (2 * (idx2 - 4), 2 * (idx2 - 4) + 1):
                            sync.dma_start(xc3h0[:, ts(k, W3)],
                                           xph[ts(k, 128),
                                               ts(0, W3)]).then_inc(s_xc30, 16)
                            sync.dma_start(xc3l0[:, ts(k, W3)],
                                           xpl[ts(k, 128),
                                               ts(0, W3)]).then_inc(s_xc30, 16)
                    sync.wait_ge(s_wsa, idx2)
                    sync.dma_start(
                        wfa[:],
                        bass.AP(waT_h, h * 128,
                                [[D, 128], [128 * D, NK], [1, 128]]),
                    ).then_inc(s_wda, 16)
                    if idx2 == NH:
                        sync.wait_ge(s_mmA, NH)
                        for k in (1, 3, 5, 7):
                            sync.dma_start(xc2l[:, ts(k, W2)],
                                           xpl[ts(k, 128),
                                               ts(1, W2)]).then_inc(s_xc2t2, 16)
                    if idx2 == 2:
                        for k in range(0, NK, 2):
                            sync.dma_start(xc2h[1][:, ts(k, W2)],
                                           xph[ts(k, 128),
                                               ts(1, W2)]).then_inc(s_xc2[1], 16)

            @blk.tensor
            def _(tensor):
                for tb in range(NT2):
                    if tb == 0:
                        tensor.wait_ge(s_xc2[0], 192)
                        tensor.wait_ge(s_xc2g, 64)
                    else:
                        tensor.wait_ge(s_xc2[1], 128)
                        tensor.wait_ge(s_xc2s[1], 128)
                        tensor.wait_ge(s_xc2g, 192)
                        tensor.wait_ge(s_xc2g2, 64)
                        tensor.wait_ge(s_xc2gs, 64)
                        tensor.wait_ge(s_xc2t2, 64)
                    for h in range(NH):
                        idx2 = tb * NH + h
                        p2 = idx2 % 2
                        tensor.wait_ge(s_wsa, idx2 + 1)
                        tensor.wait_ge(s_wsg, idx2 + 1)
                        if idx2 >= 2:
                            tensor.wait_ge(s_acA, idx2 - 1)
                        pr, pi = pr2[p2], pi2[p2]
                        mm = None
                        for k in range(NK):
                            if tb == 0 and h == 0 and k == 8:
                                tensor.wait_ge(s_xc2s[0], 64)
                                tensor.wait_ge(s_xc2u, 128)
                                tensor.wait_ge(s_xc2g2, 64)
                            first = (k == 0)
                            last = (k == NK - 1)
                            for wh, wl, po in ((wra_h, wra_l, pr),
                                               (wrx_h, wrx_l, pi)):
                                whk = wh[p2][:, ts(k, 128)]
                                wlk = wl[p2][:, ts(k, 128)]
                                for half in range(2):
                                    rhs_h = xc2h[tb][:, k * W2 + half * 512:
                                                 k * W2 + half * 512 + 512]
                                    rhs_l = xc2l[:, k * W2 + half * 512:
                                                 k * W2 + half * 512 + 512]
                                    dst = po[:, half * 512:half * 512 + 512]
                                    tensor.matmul(dst, whk, rhs_h,
                                                  start=first, stop=False,
                                                  skip_group_check=True)
                                    tensor.matmul(dst, whk, rhs_l,
                                                  start=False, stop=False,
                                                  skip_group_check=True)
                                    mm = tensor.matmul(dst, wlk, rhs_h,
                                                       start=False, stop=last,
                                                       skip_group_check=True)
                        mm.then_inc(s_mmA, 1)

            @blk.scalar
            def _(scalar):
                scalar = Ser(scalar, ssS, stS)
                for k in range(9, NK, 2):
                    scalar.dma_start(xc2h0[:, ts(k, W2)],
                                     xph[ts(k, 128),
                                         ts(0, W2)]).then_inc(s_xc2u, 16)
                    scalar.dma_start(xc2l[:, ts(k, W2)],
                                     xpl[ts(k, 128),
                                         ts(0, W2)]).then_inc(s_xc2u, 16)

                def wsplit(j):
                    pj = j % 2
                    hj = j % NH
                    if j >= 1:
                        scalar.wait_ge(s_wsg, j)
                        scalar.dma_start(
                            wfx[:],
                            bass.AP(wxT_h, hj * 128,
                                    [[D, 128], [128 * D, NK], [1, 128]]),
                        ).then_inc(s_wdx, 16)
                    scalar.wait_ge(s_wda, 16 * (j + 1))
                    if j >= 2:
                        scalar.wait_ge(s_mmA, j - 1)
                    scalar.activation(wra_h[pj][:], wfa[:],
                                      AF.Copy).then_inc(s_wca, 1)
                    scalar.wait_ge(s_wdx, 16 * (j + 1))
                    scalar.activation(wrx_h[pj][:], wfx[:],
                                      AF.Copy).then_inc(s_wcx, 1)

                for idx2 in range(NIDX2):
                    h = idx2 % NH
                    p2 = idx2 % 2
                    if idx2 == 2:
                        for k in range(1, NK, 2):
                            scalar.dma_start(xc2h[1][:, ts(k, W2)],
                                             xph[ts(k, 128),
                                                 ts(1, W2)]).then_inc(s_xc2s[1], 16)
                    if idx2 == NH:
                        scalar.wait_ge(s_mmA, NH)
                        for k in (9, 11, 13, 15):
                            scalar.wait_ge(s_rec, k + 1)
                            scalar.dma_start(xc2l[:, ts(k, W2)],
                                             xpl[ts(k, 128),
                                                 ts(1, W2)]).then_inc(s_xc2gs, 16)
                    if idx2 + 1 < NIDX2:
                        wsplit(idx2 + 1)
                    scalar.wait_ge(s_mmA, idx2 + 1)
                    if idx2 >= 2:
                        scalar.wait_ge(s_vA, idx2 - 1)
                        scalar.wait_ge(s_ya, 16 * (idx2 - 1))
                    scalar.activation(aab[p2][:], pr2[p2][:], AF.Sigmoid,
                                      bias=ba[:, h:h + 1])
                    scalar.activation(isb[p2][:], pi2[p2][:], AF.Sigmoid,
                                      bias=bx[:, h:h + 1]).then_inc(s_acA, 1)
                    scalar.activation(aab[p2][:], aab[p2][:], AF.Exp,
                                      scale=cng[:, h:h + 1])
                    scalar.activation(sfb[p2][:], aab[p2][:], AF.Square)
                    scalar.activation(sfb[p2][:], sfb[p2][:], AF.Sqrt,
                                      scale=-1.0, bias=1.0).then_inc(s_sfc, 1)
                    if idx2 == NIDX2 - 2:
                        scalar.wait_ge(s_wr, 16)
                        scalar.activation(wrh[:], wrf[:],
                                          AF.Copy).then_inc(s_wrc, 1)

            @blk.vector
            def _(vector):
                vector = Ser(vector, ssV, stV)

                def wsub(j):
                    pj = j % 2
                    vector.wait_ge(s_wca, j + 1)
                    if j >= 2:
                        vector.wait_ge(s_mmA, j - 1)
                    vector.tensor_tensor(wra_l[pj][:], wfa[:],
                                         wra_h[pj][:],
                                         ALU.subtract).then_inc(s_wsa, 1)

                for idx2 in range(NIDX2):
                    tb, h = idx2 // NH, idx2 % NH
                    p2 = idx2 % 2
                    if idx2 + 1 < NIDX2:
                        wsub(idx2 + 1)
                    vector.wait_ge(s_sfc, idx2 + 1)
                    vector.tensor_tensor(uub[:], sfb[p2][:], isb[p2][:], ALU.mult)
                    vector.tensor_tensor(sfb[p2][:], xc2h[tb][:, ts(h, W2)],
                                         xc2l[:, ts(h, W2)],
                                         ALU.add).then_inc(s_rec, 1)
                    vector.tensor_tensor(uub[:], uub[:], sfb[p2][:], ALU.mult)
                    if idx2 >= 1:
                        vector.wait_ge(s_ya, 16 * idx2)
                    init = 0.0 if tb == 0 else car[:, h:h + 1]
                    last = vector.tensor_tensor_scan(isb[p2][:], aab[p2][:],
                                                     uub[:], init,
                                                     ALU.mult, ALU.add)
                    if tb + 1 < NT2:
                        last = vector.tensor_copy(car[:, h:h + 1],
                                                  isb[p2][:, W2 - 1:W2])
                    last.then_inc(s_vA, 1)
                    if idx2 == NIDX2 - 2:
                        vector.wait_ge(s_wrc, 1)
                        vector.tensor_tensor(wrl[:], wrf[:], wrh[:],
                                             ALU.subtract).then_inc(s_wrs, 1)

            @blk.gpsimd
            def _(gpsimd):
                def wsubg(j):
                    pj = j % 2
                    gpsimd.wait_ge(s_wcx, j + 1)
                    if j >= 2:
                        gpsimd.wait_ge(s_mmA, j - 1)
                    gpsimd.tensor_tensor(wrx_l[pj][:], wfx[:],
                                         wrx_h[pj][:],
                                         ALU.subtract).then_inc(s_wsg, 1)

                for idx2 in range(NIDX2):
                    tb, h = idx2 // NH, idx2 % NH
                    if idx2 + 1 < NIDX2:
                        wsubg(idx2 + 1)
                    if idx2 == NH - 1:
                        gpsimd.wait_ge(s_mmA, NH)
                        for k in range(0, NK, 2):
                            gpsimd.wait_ge(s_rec, k + 1)
                            gpsimd.dma_start(xc2l[:, ts(k, W2)],
                                             xpl[ts(k, 128),
                                                 ts(1, W2)]).then_inc(s_xc2g, 16)
                    gpsimd.wait_ge(s_vA, idx2 + 1)
                    gpsimd.dma_start(yat[ts(h, 128), ts(tb, W2)],
                                     isb[idx2 % 2][:]).then_inc(s_ya, 16)
                gpsimd.wait_ge(s_ya, 16 * NIDX2)

    s2pre.close()

    with ExitStack() as ph:
        sb1 = lambda n, s, dt=F32: ph.enter_context(nc.sbuf_tensor(n, s, dt))
        LG = sb1("LG", [8, S])
        GR = LG
        LTt = sb1("LTt", [128, 32])
        LSt = sb1("LSt", [128, 32])
        EXt = sb1("EXt", [128, 32])
        MX4 = sb1("MX4", [128, 4])
        SM4 = sb1("SM4", [128, 4])
        RC4 = sb1("RC4", [128, 4])
        TM4 = sb1("TM4", [128, 4])
        PRt = sb1("PRt", [128, 32])
        PMt = sb1("PMt", [128, 32])
        C1t = sb1("C1t", [128, 32])
        C2t = sb1("C2t", [128, 32])
        MKt = sb1("MKt", [128, 32])
        Gt = sb1("Gt", [128, 32])
        aph = sb1("aph", [128, NK * ER], BF16)
        apl = sb1("apl", [128, NK * ER], BF16)
        bth = sb1("bth", [128, NEB * D], BF16)
        btl = sb1("btl", [128, NEB * D], BF16)
        apf = [sb1(f"apf{j}", [128, ER]) for j in range(2)]
        apfS = [sb1(f"apfS{j}", [128, ER]) for j in range(2)]
        btf = [sb1(f"btf{j}", [128, W3]) for j in range(2)]
        btq = [apfS[0], apfS[1], apf[0], apf[1]]
        apq = [apf[0], apf[1], btf[0], btf[1]]
        apqS = None
        xc3h = [xc3h0, sb1("xc3h1", [128, NK * W3], BF16)]
        xc3l = [xc3l0, sb1("xc3l1", [128, NK * W3], BF16)]
        GBC = sb1("GBC", [128, W3])
        GR2 = [sb1(f"GR2{j}", [2, W3]) for j in range(2)]
        HGF = [sb1(f"HGF{j}", [128, W3]) for j in range(2)]
        apqS = [apfS[0], apfS[1], HGF[0], HGF[1]]
        hgh = [sb1(f"hgh{j}", [128, NEB * W3], BF16) for j in range(2)]
        hgl = [sb1(f"hgl{j}", [128, NEB * W3], BF16) for j in range(2)]
        YA = [sb1(f"YA{j}", [128, W3]) for j in range(4)]
        XF3 = [sb1(f"XF3{j}", [128, W3]) for j in range(4)]
        T1 = sb1("T1", [128, W3])
        OT = [sb1(f"OT{j}", [128, W3]) for j in range(4)]
        pH = [ph.enter_context(nc.psum_tensor(f"pH{j}", [128, 512], F32))
              for j in range(4)]
        pY = [ph.enter_context(nc.psum_tensor(f"pY{j}", [128, 512], F32))
              for j in range(4)]

        s_mR = sem("s_mR")
        s_lg = sem("s_lg")
        s_lgT = sem("s_lgT")
        s_ltv = sem("s_ltv")
        s_ls = sem("s_ls")
        s_sex = sem("s_sex")
        s_g3 = sem("s_g3")
        s_gT = sem("s_gT")
        s_gr = sem("s_gr")
        s_apd = [sem(f"s_apd{j}") for j in range(4)]
        s_aps = sem("s_aps")
        s_apdS = [sem(f"s_apdS{j}") for j in range(4)]
        s_apcS = sem("s_apcS")
        s_apsS = sem("s_apsS")
        s_btd = [sem(f"s_btd{j}") for j in range(2)]
        s_bts = sem("s_bts")
        s_apcE = sem("s_apcE")
        s_btcE = sem("s_btcE")
        s_btcS = sem("s_btcS")
        s_btsO = sem("s_btsO")
        s_btdS = [sem(f"s_btdS{j}") for j in range(4)]
        s_xc3 = [sem(f"s_xc3{j}") for j in range(2)]
        s_xc3s = [sem(f"s_xc3s{j}") for j in range(2)]
        s_gr2 = [sem(f"s_gr2{j}") for j in range(2)]
        s_gbm = sem("s_gbm")
        s_gbv = sem("s_gbv")
        s_mH = sem("s_mH")
        s_hgf = sem("s_hgf")
        s_hgc = sem("s_hgc")
        s_hgs = sem("s_hgs")
        s_mY = sem("s_mY")
        s_cio = [sem(f"s_cio{j}") for j in range(4)]
        s_cmp = sem("s_cmp")
        s_cmb = sem("s_cmb")
        s_out = [sem(f"s_out{j}") for j in range(4)]

        def g4(t, off, d3=8):
            return bass.AP(t, off, [[32, 128], [8, 4], [1, d3]])

        def g4b(t, off):
            return bass.AP(t, off, [[4, 128], [1, 4], [0, 8]])

        def g4s(t, off):
            return bass.AP(t, off, [[32, 128], [8, 4]])

        def hmm_group(tensor, tb, eb, apwait=False):
            idx = tb * NEB + eb
            if idx >= 4:
                tensor.wait_ge(s_hgf, idx - 3)
            mm = None
            for k in range(NK):
                if apwait:
                    if k % 2 == 0:
                        tensor.wait_ge(s_aps, k // 2 + 1)
                    else:
                        tensor.wait_ge(s_apsS, k // 2 + 1)
                first, last = (k == 0), (k == NK - 1)
                ah = aph[:, k * ER + eb * 128: k * ER + eb * 128 + 128]
                al = apl[:, k * ER + eb * 128: k * ER + eb * 128 + 128]
                rh = xc3h[tb % 2][:, ts(k, W3)]
                rl = xc3l[tb % 2][:, ts(k, W3)]
                tensor.matmul(pH[eb][:], ah, rh, start=first,
                              stop=False, skip_group_check=True)
                tensor.matmul(pH[eb][:], ah, rl, start=False,
                              stop=False, skip_group_check=True)
                mm = tensor.matmul(pH[eb][:], al, rh, start=False,
                                   stop=last, skip_group_check=True)
            mm.then_inc(s_mH, 1)

        def gbc_mm(tensor, tb, eb):
            idx = tb * NEB + eb
            tensor.wait_ge(s_gr2[idx % 2], 16 * (idx // 2 + 1))
            if eb == 0:
                tensor.wait_ge(s_gr, tb + 1)
            tensor.matmul(pY[tb % 4][:], sel2[:], GR2[idx % 2][:],
                          start=True, stop=True,
                          skip_group_check=True).then_inc(s_gbm, 1)
            tensor.wait_ge(s_gbv, idx + 1)

        with nc.Block("ph3") as blk:
            @blk.sync
            def _(sync):
                def xc3_loads(tb):
                    if tb >= 2:
                        sync.wait_ge(s_mH, NEB * (tb - 1))
                    for k in range(NK):
                        sync.dma_start(xc3h[tb % 2][:, ts(k, W3)],
                                       xph[ts(k, 128),
                                           ts(tb, W3)]).then_inc(s_xc3[tb % 2], 16)
                        if k % 2 == 0:
                            sync.dma_start(xc3l[tb % 2][:, ts(k, W3)],
                                           xpl[ts(k, 128),
                                               ts(tb, W3)]).then_inc(s_xc3[tb % 2], 16)

                for tb in range(NT3):
                    if tb + 1 < NT3:
                        xc3_loads(tb + 1)
                    sync.wait_ge(s_gr, tb + 1)
                    for eb in range(NEB):
                        idx = tb * NEB + eb
                        if idx >= 2:
                            sync.wait_ge(s_gbm, idx - 1)
                        sync.dma_start(GR2[idx % 2][:],
                                       GR[2 * eb:2 * eb + 2, ts(tb, W3)]
                                       ).then_inc(s_gr2[idx % 2], 16)
                    for o in range(NH):
                        idx4 = tb * NH + o
                        if idx4 >= 4:
                            sync.wait_ge(s_cmb, idx4 - 3)
                        sync.dma_start(YA[idx4 % 4][:],
                                       yat[ts(o, 128),
                                           ts(tb, W3)]).then_inc(s_cio[idx4 % 4], 16)
                        sync.dma_start(XF3[idx4 % 4][:],
                                       xfm[ts(o, 128),
                                           ts(tb, W3)]).then_inc(s_cio[idx4 % 4], 16)

            @blk.tensor
            def _(tensor):
                tensor.wait_ge(s_wrs, 1)
                for tb in range(NT3):
                    if tb == 0:
                        tensor.wait_ge(s_xc30, 512)
                    else:
                        tensor.wait_ge(s_xc3[tb % 2], 16 * 24 * ((tb + 1) // 2))
                        tensor.wait_ge(s_xc3s[tb % 2], 16 * 8 * ((tb + 1) // 2))
                    BR = pY[tb % 4]
                    if tb >= 1:
                        tensor.wait_ge(s_cmp, 16 * tb)
                    for k in range(NK):
                        first, last = (k == 0), (k == NK - 1)
                        wh = wrh[:, ts(k, 8)]
                        wl = wrl[:, ts(k, 8)]
                        tensor.matmul(BR[:8, :], wh, xc3h[tb % 2][:, ts(k, W3)],
                                      start=first, stop=False, skip_group_check=True)
                        tensor.matmul(BR[:8, :], wh, xc3l[tb % 2][:, ts(k, W3)],
                                      start=False, stop=False, skip_group_check=True)
                        tensor.matmul(BR[:8, :], wl, xc3h[tb % 2][:, ts(k, W3)],
                                      start=False, stop=last,
                                      skip_group_check=True).then_inc(s_mR, 1)
                    if tb == 0:
                        hmm_group(tensor, 0, 0, apwait=True)
                    tensor.wait_ge(s_lg, tb + 1)
                    tr = None
                    for j in range(4):
                        tr = tensor.transpose(BR[:, j * 8:j * 8 + 8],
                                              LG[:8, ts(4 * tb + j, 128)],
                                              identf[:8, :8])
                    tr.then_inc(s_lgT, 1)
                    if tb != 0:
                        hmm_group(tensor, tb, 0)
                    else:
                        hmm_group(tensor, 0, 1)
                    tensor.wait_ge(s_g3, tb + 1)
                    tensor.wait_ge(s_ltv, tb + 1)
                    tr = None
                    for j in range(4):
                        tr = tensor.transpose(BR[:8, ts(j, 128)],
                                              Gt[:, ts(j, 8)], identf[:])
                    tr.then_inc(s_gT, 1)
                    hmm_group(tensor, tb, 1 if tb != 0 else 2)
                    gbc_mm(tensor, tb, 0)
                    gbc_mm(tensor, tb, 1)
                    if tb != 0:
                        hmm_group(tensor, tb, 2)
                    gbc_mm(tensor, tb, 2)
                    gbc_mm(tensor, tb, 3)
                    hmm_group(tensor, tb, 3)
                    tensor.wait_ge(s_hgs, NEB * (tb + 1))
                    tensor.wait_ge(s_hgc, NEB * (tb + 1))
                    if tb == 0:
                        tensor.wait_ge(s_bts, 8)
                        tensor.wait_ge(s_btsO, 8)

                    for o in range(NH):
                        idx4 = tb * NH + o
                        if idx4 >= 4:
                            tensor.wait_ge(s_cmp, idx4 - 3)
                        po = pY[idx4 % 4]
                        mm = None
                        for kb in range(NEB):
                            first, last = (kb == 0), (kb == NEB - 1)
                            bh = bth[:, kb * D + o * 128: kb * D + o * 128 + 128]
                            bl = btl[:, kb * D + o * 128: kb * D + o * 128 + 128]
                            rh = hgh[tb % 2][:, ts(kb, W3)]
                            rl = hgl[tb % 2][:, ts(kb, W3)]
                            tensor.matmul(po[:], bh, rh, start=first,
                                          stop=False, skip_group_check=True)
                            tensor.matmul(po[:], bh, rl, start=False,
                                          stop=False, skip_group_check=True)
                            mm = tensor.matmul(po[:], bl, rh, start=False,
                                               stop=last, skip_group_check=True)
                        mm.then_inc(s_mY, 1)

            @blk.scalar
            def _(scalar):
                scalar = Ser(scalar, ssS, stS)

                def xc3s_loads(tb):
                    if tb >= 2:
                        scalar.wait_ge(s_mH, NEB * (tb - 1))
                    for k in range(1, NK, 2):
                        scalar.dma_start(xc3l[tb % 2][:, ts(k, W3)],
                                         xpl[ts(k, 128),
                                             ts(tb, W3)]).then_inc(s_xc3s[tb % 2], 16)

                for i0 in range(4):
                    scalar.dma_start(apqS[i0][:],
                                     apk[ts(2 * i0 + 1, 128), :]
                                     ).then_inc(s_apdS[i0], 16)
                for i, k in enumerate(range(1, NK, 2)):
                    scalar.wait_ge(s_apdS[i % 4], 16 * (i // 4 + 1))
                    scalar.activation(aph[:, ts(k, ER)], apqS[i % 4][:],
                                      AF.Copy).then_inc(s_apcS, 1)
                    if i + 4 < NK // 2:
                        scalar.wait_ge(s_apsS, i + 1)
                        scalar.dma_start(apqS[i % 4][:],
                                         apk[ts(2 * i + 9, 128), :]
                                         ).then_inc(s_apdS[i % 4], 16)
                    scalar.wait_ge(s_apd[i % 4], 16 * (i // 4 + 1))
                    scalar.activation(aph[:, ts(2 * i, ER)], apq[i % 4][:],
                                      AF.Copy).then_inc(s_apcE, 1)

                def bt_odd_dma(i):
                    jo = 2 * i + 1
                    if i >= 4:
                        scalar.wait_ge(s_btsO, i - 3)
                    scalar.dma_start(btq[i % 4][:],
                                     btp[ts(jo // 4, 128),
                                         ts(jo % 4, W3)]).then_inc(s_btdS[i % 4], 16)

                def bt_casts(i):
                    je, jo = 2 * i, 2 * i + 1
                    sle = slice((je // 4) * D + (je % 4) * W3,
                                (je // 4) * D + (je % 4) * W3 + W3)
                    slo = slice((jo // 4) * D + (jo % 4) * W3,
                                (jo // 4) * D + (jo % 4) * W3 + W3)
                    scalar.wait_ge(s_btd[i % 2], 16 * (i // 2 + 1))
                    scalar.activation(bth[:, sle], btf[i % 2][:],
                                      AF.Copy).then_inc(s_btcE, 1)
                    scalar.wait_ge(s_btdS[i % 4], 16 * (i // 4 + 1))
                    scalar.activation(bth[:, slo], btq[i % 4][:],
                                      AF.Copy).then_inc(s_btcS, 1)

                scalar.wait_ge(s_apsS, NK // 2)
                scalar.wait_ge(s_aps, NK // 2)
                for i in range(4):
                    bt_odd_dma(i)
                for tb in range(NT3):
                    scalar.wait_ge(s_mR, 16 * (tb + 1))
                    scalar.add(LG[:8, ts(tb, W3)], pY[tb % 4][:8, :],
                               brs[:]).then_inc(s_lg, 1)
                    if tb + 1 < NT3:
                        xc3s_loads(tb + 1)
                    scalar.wait_ge(s_ls, tb + 1)
                    scalar.activation(EXt[:], LSt[:], AF.Exp).then_inc(s_sex, 1)
                    if tb == 0:
                        for i in range(8):
                            if 2 <= i <= 5:
                                bt_odd_dma(i + 2)
                            bt_casts(i)
                    for eb in range(NEB):
                        idx = tb * NEB + eb
                        scalar.wait_ge(s_hgf, idx + 1)
                        if tb >= 2 and eb == 0:
                            scalar.wait_ge(s_mY, NH * (tb - 1))
                        scalar.activation(hgh[tb % 2][:, ts(eb, W3)],
                                          HGF[idx % 2][:],
                                          AF.Copy).then_inc(s_hgc, 1)

            @blk.vector
            def _(vector):
                vector = Ser(vector, ssV, stV)
                for i, k in enumerate(range(1, NK, 2)):
                    vector.wait_ge(s_apcS, i + 1)
                    vector.tensor_tensor(apl[:, ts(k, ER)], apqS[i % 4][:],
                                         aph[:, ts(k, ER)],
                                         ALU.subtract).then_inc(s_apsS, 1)
                for tb in range(NT3):
                    vector.wait_ge(s_lgT, tb + 1)
                    if tb >= 1:
                        vector.wait_ge(s_sex, tb)
                    vector.tensor_copy(LTt[:], pY[tb % 4][:, :32]).then_inc(s_ltv, 1)
                    vector.reduce_max(MX4[:], g4(LTt, 0), axis=AX)
                    vector.tensor_tensor(g4(LSt, 0), g4(LTt, 0), g4b(MX4, 0),
                                         ALU.subtract).then_inc(s_ls, 1)
                    vector.wait_ge(s_sex, tb + 1)
                    vector.reduce_sum(SM4[:], g4(EXt, 0), axis=AX)
                    vector.reciprocal(RC4[:], SM4[:])
                    vector.tensor_tensor(g4(PRt, 0), g4(EXt, 0), g4b(RC4, 0),
                                         ALU.mult)
                    vector.memset(PMt[:], -1.0)
                    for e in range(7):
                        vector.tensor_max(g4s(PMt, e + 1), g4s(PMt, e),
                                          g4s(PRt, e))
                    vector.tensor_max(TM4[:], g4s(PMt, 7), g4s(PRt, 7))
                    vector.tensor_tensor(C1t[:], PRt[:], PMt[:], ALU.is_gt)
                    vector.tensor_tensor(g4(C2t, 0), g4(PRt, 0), g4b(TM4, 0),
                                         ALU.is_ge)
                    vector.tensor_tensor(MKt[:], C1t[:], C2t[:], ALU.mult)
                    vector.scalar_tensor_tensor(Gt[:], PRt[:], 1.0 / 64.0,
                                                MKt[:], ALU.mult,
                                                ALU.mult).then_inc(s_g3, 1)
                    vector.wait_ge(s_gT, tb + 1)
                    vector.tensor_copy(GR[:8, ts(tb, W3)],
                                       pY[tb % 4][:8, :512]).then_inc(s_gr, 1)
                    for eb in range(NEB):
                        idx = tb * NEB + eb
                        vector.wait_ge(s_gbm, idx + 1)
                        vector.tensor_copy(GBC[:], pY[tb % 4][:]).then_inc(s_gbv, 1)
                        vector.wait_ge(s_mH, idx + 1)
                        if idx == 0:
                            vector.wait_ge(s_apcS, NK // 2)
                        if idx >= 2:
                            vector.wait_ge(s_hgc, idx - 1)
                        if tb >= 2 and eb == 0:
                            vector.wait_ge(s_mY, NH * (tb - 1))
                        vector.tensor_tensor(HGF[idx % 2][:], pH[eb][:], GBC[:],
                                             ALU.mult).then_inc(s_hgf, 1)
                        vector.wait_ge(s_hgc, idx + 1)
                        vector.tensor_tensor(hgl[tb % 2][:, ts(eb, W3)],
                                             HGF[idx % 2][:],
                                             hgh[tb % 2][:, ts(eb, W3)],
                                             ALU.subtract).then_inc(s_hgs, 1)
                    for o in range(NH):
                        idx4 = tb * NH + o
                        vector.wait_ge(s_mY, idx4 + 1)
                        vector.wait_ge(s_cio[idx4 % 4], 32 * (idx4 // 4 + 1))
                        vector.tensor_tensor(T1[:], pY[idx4 % 4][:],
                                             YA[idx4 % 4][:],
                                             ALU.add).then_inc(s_cmp, 1)
                        if idx4 >= 4:
                            vector.wait_ge(s_out[idx4 % 4], 16 * (idx4 // 4))
                        vector.tensor_tensor(OT[idx4 % 4][:], T1[:],
                                             XF3[idx4 % 4][:],
                                             ALU.add).then_inc(s_cmb, 1)

            @blk.gpsimd
            def _(gpsimd):
                for i0 in range(4):
                    gpsimd.dma_start(apq[i0][:],
                                     apk[ts(2 * i0, 128), :]
                                     ).then_inc(s_apd[i0], 16)
                for i, k in enumerate(range(0, NK, 2)):
                    gpsimd.wait_ge(s_apcE, i + 1)
                    gpsimd.tensor_tensor(apl[:, ts(k, ER)], apq[i % 4][:],
                                         aph[:, ts(k, ER)],
                                         ALU.subtract).then_inc(s_aps, 1)
                    if i + 4 < NK // 2:
                        gpsimd.wait_ge(s_aps, i + 1)
                        gpsimd.dma_start(apq[i % 4][:],
                                         apk[ts(2 * i + 8, 128), :]
                                         ).then_inc(s_apd[i % 4], 16)
                gpsimd.wait_ge(s_aps, NK // 2)
                gpsimd.dma_start(btf[0][:],
                                 btp[ts(0, 128), ts(0, W3)]).then_inc(s_btd[0], 16)
                for i in range(8):
                    je, jo = 2 * i, 2 * i + 1
                    sle = slice((je // 4) * D + (je % 4) * W3,
                                (je // 4) * D + (je % 4) * W3 + W3)
                    slo = slice((jo // 4) * D + (jo % 4) * W3,
                                (jo // 4) * D + (jo % 4) * W3 + W3)
                    if i + 1 < 8:
                        if i + 1 >= 2:
                            gpsimd.wait_ge(s_bts, i)
                        je2 = 2 * (i + 1)
                        gpsimd.dma_start(btf[(i + 1) % 2][:],
                                         btp[ts(je2 // 4, 128),
                                             ts(je2 % 4, W3)]
                                         ).then_inc(s_btd[(i + 1) % 2], 16)
                    gpsimd.wait_ge(s_btcE, i + 1)
                    gpsimd.tensor_tensor(btl[:, sle], btf[i % 2][:],
                                         bth[:, sle],
                                         ALU.subtract).then_inc(s_bts, 1)
                    gpsimd.wait_ge(s_btcS, i + 1)
                    gpsimd.tensor_tensor(btl[:, slo], btq[i % 4][:],
                                         bth[:, slo],
                                         ALU.subtract).then_inc(s_btsO, 1)
                for tb in range(NT3):
                    for o in range(NH):
                        idx4 = tb * NH + o
                        gpsimd.wait_ge(s_cmb, idx4 + 1)
                        gpsimd.dma_start(out[ts(o, 128), ts(tb, W3)],
                                         OT[idx4 % 4][:]).then_inc(s_out[idx4 % 4], 16)
                for j in range(4):
                    gpsimd.wait_ge(s_out[j], 16 * (NT3 * NH // 4))

    s3pre.close()
    top.close()
    return nc


def _prep_shared(inputs):
    f32 = np.float32
    Wa = np.asarray(inputs["L0_attn_Wa"], f32)
    Wx = np.asarray(inputs["L0_attn_Wx"], f32)
    A = np.asarray(inputs["L0_ffn_A"], f32)
    B = np.asarray(inputs["L0_ffn_B"], f32)
    Wr = np.asarray(inputs["L0_ffn_Wr"], f32)
    ba = np.asarray(inputs["L0_attn_ba"], f32)
    bx = np.asarray(inputs["L0_attn_bx"], f32)
    lam = np.asarray(inputs["L0_attn_Lam"], f32)
    br = np.asarray(inputs["L0_ffn_br"], f32)
    shared = {
        "waT": np.ascontiguousarray(Wa.T),
        "wxT": np.ascontiguousarray(Wx.T),
        "apk": np.ascontiguousarray(A.transpose(2, 0, 1).reshape(D, ER)),
        "btp": np.ascontiguousarray(B.transpose(0, 2, 1).reshape(ER, D)),
        "wrT": np.ascontiguousarray(Wr.T),
        "ba_t": np.ascontiguousarray(ba.reshape(16, 128).T),
        "bx_t": np.ascontiguousarray(bx.reshape(16, 128).T),
        "lam_t": np.ascontiguousarray(lam.reshape(16, 128).T),
        "br": np.ascontiguousarray(br.reshape(8, 1)),
        "ident": np.eye(128, dtype=f32),
        "ones1": np.ones((1, 128), dtype=f32),
        "sel2": np.repeat(np.eye(2, dtype=f32), 64, axis=1),
    }
    return shared


def kernel(**inputs):
    x = np.asarray(inputs["L0_attn_x"], np.float32)
    shared = _prep_shared(inputs)
    nc = build_nc(debug=False)
    in_maps = [dict(shared, x=np.ascontiguousarray(x[c])) for c in range(8)]
    res = run_bass_kernel_spmd(nc, in_maps, core_ids=list(range(8)))
    outp = np.empty((8, S, D), np.float32)
    for c, r in enumerate(res.results):
        outp[c] = np.asarray(r["out"]).T
    return outp
